# version 04
# baseline (speedup 1.0000x reference)
import numpy as np
import ml_dtypes
import concourse.bass as bass
import concourse.mybir as mybir
from concourse.bass_utils import run_bass_kernel_spmd

F32 = mybir.dt.float32
BF16 = mybir.dt.bfloat16
AF = mybir.ActivationFunctionType
ALU = mybir.AluOpType
AX = mybir.AxisListType
bf16 = ml_dtypes.bfloat16

DM = 4096
NB = 4
L = 4096
DH = 2048
LC = 512
NLC = L // LC
NDT = DH // 128
NDF = DM // 128

BYP = ALU.bypass


class SEng:

    def __init__(self, sem, cnt):
        self.s = sem
        self.c = cnt
        self.q = []
        self.cell = {}

    def op(self, name, *a, **k):
        n0 = self.c["n"]
        s_ = self.s

        def run(e, n0=n0, name=name, a=a, k=k):
            if n0 > 0:
                e.wait_ge(s_, n0)
            getattr(e, name)(*a, **k).then_inc(s_, 1)
        self.q.append(run)
        self.c["n"] += 1
        return self.c["n"]

    def mm(self, out, lhsT, rhs, start, stop, **k):
        cell = self.cell

        def run(e, out=out, lhsT=lhsT, rhs=rhs, start=start, stop=stop, k=k):
            cell["last"] = e.matmul(out, lhsT, rhs, start=start, stop=stop, **k)
        self.q.append(run)

    def raw(self, name, *a, **k):
        cell = self.cell

        def run(e, name=name, a=a, k=k):
            cell["last"] = getattr(e, name)(*a, **k)
        self.q.append(run)

    def mark(self):
        cell = self.cell
        s_ = self.s

        def run(e):
            cell["last"].then_inc(s_, 1)
        self.q.append(run)
        self.c["n"] += 1
        return self.c["n"]

    def wait(self, sem, v):
        if v > 0:
            self.q.append(lambda e, sem=sem, v=v: e.wait_ge(sem, v))

    def dma(self, out, in_, sem, cnt, **k):
        def run(e, out=out, in_=in_, sem=sem, k=k):
            e.dma_start(out=out, in_=in_, **k).then_inc(sem, 16)
        self.q.append(run)
        cnt["n"] += 16
        return cnt["n"]

    def tok(self):
        return self.c["n"]


class Prog:
    ENGS = ("vector", "scalar", "tensor", "gpsimd", "sync")

    def __init__(self, nc):
        self.nc = nc
        self.sem = {e: nc.alloc_semaphore(f"es_{e}") for e in self.ENGS}
        self.cnt = {e: {"n": 0} for e in self.ENGS}
        self.dsem = {}
        self.dcnt = {}
        self.eng = {e: SEng(self.sem[e], self.cnt[e]) for e in self.ENGS}

    def dma_sem(self, name):
        if name not in self.dsem:
            self.dsem[name] = self.nc.alloc_semaphore(f"ds_{name}")
            self.dcnt[name] = {"n": 0}
        return self.dsem[name], self.dcnt[name]

    def finalize(self, blk):
        for name in self.ENGS:
            se = self.eng[name]

            def deco(raw, se=se):
                for fn in se.q:
                    fn(raw)
            getattr(blk, name)(deco)


def build_launch_A():
    nc = bass.Bass("TRN2", target_bir_lowering=False)

    x_fm = nc.dram_tensor("x_fm", [DM, L], F32, kind="ExternalInput")
    w_dtin = nc.dram_tensor("w_dtin", [DM, 128], F32, kind="ExternalInput")
    w_bcr = nc.dram_tensor("w_bcr", [DM, 68], F32, kind="ExternalInput")
    w_mid = nc.dram_tensor("w_mid", [DM, 64], F32, kind="ExternalInput")
    w_dt = nc.dram_tensor("w_dt", [128, DH], F32, kind="ExternalInput")
    dt_bias = nc.dram_tensor("dt_bias", [128, NDT], F32, kind="ExternalInput")
    A_log_t = nc.dram_tensor("A_log_t", [128, NDT * 16], F32, kind="ExternalInput")
    D_t = nc.dram_tensor("D_t", [128, NDT], F32, kind="ExternalInput")
    w_B0 = nc.dram_tensor("w_B0", [64, DH], F32, kind="ExternalInput")
    sel0 = nc.dram_tensor("sel0", [4, 64], F32, kind="ExternalInput")
    ident = nc.dram_tensor("ident", [128, 128], F32, kind="ExternalInput")
    x1_out = nc.dram_tensor("x1_out", [DH, L], F32, kind="ExternalOutput")

    xn_sc = nc.dram_tensor("xn_sc", [DH, L], BF16, kind="Internal")
    Bb_sc = nc.dram_tensor("Bb_sc", [16, L], BF16, kind="Internal")
    Cb_sc = nc.dram_tensor("Cb_sc", [16, L], BF16, kind="Internal")

    sb = nc.alloc_sbuf_tensor
    AR1 = sb("AR1", [128, 16384], F32)
    AR2 = sb("AR2", [128, 8192], F32)
    AR3 = sb("AR3", [128, 8192], F32)

    wtmp = sb("wtmp", [128, 4096], F32)
    wdtin_b = sb("wdtin_b", [128, NDF * 128], BF16)
    wbcr_b = sb("wbcr_b", [128, NDF * 68], BF16)
    wmid_b = sb("wmid_b", [128, NDF * 64], BF16)
    wB0_b = sb("wB0_b", [64, DH], BF16)
    wdt_b = wmid_b
    sel0_b = sb("sel0_b", [4, 64], BF16)
    identb = sb("identb", [128, 128], BF16)
    dtb_t = sb("dtb_t", [128, NDT], F32)
    Alog_s = sb("Alog_s", [128, NDT * 16], F32)
    negA = sb("negA", [128, NDT * 16], F32)
    Dcol = sb("Dcol", [128, NDT], F32)

    onesc = sb("onesc", [128, 1], BF16)
    onesf = sb("onesf", [128, 1], F32)
    onesr = sb("onesr", [1, 128], BF16)
    ones1 = sb("ones1", [128, 1], F32)
    epsc = sb("epsc", [1, 1], F32)
    rows_a = sb("rows_a", [128, LC], F32)
    rows_b2 = sb("rows_b2", [128, LC], F32)
    rows_c = sb("rows_c", [1, LC], F32)
    rows_d = sb("rows_d", [1, LC], F32)
    srow = rows_a[0:1, :]
    sqrow = rows_b2[0:1, :]
    mrow = rows_c[0:1, :]
    qrow = rows_d[0:1, :]
    m2row = rows_a[0:1, :]
    vrow = rows_b2[0:1, :]
    sdrow = rows_d[0:1, :]
    rrow = rows_a[0:1, :]
    mrow_b = sb("mrow_b", [1, LC], BF16)
    rrow_b = sb("rrow_b", [1, LC], BF16)
    m_bc = sb("m_bc", [128, LC], F32)
    r_bc = sb("r_bc", [128, LC], F32)

    dtin_f = sb("dtin_f", [128, L], BF16)
    Bfb = wtmp[0:16, 0:2048].bitcast(BF16)
    Cfb = wtmp[32:48, 0:2048].bitcast(BF16)
    mid0b = wtmp[64:128, 0:2048].bitcast(BF16)
    lgf = wtmp[0:4, 2048:4096].bitcast(BF16)
    lgT = sb("lgT", [128, 4], F32)
    g_m1 = sb("g_m1", [128, 1], F32)
    g_nm1 = sb("g_nm1", [128, 1], F32)
    g_t = sb("g_t", [128, 4], F32)
    g_m2 = sb("g_m2", [128, 1], F32)
    g_e = sb("g_e", [128, 4], F32)
    g_ind = sb("g_ind", [128, 4], F32)
    g_raw = sb("g_raw", [128, 4], F32)
    g_s = sb("g_s", [128, 1], F32)
    g_sr = sb("g_sr", [128, 1], F32)
    g_tok = sb("g_tok", [128, 4], BF16)
    mid0p = sb("mid0p", [64, L], BF16)

    abs_t = sb("abs_t", [128, LC], F32)
    exp_t = rows_a
    relu_t = rows_b2
    xn_t = abs_t
    r8_t = wdtin_b
    r4_t = wbcr_b[:, 0:4 * LC]
    ysc_t = m_bc
    ym_t = r_bc
    xr_t = [sb("xr_t0", [128, LC], F32)]
    xnA = [sb(f"xnA{i}", [128, LC], BF16) for i in range(2)]
    xnB = [sb(f"xnB{i}", [128, LC], BF16) for i in range(2)]
    state = sb("state", [128, NDT * 16], F32)

    ps = nc.alloc_psum_tensor
    p_stats = ps("p_stats", [128, LC], F32)
    p_s1 = p_stats[0:1, :]
    p_s2 = p_stats[32:33, :]
    p_bc = ps("p_bc", [128, LC], F32)
    p_dtin = ps("p_dtin", [128, LC], F32)
    p_bcrF = ps("p_bcrF", [128, LC], F32)
    p_bcr = p_bcrF[0:68, :]
    p_midF = ps("p_midF", [128, LC], F32)
    p_mid = p_midF[0:64, :]
    p_dtr = [p_bcrF, p_midF]
    p_d0 = p_dtin
    p_g = p_bc[:, 0:2].bitcast(BF16)
    p_gb = p_bc[0:4, 64:128].bitcast(BF16)

    g_fm = AR1[0:4, 0:2048].bitcast(BF16)
    gbc = AR1[64:128, 0:2048].bitcast(BF16)
    dA_v = AR1[:, 0:8192]
    u_v = AR1[:, 8192:12288].bitcast(BF16)
    h_v = AR1[:, 12288:16384].bitcast(BF16)
    hc_v = u_v
    xall = AR1
    xnb_v = AR2[:].bitcast(BF16)
    dtall_v = AR2[:, 0:4096].bitcast(BF16)
    dtx_v = AR2[:, 4096:8192].bitcast(BF16)
    xsq_v = AR3[:].bitcast(BF16)
    Brep_v = AR3[:, 0:4096].bitcast(BF16)
    Crep_v = AR3[:, 4096:8192].bitcast(BF16)

    P = Prog(nc)
    V, S, T, G, Y = (P.eng["vector"], P.eng["scalar"], P.eng["tensor"],
                     P.eng["gpsimd"], P.eng["sync"])
    if True:
        ld_s, ld_c = P.dma_sem("ld")
        wl_s, wl_c = P.dma_sem("wl")
        wlg_s, wlg_c = P.dma_sem("wlg")
        st_s, st_c = P.dma_sem("st")
        br_s, br_c = P.dma_sem("br")
        xn_s, xn_c = P.dma_sem("xn")

        Y.dma(rows_a[:, 0:128], ident[:], wl_s, wl_c)
        Y.dma(dtb_t[:], dt_bias[:], wl_s, wl_c)
        Y.dma(Alog_s[:], A_log_t[:], wl_s, wl_c)
        Y.dma(Dcol[:], D_t[:], wl_s, wl_c)
        Y.dma(m_bc[0:4, 0:64], sel0[:], wl_s, wl_c)
        G.dma(wtmp[:, 0:4096].rearrange("p (c k) -> p c k", k=128),
              w_dtin[:].rearrange("(c p) k -> p c k", p=128), wlg_s, wlg_c)
        V.wait(wl_s, wl_c["n"])
        V.wait(wlg_s, wlg_c["n"])
        V.op("memset", onesc[:], 1.0)
        V.op("memset", onesf[:], 1.0)
        V.op("memset", onesr[:], 1.0)
        V.op("memset", ones1[:], 1.0)
        V.op("memset", epsc[:], 1e-5)
        V.op("tensor_copy", out=sel0_b[:], in_=m_bc[0:4, 0:64])
        V.op("tensor_copy", out=identb[:], in_=rows_a[:, 0:128])
        V.op("tensor_copy", out=wdtin_b[:], in_=wtmp[:, 0:4096])
        G.wait(P.sem["vector"], V.tok())
        G.dma(wtmp[:, 0:NDF * 68].rearrange("p (c k) -> p c k", k=68),
              w_bcr[:].rearrange("(c p) k -> p c k", p=128), wlg_s, wlg_c)
        V.wait(wlg_s, wlg_c["n"])
        V.op("tensor_copy", out=wbcr_b[:], in_=wtmp[:, 0:NDF * 68])
        G.wait(P.sem["vector"], V.tok())
        G.dma(wtmp[:, 0:NDF * 64].rearrange("p (c k) -> p c k", k=64),
              w_mid[:].rearrange("(c p) k -> p c k", p=128), wlg_s, wlg_c)
        V.wait(wlg_s, wlg_c["n"])
        V.op("tensor_copy", out=wmid_b[:], in_=wtmp[:, 0:NDF * 64])
        S.wait(wl_s, wl_c["n"])
        S.op("activation", out=negA[:], in_=Alog_s[:], func=AF.Exp)
        V.wait(P.sem["scalar"], S.tok())
        V.op("tensor_scalar", out=negA[:], in0=negA[:], scalar1=-1.0, scalar2=0.0,
             op0=ALU.mult, op1=BYP)

        t_prep = V.tok()
        T.wait(P.sem["vector"], t_prep)
        S.wait(P.sem["vector"], t_prep)
        ev = {"dtin": 0, "bcr": 0, "mid": 0, "srow": 0, "rbc": 0}
        for lc in range(NLC):
            cs = slice(lc * LC, (lc + 1) * LC)
            for j in range(NDF):
                if lc > 0 and j == 0:
                    Y.wait(P.sem["tensor"], T.tok())
                    Y.wait(P.sem["scalar"], S.tok())
                Y.dma(xall[:, j * LC:(j + 1) * LC],
                      x_fm[j * 128:(j + 1) * 128, cs], ld_s, ld_c)
            ld_now = ld_c["n"]
            T.wait(P.sem["scalar"], ev["srow"])
            for j in range(NDF):
                T.wait(ld_s, ld_now - 16 * (NDF - 1 - j))
                T.mm(p_s1, onesf[:], xall[:, j * LC:(j + 1) * LC],
                           start=(j == 0), stop=(j == NDF - 1))
            T.mark()
            t_s1 = T.tok()
            S.wait(ld_s, ld_now)
            for j in range(NDF):
                S.op("activation", out=xsq_v[:, j * LC:(j + 1) * LC],
                     in_=xall[:, j * LC:(j + 1) * LC], func=AF.Square)
            t_sq = S.tok()
            T.wait(P.sem["scalar"], t_sq)
            for j in range(NDF):
                T.mm(p_s2, onesc[:], xsq_v[:, j * LC:(j + 1) * LC],
                           start=(j == 0), stop=(j == NDF - 1))
            T.mark()
            t_s2 = T.tok()
            S.wait(P.sem["tensor"], t_s1)
            S.op("copy", out=srow[:], in_=p_s1)
            S.wait(P.sem["tensor"], t_s2)
            S.op("copy", out=sqrow[:], in_=p_s2)
            ev["srow"] = S.tok()
            V.wait(P.sem["scalar"], ev["srow"])
            V.op("tensor_scalar", out=mrow[:], in0=srow[:], scalar1=1.0 / DM,
                 scalar2=0.0, op0=ALU.mult, op1=BYP)
            V.op("tensor_scalar", out=qrow[:], in0=sqrow[:], scalar1=1.0 / DM,
                 scalar2=0.0, op0=ALU.mult, op1=BYP)
            V.op("tensor_tensor", out=m2row[:], in0=mrow[:], in1=mrow[:], op=ALU.mult)
            V.op("tensor_tensor", out=vrow[:], in0=qrow[:], in1=m2row[:], op=ALU.subtract)
            S.wait(P.sem["vector"], V.tok())
            S.op("activation", out=sdrow[:], in_=vrow[:], func=AF.Sqrt, bias=epsc[:])
            V.wait(P.sem["scalar"], S.tok())
            V.op("reciprocal", out=rrow[:], in_=sdrow[:])
            V.op("tensor_copy", out=mrow_b[:], in_=mrow[:])
            V.op("tensor_copy", out=rrow_b[:], in_=rrow[:])
            t_rowsb = V.tok()
            T.wait(P.sem["vector"], t_rowsb)
            T.wait(P.sem["scalar"], ev["rbc"])
            T.mm(p_bc[:], onesr[:], mrow_b[:], start=True, stop=True)
            T.mark()
            S.wait(P.sem["tensor"], T.tok())
            S.op("copy", out=m_bc[:], in_=p_bc[:])
            T.wait(P.sem["scalar"], S.tok())
            T.mm(p_bc[:], onesr[:], rrow_b[:], start=True, stop=True)
            T.mark()
            S.wait(P.sem["tensor"], T.tok())
            S.op("copy", out=r_bc[:], in_=p_bc[:])
            ev["rbc"] = S.tok()
            V.wait(P.sem["scalar"], ev["rbc"])
            for j in range(NDF):
                V.op("tensor_tensor", out=xn_t[:], in0=xall[:, j * LC:(j + 1) * LC],
                     in1=m_bc[:], op=ALU.subtract)
                V.op("tensor_tensor", out=xnb_v[:, j * LC:(j + 1) * LC], in0=xn_t[:],
                     in1=r_bc[:], op=ALU.mult)
            t_norm = V.tok()
            Y.wait(P.sem["vector"], t_norm)
            for jj in range(NDT):
                Y.dma(xn_sc[jj * 128:(jj + 1) * 128, cs],
                      xnb_v[:, jj * LC:(jj + 1) * LC], xn_s, xn_c)
            T.wait(P.sem["vector"], t_norm)
            T.wait(P.sem["scalar"], ev["dtin"])
            for j in range(NDF):
                T.mm(p_dtin[:], wdtin_b[:, j * 128:(j + 1) * 128],
                           xnb_v[:, j * LC:(j + 1) * LC],
                           start=(j == 0), stop=(j == NDF - 1))
            T.mark()
            t_g1 = T.tok()
            T.wait(P.sem["scalar"], ev["bcr"])
            for j in range(NDF):
                T.mm(p_bcr, wbcr_b[:, j * 68:(j + 1) * 68],
                           xnb_v[:, j * LC:(j + 1) * LC],
                           start=(j == 0), stop=(j == NDF - 1))
            T.mark()
            t_g2 = T.tok()
            T.wait(P.sem["scalar"], ev["mid"])
            for j in range(NDF):
                T.mm(p_mid, wmid_b[:, j * 64:(j + 1) * 64],
                           xnb_v[:, j * LC:(j + 1) * LC],
                           start=(j == 0), stop=(j == NDF - 1))
            T.mark()
            t_g3 = T.tok()
            S.wait(P.sem["tensor"], t_g1)
            S.op("copy", out=dtin_f[:, cs], in_=p_dtin[:])
            ev["dtin"] = S.tok()
            S.wait(P.sem["tensor"], t_g2)
            S.op("copy", out=Bfb[:, cs], in_=p_bcrF[0:16, :])
            S.op("copy", out=Cfb[:, cs], in_=p_bcrF[32:48, :])
            S.op("copy", out=lgf[:, cs], in_=p_bcrF[64:68, :])
            ev["bcr"] = S.tok()
            S.wait(P.sem["tensor"], t_g3)
            S.op("copy", out=mid0b[:, cs], in_=p_midF[0:64, :])
            ev["mid"] = S.tok()

        Y.wait(P.sem["scalar"], S.tok())
        Y.dma(Bb_sc[:], Bfb[:], st_s, st_c)
        Y.dma(Cb_sc[:], Cfb[:], st_s, st_c)
        G.wait(P.sem["tensor"], T.tok())
        G.dma(AR3[:, 0:DH], w_dt[:], wlg_s, wlg_c)
        G.dma(AR3[0:64, 2048:2048 + DH], w_B0[:], wlg_s, wlg_c)
        V.wait(wlg_s, wlg_c["n"])
        V.wait(P.sem["tensor"], T.tok())
        V.op("tensor_copy", out=wdt_b[:], in_=AR3[:, 0:DH])
        V.op("tensor_copy", out=wB0_b[:], in_=AR3[0:64, 2048:2048 + DH])
        t_wdt_cast = V.tok()
        t_lgT_evac = 0
        t_gmath = 0
        t_gfm_evac = 0
        for q in range(L // 128):
            T.wait(P.sem["scalar"], S.tok() if q == 0 else t_lgT_evac)
            T.wait(P.sem["vector"], t_gmath)
            T.raw("transpose", p_g, lgf[:, q * 128:(q + 1) * 128], identb[0:4, 0:4])
            T.mark()
            S.wait(P.sem["tensor"], T.tok())
            S.op("copy", out=lgT[:], in_=p_g)
            t_lgT_evac = S.tok()
            V.wait(P.sem["scalar"], t_lgT_evac)
            V.op("tensor_reduce", out=g_m1[:], in_=lgT[:], axis=AX.X, op=ALU.max)
            V.op("tensor_scalar", out=g_nm1[:], in0=g_m1[:], scalar1=-1.0, scalar2=0.0,
                 op0=ALU.mult, op1=BYP)
            V.op("tensor_scalar", out=g_ind[:], in0=lgT[:], scalar1=g_m1[:], scalar2=0.0,
                 op0=ALU.is_ge, op1=BYP)
            V.op("scalar_tensor_tensor", out=g_t[:], in0=g_ind[:], scalar=-1e30,
                 in1=lgT[:], op0=ALU.mult, op1=ALU.add)
            V.op("tensor_reduce", out=g_m2[:], in_=g_t[:], axis=AX.X, op=ALU.max)
            V.op("tensor_scalar", out=g_ind[:], in0=lgT[:], scalar1=g_m2[:], scalar2=0.0,
                 op0=ALU.is_ge, op1=BYP)
            S.wait(P.sem["vector"], V.tok())
            S.op("activation", out=g_e[:], in_=lgT[:], func=AF.Exp, bias=g_nm1[:])
            V.wait(P.sem["scalar"], S.tok())
            V.op("tensor_tensor", out=g_raw[:], in0=g_e[:], in1=g_ind[:], op=ALU.mult)
            V.op("tensor_reduce", out=g_s[:], in_=g_raw[:], axis=AX.X, op=ALU.add)
            V.op("reciprocal", out=g_sr[:], in_=g_s[:])
            V.op("tensor_scalar", out=g_raw[:], in0=g_raw[:], scalar1=g_sr[:],
                 scalar2=0.0, op0=ALU.mult, op1=BYP)
            V.op("tensor_copy", out=g_tok[:], in_=g_raw[:])
            t_gmath = V.tok()
            T.wait(P.sem["vector"], t_gmath)
            T.wait(P.sem["scalar"], t_gfm_evac)
            T.raw("transpose", p_gb, g_tok[:], identb[:])
            T.mark()
            S.wait(P.sem["tensor"], T.tok())
            S.op("copy", out=g_fm[:, q * 128:(q + 1) * 128], in_=p_gb)
            t_gfm_evac = S.tok()
        for lc in range(NLC):
            cs = slice(lc * LC, (lc + 1) * LC)
            T.wait(P.sem["scalar"], S.tok())
            T.mm(p_bcrF[0:64, :], sel0_b[:], g_fm[:, cs], start=True, stop=True)
            T.mark()
            S.wait(P.sem["tensor"], T.tok())
            S.op("activation", out=gbc[:, cs], in_=p_bcrF[0:64, :], func=AF.Copy,
                 scale=4.0)
        V.wait(P.sem["scalar"], S.tok())
        V.op("tensor_tensor", out=mid0p[:], in0=mid0b[:], in1=gbc[:], op=ALU.mult)
        t_mid0p = V.tok()

        V.op("memset", state[:], 0.0)
        st_pp = [0]
        t_x1 = 0
        t_scan_done = 0
        for lc in range(NLC):
            cs = slice(lc * LC, (lc + 1) * LC)
            G.wait(st_s, 32)
            G.wait(P.sem["vector"], max(t_scan_done, t_wdt_cast))
            G.dma(BrepF_v[:].rearrange("p (n l) -> p n l", n=16),
                  Bb_sc[:, cs].unsqueeze(0).broadcast_to((128, 16, LC)), br_s, br_c)
            G.dma(wslab[0][:].rearrange("p (n l) -> p n l", n=8),
                  Cb_sc[0:8, cs].unsqueeze(0).broadcast_to((128, 8, LC)), br_s, br_c)
            G.dma(wslab[1][:].rearrange("p (n l) -> p n l", n=8),
                  Cb_sc[8:16, cs].unsqueeze(0).broadcast_to((128, 8, LC)), br_s, br_c)
            br_now = br_c["n"]
            Y.wait(xn_s, xn_c["n"])
            for jj in range(NDT):
                Y.wait(P.sem["vector"], V.tok())
                Y.dma(xnA[jj % 2][:], xn_sc[jj * 128:(jj + 1) * 128, cs], ld_s, ld_c)
                ld_xn = ld_c["n"]
                T.wait(P.sem["vector"], V.tok())
                T.wait(P.sem["scalar"], S.tok())
                T.mm(p_dtr[jj % 2][:], wdt_b[:, jj * 128:(jj + 1) * 128],
                           dtin_f[:, cs], start=True, stop=True)
                T.mark()
                t_dtr = T.tok()
                S.wait(P.sem["tensor"], t_dtr)
                S.op("activation", out=abs_t[:], in_=p_dtr[jj % 2][:], func=AF.Abs,
                     bias=dtb_t[:, jj:jj + 1])
                S.op("activation", out=exp_t[:], in_=abs_t[:], func=AF.Exp, scale=-1.0)
                S.op("activation", out=abs_t[:], in_=exp_t[:], func=AF.Ln, bias=ones1[:])
                S.op("activation", out=relu_t[:], in_=p_dtr[jj % 2][:], func=AF.Relu,
                     bias=dtb_t[:, jj:jj + 1])
                t_sp = S.tok()
                V.wait(P.sem["scalar"], t_sp)
                V.op("tensor_tensor", out=dtall_v[:, jj * LC:(jj + 1) * LC],
                     in0=relu_t[:], in1=abs_t[:], op=ALU.add)
                V.wait(ld_s, ld_xn)
                V.op("tensor_tensor", out=dtx_v[:, jj * LC:(jj + 1) * LC],
                     in0=dtall_v[:, jj * LC:(jj + 1) * LC],
                     in1=xnA[jj % 2][:], op=ALU.mult)
            t_dtdone = V.tok()
            V.wait(br_s, br_now)
            for jj in range(NDT):
                Y.wait(P.sem["vector"], V.tok())
                Y.dma(xnB[jj % 2][:], xn_sc[jj * 128:(jj + 1) * 128, cs], ld_s, ld_c)
                ld_xnB = ld_c["n"]
                S.wait(P.sem["vector"], t_scan_done if (lc + jj) > 0 else 0)
                S.wait(P.sem["vector"], t_dtdone)
                for n in range(16):
                    S.op("activation", out=dA_v[:, n * LC:(n + 1) * LC],
                         in_=dtall_v[:, jj * LC:(jj + 1) * LC], func=AF.Exp,
                         scale=negA[:, jj * 16 + n:jj * 16 + n + 1])
                t_dA = S.tok()
                V.wait(P.sem["scalar"], t_dA)
                V.op("tensor_tensor", out=u_v[:],
                     in0=dtx_v[:, jj * LC:(jj + 1) * LC].unsqueeze(1)
                     .broadcast_to((128, 16, LC)),
                     in1=Brep_v[:].rearrange("p (n l) -> p n l", n=16), op=ALU.mult)
                for n in range(16):
                    V.op("tensor_tensor_scan", out=h_v[:, n * LC:(n + 1) * LC],
                         data0=dA_v[:, n * LC:(n + 1) * LC],
                         data1=u_v[:, n * LC:(n + 1) * LC],
                         initial=state[:, jj * 16 + n:jj * 16 + n + 1],
                         op0=ALU.mult, op1=ALU.add)
                V.op("tensor_copy", out=state[:, jj * 16:(jj + 1) * 16],
                     in_=h_v[:].rearrange("p (n l) -> p n l", n=16)[:, :, LC - 1:LC])
                V.op("tensor_tensor", out=hc_v[:], in0=h_v[:], in1=Crep_v[:], op=ALU.mult)
                V.op("tensor_tensor", out=r8_t[:], in0=hc_v[:, 0:8 * LC],
                     in1=hc_v[:, 8 * LC:16 * LC], op=ALU.add)
                V.op("tensor_tensor", out=r4_t[:], in0=r8_t[:, 0:4 * LC],
                     in1=r8_t[:, 4 * LC:8 * LC], op=ALU.add)
                V.op("tensor_tensor", out=r8_t[:, 0:2 * LC], in0=r4_t[:, 0:2 * LC],
                     in1=r4_t[:, 2 * LC:4 * LC], op=ALU.add)
                V.op("tensor_tensor", out=ysc_t[:], in0=r8_t[:, 0:LC],
                     in1=r8_t[:, LC:2 * LC], op=ALU.add)
                V.wait(ld_s, ld_xnB)
                V.op("scalar_tensor_tensor", out=ym_t[:],
                     in0=xnB[jj % 2][:], scalar=Dcol[:, jj:jj + 1],
                     in1=ysc_t[:], op0=ALU.mult, op1=ALU.add)
                t_scan_done = V.tok()
                T.wait(P.sem["vector"], t_mid0p)
                T.wait(P.sem["vector"], t_x1)
                T.mm(p_d0[:], wB0_b[:, jj * 128:(jj + 1) * 128], mid0p[:, cs],
                           start=True, stop=True)
                T.mark()
                t_d0 = T.tok()
                pp = 0
                Y.wait(P.sem["vector"], t_scan_done)
                Y.wait(st_s, st_pp[pp])
                Y.dma(xr_t[pp][:], x_fm[jj * 128:(jj + 1) * 128, cs], ld_s, ld_c)
                ld_xr = ld_c["n"]
                V.wait(ld_s, ld_xr)
                V.op("tensor_tensor", out=ysc_t[:], in0=ym_t[:], in1=xr_t[pp][:],
                     op=ALU.add)
                V.wait(P.sem["tensor"], t_d0)
                V.op("tensor_tensor", out=xr_t[pp][:], in0=ysc_t[:], in1=p_d0[:],
                     op=ALU.add)
                t_x1 = V.tok()
                Y.wait(P.sem["vector"], t_x1)
                st_pp[pp] = Y.dma(x1_out[jj * 128:(jj + 1) * 128, cs], xr_t[pp][:],
                                  st_s, st_c)
        Y.wait(st_s, st_c["n"])
        Y.wait(xn_s, xn_c["n"])
    with nc.Block() as blk:
        P.finalize(blk)
    return nc


def _prep_A_inputs(inputs):
    x = inputs["L0_attn_x"]
    xpw = inputs["L0_attn_x_proj_w"]
    dtw = inputs["L0_attn_dt_proj_w"]
    dtb = inputs["L0_attn_dt_proj_b"]
    Alog = inputs["L0_attn_A_log"]
    Dv = inputs["L0_attn_D"]
    rw = inputs["L0_ffn_router_w"]
    A0 = inputs["L0_ffn_A"]
    B0 = inputs["L0_ffn_B"]

    w_dtin = np.ascontiguousarray(xpw[:128].T)
    w_bcr = np.zeros((DM, 68), np.float32)
    w_bcr[:, 0:16] = xpw[128:144].T
    w_bcr[:, 32:48] = xpw[144:160].T
    w_bcr[:, 64:68] = rw.T
    w_mid = np.ascontiguousarray(A0.reshape(64, DM).T)
    ident = np.eye(128, dtype=np.float32)
    sel0v = np.zeros((4, 64), np.float32)
    for e in range(4):
        sel0v[e, e * 16:(e + 1) * 16] = 1.0
    B0T = np.ascontiguousarray(B0.transpose(0, 2, 1).reshape(64, DM))

    maps = []
    for c in range(8):
        b, hp = c // 2, c % 2
        ds = slice(hp * DH, (hp + 1) * DH)
        perm = np.concatenate([np.arange(hp * DH, (hp + 1) * DH),
                               np.arange((1 - hp) * DH, (2 - hp) * DH)])
        m = {
            "x_fm": np.ascontiguousarray(x[b].T[perm]),
            "w_dtin": np.ascontiguousarray(w_dtin[perm]),
            "w_bcr": np.ascontiguousarray(w_bcr[perm]),
            "w_mid": np.ascontiguousarray(w_mid[perm]),
            "w_dt": np.ascontiguousarray(dtw[ds].T),
            "dt_bias": np.ascontiguousarray(dtb[ds].reshape(NDT, 128).T),
            "A_log_t": np.ascontiguousarray(
                Alog[ds].reshape(NDT, 128, 16).transpose(1, 0, 2).reshape(128, NDT * 16)),
            "D_t": np.ascontiguousarray(Dv[ds].reshape(NDT, 128).T),
            "w_B0": np.ascontiguousarray(B0T[:, ds]),
            "sel0": sel0v,
            "ident": ident,
        }
        maps.append(m)
    return maps


def run_A(inputs):
    maps = _prep_A_inputs(inputs)
    nc = build_launch_A()
    res = run_bass_kernel_spmd(nc, maps, core_ids=list(range(8)))
    x1 = []
    for b in range(4):
        h0 = res.results[2 * b]["x1_out"]
        h1 = res.results[2 * b + 1]["x1_out"]
        x1.append(np.concatenate([h0, h1], axis=0))
    return x1




def build_launch_B():
    nc = bass.Bass("TRN2", target_bir_lowering=False)
    HL = 2048
    x1_fm = nc.dram_tensor("x1_fm", [DM, L], F32, kind="ExternalInput")
    w_q = nc.dram_tensor("w_q", [DM, 2048], F32, kind="ExternalInput")
    w_k = nc.dram_tensor("w_k", [DM, 2048], F32, kind="ExternalInput")
    w_v = nc.dram_tensor("w_v", [DM, 2048], F32, kind="ExternalInput")
    w_g = nc.dram_tensor("w_g", [DM, 8], F32, kind="ExternalInput")
    w_A1 = nc.dram_tensor("w_A1", [DM, 512], F32, kind="ExternalInput")
    cosb = nc.dram_tensor("cosb", [128, L], BF16, kind="ExternalInput")
    sinb = nc.dram_tensor("sinb", [128, L], BF16, kind="ExternalInput")
    masks = nc.dram_tensor("masks", [128, 4 * 512], BF16, kind="ExternalInput")
    pswap = nc.dram_tensor("pswap", [128, 128], BF16, kind="ExternalInput")
    identbi = nc.dram_tensor("identbi", [128, 128], BF16, kind="ExternalInput")
    sel1 = nc.dram_tensor("sel1", [8, 512], BF16, kind="ExternalInput")
    oh_out = nc.dram_tensor("oh_out", [2048, L], BF16, kind="ExternalOutput")
    mid1p_out = nc.dram_tensor("mid1p_out", [512, L], BF16, kind="ExternalOutput")

    xn1_sc = nc.dram_tensor("xn1_sc", [DM, L], BF16, kind="Internal")
    q_sp = nc.dram_tensor("q_sp", [2048, L], BF16, kind="Internal")
    k_sp = nc.dram_tensor("k_sp", [2048, L], BF16, kind="Internal")
    v_sp = nc.dram_tensor("v_sp", [L, 2048], BF16, kind="Internal")

    sb = nc.alloc_sbuf_tensor
    ARB = sb("ARB", [128, 32768], F32)
    xall = ARB[:, 0:16384]
    xsq_v = ARB[:, 16384:24576].bitcast(BF16)
    xnb_v = ARB[:, 24576:32768].bitcast(BF16)
    xnh = ARB[:].bitcast(BF16)

    wstg = sb("wstg", [128, 4096], F32)
    wslab = sb("wslab", [128, 4096], BF16)
    Kh = wstg[:, 0:2048].bitcast(BF16)
    Qh = wstg[:, 2048:4096].bitcast(BF16)
    Vsb = wslab
    gT_b = sb("gT_b", [128, 8 * NDF], BF16)
    cos_s = sb("cos_s", [128, L], BF16)
    sin_s = sb("sin_s", [128, L], BF16)
    mask_s = sb("mask_s", [128, 4 * 512], BF16)
    pswap_s = sb("pswap_s", [128, 128], BF16)
    identb = sb("identb", [128, 128], BF16)
    sel1_s = sb("sel1_s", [8, 512], BF16)
    onesc = sb("onesc", [128, 1], BF16)
    onesf = sb("onesf", [128, 1], F32)
    onesr = sb("onesr", [1, 128], BF16)
    epsc = sb("epsc", [1, 1], F32)
    rows_a = sb("rows_a", [128, LC], F32)
    rows_b2 = sb("rows_b2", [128, LC], F32)
    rows_c = sb("rows_c", [1, LC], F32)
    rows_d = sb("rows_d", [1, LC], F32)
    srow = rows_a[0:1, :]
    sqrow = rows_b2[0:1, :]
    mrow = rows_c[0:1, :]
    qrow = rows_d[0:1, :]
    m2row = rows_a[0:1, :]
    vrow = rows_b2[0:1, :]
    sdrow = rows_d[0:1, :]
    rrow = rows_a[0:1, :]
    mrow_b = sb("mrow_b", [1, LC], BF16)
    rrow_b = sb("rrow_b", [1, LC], BF16)
    m_bc = sb("m_bc", [128, LC], F32)
    r_bc = sb("r_bc", [128, LC], F32)
    xn_t = rows_a
    qk_b = sb("qk_b", [128, LC], BF16)
    rot1 = [sb(f"rot1_{i}", [128, LC], BF16) for i in range(2)]
    rot2 = sb("rot2", [128, LC], BF16)
    vout_b = [sb(f"vout_b{i}", [128, 128], BF16) for i in range(2)]
    gtok = sb("gtok", [128, 8], F32)
    g_m1 = sb("g_m1", [128, 1], F32)
    g_nm1 = sb("g_nm1", [128, 1], F32)
    g_t = sb("g_t", [128, 8], F32)
    g_m2 = sb("g_m2", [128, 1], F32)
    g_e = sb("g_e", [128, 8], F32)
    g_ind = sb("g_ind", [128, 8], F32)
    g_raw = sb("g_raw", [128, 8], F32)
    g_s = sb("g_s", [128, 1], F32)
    g_sr = sb("g_sr", [128, 1], F32)
    g_tokb = sb("g_tokb", [128, 8], BF16)
    g_fm = sb("g_fm", [8, HL], BF16)
    gbc_t = sb("gbc_t", [128, LC], BF16)
    mid1_t = sb("mid1_t", [128, HL], BF16)
    PT = [sb(f"PT{i}", [128, LC], BF16) for i in range(2)]
    lrow = rows_c[0:1, :]
    lrec = rows_d[0:1, :]
    lrec_b = sb("lrec_b", [1, LC], BF16)
    lbc = sb("lbc", [128, LC], BF16)
    oh_t = sb("oh_t", [128, LC], BF16)

    ps = nc.alloc_psum_tensor
    p_stats = ps("p_stats", [128, LC], F32)
    p_s1 = p_stats[0:1, :]
    p_s2 = p_stats[32:33, :]
    p_bc = ps("p_bc", [128, LC], F32)
    p_qkv = [ps(f"p_qkv{i}", [128, LC], F32) for i in range(2)]
    p_sw = ps("p_sw", [128, LC], F32)
    p_o = ps("p_o", [128, LC], F32)
    p_l = ps("p_l", [128, LC], F32)
    p_g8 = p_stats[:, 0:8]
    p_gt = p_bc[0:8, 0:64].bitcast(BF16)
    p_s = p_sw

    P = Prog(nc)
    V, S, T, G, Y = (P.eng["vector"], P.eng["scalar"], P.eng["tensor"],
                     P.eng["gpsimd"], P.eng["sync"])
    if True:
        ld_s, ld_c = P.dma_sem("ld")
        wl_s, wl_c = P.dma_sem("wl")
        wlg_s, wlg_c = P.dma_sem("wlg")
        st_s, st_c = P.dma_sem("st")
        sp_s, sp_c = P.dma_sem("sp")

        Y.dma(cos_s[:], cosb[:], wl_s, wl_c)
        Y.dma(sin_s[:], sinb[:], wl_s, wl_c)
        Y.dma(mask_s[:], masks[:], wl_s, wl_c)
        Y.dma(pswap_s[:], pswap[:], wl_s, wl_c)
        Y.dma(identb[:], identbi[:], wl_s, wl_c)
        Y.dma(sel1_s[:], sel1[:], wl_s, wl_c)
        G.dma(wstg[:, 0:8 * NDF].rearrange("p (c k) -> p c k", k=8),
              w_g[:].rearrange("(c p) k -> p c k", p=128), wlg_s, wlg_c)
        V.wait(wl_s, wl_c["n"])
        V.wait(wlg_s, wlg_c["n"])
        V.op("memset", onesc[:], 1.0)
        V.op("memset", onesf[:], 1.0)
        V.op("memset", onesr[:], 1.0)
        V.op("memset", epsc[:], 1e-5)
        V.op("tensor_copy", out=gT_b[:], in_=wstg[:, 0:8 * NDF])
        t_prep = V.tok()
        T.wait(P.sem["vector"], t_prep)
        S.wait(P.sem["vector"], t_prep)

        ev = {"srow": 0, "rbc": 0}
        t_norm = 0
        for lc in range(NLC):
            cs = slice(lc * LC, (lc + 1) * LC)
            for j in range(NDF):
                if lc > 0 and j == 0:
                    Y.wait(P.sem["tensor"], T.tok())
                    Y.wait(P.sem["scalar"], S.tok())
                    Y.wait(P.sem["vector"], t_norm)
                Y.dma(xall[:, j * LC:(j + 1) * LC],
                      x1_fm[j * 128:(j + 1) * 128, cs], ld_s, ld_c)
            ld_now = ld_c["n"]
            T.wait(P.sem["scalar"], ev["srow"])
            for j in range(NDF):
                T.wait(ld_s, ld_now - 16 * (NDF - 1 - j))
                T.mm(p_s1, onesf[:], xall[:, j * LC:(j + 1) * LC],
                     start=(j == 0), stop=(j == NDF - 1))
            T.mark()
            t_s1 = T.tok()
            S.wait(ld_s, ld_now)
            for j in range(NDF):
                S.op("activation", out=xsq_v[:, j * LC:(j + 1) * LC],
                     in_=xall[:, j * LC:(j + 1) * LC], func=AF.Square)
            t_sq = S.tok()
            T.wait(P.sem["scalar"], t_sq)
            for j in range(NDF):
                T.mm(p_s2, onesc[:], xsq_v[:, j * LC:(j + 1) * LC],
                     start=(j == 0), stop=(j == NDF - 1))
            T.mark()
            t_s2 = T.tok()
            S.wait(P.sem["tensor"], t_s1)
            S.op("copy", out=srow[:], in_=p_s1)
            S.wait(P.sem["tensor"], t_s2)
            S.op("copy", out=sqrow[:], in_=p_s2)
            ev["srow"] = S.tok()
            V.wait(P.sem["scalar"], ev["srow"])
            V.op("tensor_scalar", out=mrow[:], in0=srow[:], scalar1=1.0 / DM,
                 scalar2=0.0, op0=ALU.mult, op1=BYP)
            V.op("tensor_scalar", out=qrow[:], in0=sqrow[:], scalar1=1.0 / DM,
                 scalar2=0.0, op0=ALU.mult, op1=BYP)
            V.op("tensor_tensor", out=m2row[:], in0=mrow[:], in1=mrow[:], op=ALU.mult)
            V.op("tensor_tensor", out=vrow[:], in0=qrow[:], in1=m2row[:], op=ALU.subtract)
            S.wait(P.sem["vector"], V.tok())
            S.op("activation", out=sdrow[:], in_=vrow[:], func=AF.Sqrt, bias=epsc[:])
            V.wait(P.sem["scalar"], S.tok())
            V.op("reciprocal", out=rrow[:], in_=sdrow[:])
            V.op("tensor_copy", out=mrow_b[:], in_=mrow[:])
            V.op("tensor_copy", out=rrow_b[:], in_=rrow[:])
            t_rowsb = V.tok()
            T.wait(P.sem["vector"], t_rowsb)
            T.wait(P.sem["scalar"], ev["rbc"])
            T.mm(p_bc[:], onesr[:], mrow_b[:], start=True, stop=True)
            T.mark()
            S.wait(P.sem["tensor"], T.tok())
            S.op("copy", out=m_bc[:], in_=p_bc[:])
            T.wait(P.sem["scalar"], S.tok())
            T.mm(p_bc[:], onesr[:], rrow_b[:], start=True, stop=True)
            T.mark()
            S.wait(P.sem["tensor"], T.tok())
            S.op("copy", out=r_bc[:], in_=p_bc[:])
            ev["rbc"] = S.tok()
            V.wait(P.sem["scalar"], ev["rbc"])
            for j in range(NDF):
                V.op("tensor_tensor", out=xn_t[:], in0=xall[:, j * LC:(j + 1) * LC],
                     in1=m_bc[:], op=ALU.subtract)
                V.op("tensor_tensor", out=xnb_v[:, j * LC:(j + 1) * LC], in0=xn_t[:],
                     in1=r_bc[:], op=ALU.mult)
                t_nj = V.tok()
                Y.wait(P.sem["vector"], t_nj)
                Y.dma(xn1_sc[j * 128:(j + 1) * 128, cs],
                      xnb_v[:, j * LC:(j + 1) * LC], st_s, st_c)
            t_norm = V.tok()

        t_cast = 0
        t_rot = [0, 0]
        t_ve = [0, 0]
        for lh in range(2):
            hs = slice(lh * HL, (lh + 1) * HL)
            Y.wait(st_s, st_c["n"])
            Y.wait(P.sem["tensor"], T.tok())
            Y.wait(P.sem["scalar"], S.tok())
            for j in range(NDF):
                Y.dma(xnh[:, j * HL:(j + 1) * HL],
                      xn1_sc[j * 128:(j + 1) * 128, hs], ld_s, ld_c)
            ld_xnh = ld_c["n"]
            T.wait(ld_s, ld_xnh)
            for w_in, spill in ((w_k, k_sp), (w_q, q_sp)):
                for mch in range(16):
                    G.wait(P.sem["vector"], t_cast)
                    G.wait(P.sem["tensor"], T.tok())
                    G.dma(wstg[:, 0:4096].rearrange("p (c k) -> p c k", k=128),
                          w_in[:, mch * 128:(mch + 1) * 128]
                          .rearrange("(c p) k -> p c k", p=128), wlg_s, wlg_c)
                    V.wait(wlg_s, wlg_c["n"])
                    V.wait(P.sem["tensor"], T.tok())
                    V.op("tensor_copy", out=wslab[:], in_=wstg[:, 0:4096])
                    t_cast = V.tok()
                    for lt in range(HL // LC):
                        gcs = lh * HL + lt * LC
                        pp = lt % 2
                        T.wait(P.sem["vector"], t_cast)
                        T.wait(P.sem["scalar"], S.tok())
                        T.wait(P.sem["vector"], V.tok())
                        for kc in range(NDF):
                            T.mm(p_qkv[pp][:], wslab[:, kc * 128:(kc + 1) * 128],
                                 xnh[:, kc * HL + lt * LC:kc * HL + (lt + 1) * LC],
                                 start=(kc == 0), stop=(kc == NDF - 1))
                        T.mark()
                        t_mm = T.tok()
                        S.wait(P.sem["tensor"], t_mm)
                        S.wait(P.sem["vector"], V.tok())
                        S.op("copy", out=qk_b[:], in_=p_qkv[pp][:])
                        t_qkb = S.tok()
                        T.wait(P.sem["scalar"], t_qkb)
                        T.mm(p_sw[:], pswap_s[:], qk_b[:], start=True, stop=True)
                        T.mark()
                        t_sw = T.tok()
                        V.wait(P.sem["tensor"], t_sw)
                        V.wait(sp_s, t_rot[pp])
                        V.op("tensor_tensor", out=rot1[pp][:], in0=p_sw[:],
                             in1=sin_s[:, gcs:gcs + LC], op=ALU.mult)
                        V.op("tensor_tensor", out=rot2[:], in0=qk_b[:],
                             in1=cos_s[:, gcs:gcs + LC], op=ALU.mult)
                        V.op("tensor_tensor", out=rot1[pp][:], in0=rot1[pp][:],
                             in1=rot2[:], op=ALU.add)
                        t_rv = V.tok()
                        Y.wait(P.sem["vector"], t_rv)
                        t_rot[pp] = Y.dma(
                            spill[mch * 128:(mch + 1) * 128, gcs:gcs + LC],
                            rot1[pp][:], sp_s, sp_c)
            for dvc in range(16):
                G.wait(P.sem["vector"], t_cast)
                G.wait(P.sem["tensor"], T.tok())
                G.dma(wstg[:, 0:4096].rearrange("p (c k) -> p c k", k=128),
                      w_v[:, dvc * 128:(dvc + 1) * 128]
                      .rearrange("(c p) k -> p c k", p=128), wlg_s, wlg_c)
                V.wait(wlg_s, wlg_c["n"])
                V.wait(P.sem["tensor"], T.tok())
                V.op("tensor_copy", out=wslab[:], in_=wstg[:, 0:4096])
                t_cast = V.tok()
                for lt in range(HL // 128):
                    l0 = lh * HL + lt * 128
                    pp = lt % 2
                    T.wait(P.sem["vector"], t_cast)
                    T.wait(P.sem["scalar"], S.tok())
                    for kc in range(NDF):
                        T.mm(p_qkv[pp][0:128, 0:128],
                             xnh[:, kc * HL + lt * 128:kc * HL + (lt + 1) * 128],
                             wslab[:, kc * 128:(kc + 1) * 128],
                             start=(kc == 0), stop=(kc == NDF - 1))
                    T.mark()
                    t_mm = T.tok()
                    S.wait(P.sem["tensor"], t_mm)
                    S.wait(sp_s, t_ve[pp])
                    S.op("copy", out=vout_b[pp][:], in_=p_qkv[pp][0:128, 0:128])
                    t_vec = S.tok()
                    Y.wait(P.sem["scalar"], t_vec)
                    t_ve[pp] = Y.dma(v_sp[l0:l0 + 128, dvc * 128:(dvc + 1) * 128],
                                     vout_b[pp][:], sp_s, sp_c)
            for ltt in range(HL // 128):
                T.wait(P.sem["scalar"], S.tok())
                for kc in range(NDF):
                    T.mm(p_g8, xnh[:, kc * HL + ltt * 128:kc * HL + (ltt + 1) * 128],
                         gT_b[:, kc * 8:(kc + 1) * 8], start=(kc == 0),
                         stop=(kc == NDF - 1))
                T.mark()
                t_gmm = T.tok()
                S.wait(P.sem["tensor"], t_gmm)
                S.op("copy", out=gtok[:], in_=p_g8)
                t_ge = S.tok()
                V.wait(P.sem["scalar"], t_ge)
                V.op("tensor_reduce", out=g_m1[:], in_=gtok[:], axis=AX.X, op=ALU.max)
                V.op("tensor_scalar", out=g_nm1[:], in0=g_m1[:], scalar1=-1.0,
                     scalar2=0.0, op0=ALU.mult, op1=BYP)
                V.op("tensor_scalar", out=g_ind[:], in0=gtok[:], scalar1=g_m1[:],
                     scalar2=0.0, op0=ALU.is_ge, op1=BYP)
                V.op("scalar_tensor_tensor", out=g_t[:], in0=g_ind[:], scalar=-1e30,
                     in1=gtok[:], op0=ALU.mult, op1=ALU.add)
                V.op("tensor_reduce", out=g_m2[:], in_=g_t[:], axis=AX.X, op=ALU.max)
                V.op("tensor_scalar", out=g_ind[:], in0=gtok[:], scalar1=g_m2[:],
                     scalar2=0.0, op0=ALU.is_ge, op1=BYP)
                S.wait(P.sem["vector"], V.tok())
                S.op("activation", out=g_e[:], in_=gtok[:], func=AF.Exp, bias=g_nm1[:])
                V.wait(P.sem["scalar"], S.tok())
                V.op("tensor_tensor", out=g_raw[:], in0=g_e[:], in1=g_ind[:], op=ALU.mult)
                V.op("tensor_reduce", out=g_s[:], in_=g_raw[:], axis=AX.X, op=ALU.add)
                V.op("reciprocal", out=g_sr[:], in_=g_s[:])
                V.op("tensor_scalar", out=g_raw[:], in0=g_raw[:], scalar1=g_sr[:],
                     scalar2=0.0, op0=ALU.mult, op1=BYP)
                V.op("tensor_copy", out=g_tokb[:], in_=g_raw[:])
                t_gmath = V.tok()
                T.wait(P.sem["vector"], t_gmath)
                T.wait(P.sem["scalar"], S.tok())
                T.raw("transpose", p_gt, g_tokb[:], identb[:])
                T.mark()
                S.wait(P.sem["tensor"], T.tok())
                S.op("copy", out=g_fm[:, ltt * 128:(ltt + 1) * 128], in_=p_gt)
            for erc in range(4):
                G.wait(P.sem["vector"], t_cast)
                G.wait(P.sem["tensor"], T.tok())
                G.dma(wstg[:, 0:4096].rearrange("p (c k) -> p c k", k=128),
                      w_A1[:, erc * 128:(erc + 1) * 128]
                      .rearrange("(c p) k -> p c k", p=128), wlg_s, wlg_c)
                V.wait(wlg_s, wlg_c["n"])
                V.wait(P.sem["tensor"], T.tok())
                V.op("tensor_copy", out=wslab[:], in_=wstg[:, 0:4096])
                t_cast = V.tok()
                for lt in range(HL // LC):
                    pp = lt % 2
                    T.wait(P.sem["vector"], t_cast)
                    T.wait(P.sem["vector"], V.tok())
                    for kc in range(NDF):
                        T.mm(p_qkv[pp][:], wslab[:, kc * 128:(kc + 1) * 128],
                             xnh[:, kc * HL + lt * LC:kc * HL + (lt + 1) * LC],
                             start=(kc == 0), stop=(kc == NDF - 1))
                    T.mark()
                    t_mm = T.tok()
                    T.wait(P.sem["scalar"], S.tok())
                    T.mm(p_sw[:], sel1_s[:, erc * 128:(erc + 1) * 128],
                         g_fm[:, lt * LC:(lt + 1) * LC], start=True, stop=True)
                    T.mark()
                    t_gbc = T.tok()
                    S.wait(P.sem["tensor"], t_gbc)
                    S.op("copy", out=gbc_t[:], in_=p_sw[:])
                    t_gbce = S.tok()
                    V.wait(P.sem["tensor"], t_mm)
                    V.wait(P.sem["scalar"], t_gbce)
                    V.op("tensor_tensor", out=mid1_t[:, lt * LC:(lt + 1) * LC],
                         in0=p_qkv[pp][:], in1=gbc_t[:], op=ALU.mult)
                t_mid = V.tok()
                Y.wait(P.sem["vector"], t_mid)
                Y.dma(mid1p_out[erc * 128:(erc + 1) * 128, hs], mid1_t[:], st_s, st_c)

        t_oh_dma = 0
        for h in range(16):
            Y.wait(sp_s, sp_c["n"])
            Y.wait(P.sem["tensor"], T.tok())
            Y.wait(P.sem["scalar"], S.tok())
            Y.wait(P.sem["vector"], V.tok())
            Y.dma(Kh[:], k_sp[h * 128:(h + 1) * 128, :], ld_s, ld_c)
            Y.dma(Qh[:], q_sp[h * 128:(h + 1) * 128, :], ld_s, ld_c)
            Y.dma(Vsb[:].rearrange("p (c k) -> p c k", k=128),
                  v_sp[:, h * 128:(h + 1) * 128].rearrange("(c p) k -> p c k", p=128),
                  ld_s, ld_c)
            ld_h = ld_c["n"]
            T.wait(ld_s, ld_h)
            for qb in range(NLC):
                nkt = 4 * (qb + 1)
                T.wait(P.sem["vector"], V.tok())
                T.wait(P.sem["scalar"], S.tok())
                t_exp_last = 0
                for kt in range(nkt):
                    pp = kt % 2
                    T.wait(P.sem["scalar"], t_exp_last)
                    T.mm(p_s[:], Kh[:, kt * 128:(kt + 1) * 128],
                         Qh[:, qb * LC:(qb + 1) * LC], start=True, stop=True)
                    T.mark()
                    t_sc = T.tok()
                    S.wait(P.sem["tensor"], t_sc)
                    S.wait(P.sem["vector"], V.tok())
                    S.op("activation", out=PT[pp][:], in_=p_s[:], func=AF.Exp,
                         scale=0.08838834764831845)
                    t_exp_last = S.tok()
                    if kt >= nkt - 4:
                        mj = kt - (nkt - 4)
                        V.wait(P.sem["scalar"], t_exp_last)
                        V.op("tensor_tensor", out=PT[pp][:], in0=PT[pp][:],
                             in1=mask_s[:, mj * 512:(mj + 1) * 512], op=ALU.mult)
                        T.wait(P.sem["vector"], V.tok())
                    else:
                        T.wait(P.sem["scalar"], t_exp_last)
                    T.mm(p_o[:], Vsb[:, kt * 128:(kt + 1) * 128], PT[pp][:],
                         start=(kt == 0), stop=(kt == nkt - 1))
                    T.mm(p_l[0:1, :], onesc[:], PT[pp][:], start=(kt == 0),
                         stop=(kt == nkt - 1))
                    T.mark()
                t_av = T.tok()
                S.wait(P.sem["tensor"], t_av)
                S.op("copy", out=lrow[:], in_=p_l[0:1, :])
                t_lrow = S.tok()
                V.wait(P.sem["scalar"], t_lrow)
                V.op("reciprocal", out=lrec[:], in_=lrow[:])
                V.op("tensor_copy", out=lrec_b[:], in_=lrec[:])
                t_lr = V.tok()
                T.wait(P.sem["vector"], t_lr)
                T.wait(P.sem["scalar"], S.tok())
                T.mm(p_bc[:], onesr[:], lrec_b[:], start=True, stop=True)
                T.mark()
                t_lb = T.tok()
                S.wait(P.sem["tensor"], t_lb)
                S.op("copy", out=lbc[:], in_=p_bc[:])
                t_lbc = S.tok()
                V.wait(P.sem["scalar"], t_lbc)
                V.wait(st_s, t_oh_dma)
                V.op("tensor_tensor", out=oh_t[:], in0=p_o[:], in1=lbc[:], op=ALU.mult)
                t_oh = V.tok()
                Y.wait(P.sem["vector"], t_oh)
                t_oh_dma = Y.dma(oh_out[h * 128:(h + 1) * 128, qb * LC:(qb + 1) * LC],
                                 oh_t[:], st_s, st_c)
        Y.wait(st_s, st_c["n"])
    with nc.Block() as blk:
        P.finalize(blk)
    return nc


def build_launch_C():
    nc = bass.Bass("TRN2", target_bir_lowering=False)
    HL = 2048
    x1l = nc.dram_tensor("x1l", [DM, HL], F32, kind="ExternalInput")
    ohf = nc.dram_tensor("ohf", [DM, HL], BF16, kind="ExternalInput")
    w_o = nc.dram_tensor("w_o", [DM, DM], F32, kind="ExternalInput")
    w_B1 = nc.dram_tensor("w_B1", [512, DM], F32, kind="ExternalInput")
    mid1p = nc.dram_tensor("mid1p", [512, HL], BF16, kind="ExternalInput")
    x2_out = nc.dram_tensor("x2_out", [DM, HL], F32, kind="ExternalOutput")

    sb = nc.alloc_sbuf_tensor
    ohs = sb("ohs", [128, NDF * HL // 2], F32)
    oh_v = ohs[:].bitcast(BF16)
    mids = sb("mids", [128, 4 * HL // 2], F32)
    mid_v = mids[:].bitcast(BF16)
    wstg = sb("wstg", [128, 4096], F32)
    wslab = sb("wslab", [128, 4096], BF16)
    bstg = sb("bstg", [128, 512], F32)
    bslab = sb("bslab", [128, 512], BF16)
    x1t = [sb(f"x1t{i}", [128, LC], F32) for i in range(2)]
    x2t = [sb("x2t0", [128, LC], F32), yh1]

    ps = nc.alloc_psum_tensor
    p_y = [ps(f"p_y{i}", [128, LC], F32) for i in range(2)]

    P = Prog(nc)
    V, S, T, G, Y = (P.eng["vector"], P.eng["scalar"], P.eng["tensor"],
                     P.eng["gpsimd"], P.eng["sync"])
    if True:
        ld_s, ld_c = P.dma_sem("ld")
        wlg_s, wlg_c = P.dma_sem("wlg")
        st_s, st_c = P.dma_sem("st")

        for j in range(NDF):
            Y.dma(oh_v[:, j * HL:(j + 1) * HL],
                  ohf[j * 128:(j + 1) * 128, :], ld_s, ld_c)
        for j in range(4):
            Y.dma(mid_v[:, j * HL:(j + 1) * HL],
                  mid1p[j * 128:(j + 1) * 128, :], ld_s, ld_c)
        ld_all = ld_c["n"]
        T.wait(ld_s, ld_all)
        t_cast = 0
        t_x2 = [0, 0]
        st_t = [0, 0]
        for m in range(NDF):
            G.wait(P.sem["vector"], t_cast)
            G.wait(P.sem["tensor"], T.tok())
            G.dma(wstg[:, 0:4096].rearrange("p (c k) -> p c k", k=128),
                  w_o[:, m * 128:(m + 1) * 128].rearrange("(c p) k -> p c k", p=128),
                  wlg_s, wlg_c)
            G.dma(bstg[:, 0:512].rearrange("p (c k) -> p c k", k=128),
                  w_B1[:, m * 128:(m + 1) * 128].rearrange("(c p) k -> p c k", p=128),
                  wlg_s, wlg_c)
            V.wait(wlg_s, wlg_c["n"])
            V.wait(P.sem["tensor"], T.tok())
            V.op("tensor_copy", out=wslab[:], in_=wstg[:, 0:4096])
            V.op("tensor_copy", out=bslab[:], in_=bstg[:, 0:512])
            t_cast = V.tok()
            for lt in range(HL // LC):
                pp = lt % 2
                Y.wait(P.sem["vector"], t_x2[pp])
                Y.dma(x1t[pp][:], x1l[m * 128:(m + 1) * 128, lt * LC:(lt + 1) * LC],
                      ld_s, ld_c)
                ld_x1 = ld_c["n"]
                T.wait(P.sem["vector"], t_cast)
                T.wait(P.sem["vector"], t_x2[pp])
                for kc in range(NDF):
                    T.mm(p_y[pp][:], wslab[:, kc * 128:(kc + 1) * 128],
                         oh_v[:, kc * HL + lt * LC:kc * HL + (lt + 1) * LC],
                         start=(kc == 0), stop=False)
                for ec in range(4):
                    T.mm(p_y[pp][:], bslab[:, ec * 128:(ec + 1) * 128],
                         mid_v[:, ec * HL + lt * LC:ec * HL + (lt + 1) * LC],
                         start=False, stop=(ec == 3))
                T.mark()
                t_mm = T.tok()
                V.wait(P.sem["tensor"], t_mm)
                V.wait(ld_s, ld_x1)
                V.wait(st_s, st_t[pp])
                V.op("tensor_tensor", out=x2t[pp][:], in0=p_y[pp][:], in1=x1t[pp][:],
                     op=ALU.add)
                t_x2[pp] = V.tok()
                Y.wait(P.sem["vector"], t_x2[pp])
                st_t[pp] = Y.dma(x2_out[m * 128:(m + 1) * 128, lt * LC:(lt + 1) * LC],
                                 x2t[pp][:], st_s, st_c)
        Y.wait(st_s, st_c["n"])
    with nc.Block() as blk:
        P.finalize(blk)
    return nc


def _prep_B_inputs(inputs, x1_list):
    wq = inputs["L1_attn_wq"]
    wk = inputs["L1_attn_wk"]
    wv = inputs["L1_attn_wv"]
    gw = inputs["L1_ffn_gate_w"]
    lA = inputs["L1_ffn_lora_A"]

    t = np.arange(L, dtype=np.float64)
    cosr = np.cos(t).astype(bf16).astype(np.float32)
    sinr = np.sin(t).astype(bf16).astype(np.float32)
    cosb = np.ascontiguousarray(np.broadcast_to(cosr, (128, L))).astype(bf16)
    sinb = np.ascontiguousarray(np.broadcast_to(sinr, (128, L))).astype(bf16)
    mk = np.zeros((128, 4, 512), np.float32)
    for mj in range(4):
        for k in range(128):
            mk[k, mj, mj * 128 + k:] = 1.0
    masks = mk.reshape(128, 2048).astype(bf16)
    psw = np.zeros((128, 128), np.float32)
    for i in range(64):
        psw[2 * i + 1, 2 * i] = -1.0
        psw[2 * i, 2 * i + 1] = 1.0
    psw = psw.astype(bf16)
    identb = np.eye(128, dtype=np.float32).astype(bf16)
    sel1v = np.zeros((8, 512), np.float32)
    for e in range(8):
        sel1v[e, e * 64:(e + 1) * 64] = 1.0
    sel1v = sel1v.astype(bf16)
    wA1 = np.ascontiguousarray(lA.reshape(512, DM).T)

    maps = []
    for c in range(8):
        b, hp = c // 2, c % 2
        hsl = slice(hp * 2048, (hp + 1) * 2048)
        m = {
            "x1_fm": x1_list[b],
            "w_q": np.ascontiguousarray(wq.T[:, hsl]),
            "w_k": np.ascontiguousarray(wk.T[:, hsl]),
            "w_v": np.ascontiguousarray(wv.T[:, hsl]),
            "w_g": np.ascontiguousarray(gw.T),
            "w_A1": wA1,
            "cosb": cosb,
            "sinb": sinb,
            "masks": masks,
            "pswap": psw,
            "identbi": identb,
            "sel1": sel1v,
        }
        maps.append(m)
    return maps


def _prep_C_inputs(inputs, x1_list, oh_list, mid_list):
    wo = inputs["L1_attn_wo"]
    lB = inputs["L1_ffn_lora_B"]
    woT = np.ascontiguousarray(wo.T)
    wB1 = np.ascontiguousarray(lB.transpose(0, 2, 1).reshape(512, DM))
    maps = []
    for c in range(8):
        b, hp = c // 2, c % 2
        hsl = slice(hp * 2048, (hp + 1) * 2048)
        m = {
            "x1l": np.ascontiguousarray(x1_list[b][:, hsl]),
            "ohf": np.ascontiguousarray(oh_list[b][:, hsl]),
            "w_o": woT,
            "w_B1": wB1,
            "mid1p": np.ascontiguousarray(mid_list[c][:, hsl]),
        }
        maps.append(m)
    return maps


def kernel_unfused(**inputs):
    maps_a = _prep_A_inputs(inputs)
    nc_a = build_launch_A()
    res_a = run_bass_kernel_spmd(nc_a, maps_a, core_ids=list(range(8)))
    x1_list = []
    for b in range(4):
        h0 = np.asarray(res_a.results[2 * b]["x1_out"])
        h1 = np.asarray(res_a.results[2 * b + 1]["x1_out"])
        x1_list.append(np.ascontiguousarray(np.concatenate([h0, h1], axis=0)))

    maps_b = _prep_B_inputs(inputs, x1_list)
    nc_b = build_launch_B()
    res_b = run_bass_kernel_spmd(nc_b, maps_b, core_ids=list(range(8)))
    oh_list = []
    mid_list = []
    for b in range(4):
        o0 = np.asarray(res_b.results[2 * b]["oh_out"])
        o1 = np.asarray(res_b.results[2 * b + 1]["oh_out"])
        oh_list.append(np.ascontiguousarray(np.concatenate([o0, o1], axis=0)))
    for c in range(8):
        mid_list.append(np.asarray(res_b.results[c]["mid1p_out"]))

    maps_c = _prep_C_inputs(inputs, x1_list, oh_list, mid_list)
    nc_c = build_launch_C()
    res_c = run_bass_kernel_spmd(nc_c, maps_c, core_ids=list(range(8)))
    out = np.empty((NB, L, DM), np.float32)
    for c in range(8):
        b, hp = c // 2, c % 2
        x2 = np.asarray(res_c.results[c]["x2_out"])
        out[b, hp * 2048:(hp + 1) * 2048, :] = x2.T
    return out




def build_fused():
    nc = bass.Bass("TRN2", target_bir_lowering=False)
    NT = NDF
    HL2 = 2048
    LQ = 1024

    x_fm = nc.dram_tensor("x_fm", [DM, L], F32, kind="ExternalInput")
    w_dtin = nc.dram_tensor("w_dtin", [DM, 128], F32, kind="ExternalInput")
    w_bcr = nc.dram_tensor("w_bcr", [DM, 68], F32, kind="ExternalInput")
    w_mid = nc.dram_tensor("w_mid", [DM, 64], F32, kind="ExternalInput")
    w_dt = nc.dram_tensor("w_dt", [128, DM], F32, kind="ExternalInput")
    dt_bias = nc.dram_tensor("dt_bias", [128, NT], F32, kind="ExternalInput")
    A_log_t = nc.dram_tensor("A_log_t", [128, NT * 16], F32, kind="ExternalInput")
    D_t = nc.dram_tensor("D_t", [128, NT], F32, kind="ExternalInput")
    w_B0 = nc.dram_tensor("w_B0", [64, DM], F32, kind="ExternalInput")
    sel0 = nc.dram_tensor("sel0", [4, 64], F32, kind="ExternalInput")
    ident = nc.dram_tensor("ident", [128, 128], F32, kind="ExternalInput")
    w_q = nc.dram_tensor("w_q", [DM, DM], F32, kind="ExternalInput")
    w_k = nc.dram_tensor("w_k", [DM, DM], F32, kind="ExternalInput")
    w_v = nc.dram_tensor("w_v", [DM, DM], F32, kind="ExternalInput")
    w_g = nc.dram_tensor("w_g", [DM, 8], F32, kind="ExternalInput")
    w_A1 = nc.dram_tensor("w_A1", [DM, 512], F32, kind="ExternalInput")
    w_o = nc.dram_tensor("w_o", [DM, HL2], F32, kind="ExternalInput")
    w_B1 = nc.dram_tensor("w_B1", [512, HL2], F32, kind="ExternalInput")
    cosb = nc.dram_tensor("cosb", [128, L], BF16, kind="ExternalInput")
    sinb = nc.dram_tensor("sinb", [128, L], BF16, kind="ExternalInput")
    masks = nc.dram_tensor("masks", [128, 4 * 512], BF16, kind="ExternalInput")
    pswap = nc.dram_tensor("pswap", [128, 128], BF16, kind="ExternalInput")
    identbi = nc.dram_tensor("identbi", [128, 128], BF16, kind="ExternalInput")
    sel1 = nc.dram_tensor("sel1", [8, 512], BF16, kind="ExternalInput")
    x2_out = nc.dram_tensor("x2_out", [HL2, L], F32, kind="ExternalOutput")

    xn_sc = nc.dram_tensor("xn_sc", [DM, L], BF16, kind="Internal")
    Bb_sc = nc.dram_tensor("Bb_sc", [16, L], BF16, kind="Internal")
    Cb_sc = nc.dram_tensor("Cb_sc", [16, L], BF16, kind="Internal")
    x1_sc = nc.dram_tensor("x1_sc", [DM, L], F32, kind="Internal")
    xn1_sc = nc.dram_tensor("xn1_sc", [DM, L], BF16, kind="Internal")
    q_sp = nc.dram_tensor("q_sp", [DM, L], BF16, kind="Internal")
    k_sp = nc.dram_tensor("k_sp", [DM, L], BF16, kind="Internal")
    v_sp = nc.dram_tensor("v_sp", [L, DM], BF16, kind="Internal")
    oh_sc = nc.dram_tensor("oh_sc", [DM, L], BF16, kind="Internal")
    mid_sc = nc.dram_tensor("mid_sc", [512, L], BF16, kind="Internal")

    sb = nc.alloc_sbuf_tensor
    MEGA = sb("MEGA", [128, 16384], F32)
    xnb_v = MEGA[:, 0:8192].bitcast(BF16)
    xsq_v = MEGA[:, 8192:16384].bitcast(BF16)
    g_fm0 = MEGA[0:4, 0:2048].bitcast(BF16)
    gbc0 = MEGA[64:128, 0:2048].bitcast(BF16)
    wB0stg = MEGA[0:64, 4096:8192]
    dA_pp = [MEGA[:, 0:4096], MEGA[:, 4096:8192]]
    u_v = MEGA[:, 8192:10240].bitcast(BF16)
    h_v = MEGA[:, 10240:12288].bitcast(BF16)
    hc_v = u_v
    BrepF_v = MEGA[:, 12288:16384].bitcast(BF16)
    xnh = MEGA[:].bitcast(BF16)
    Kh = MEGA[:, 0:2048].bitcast(BF16)
    Qh = MEGA[:, 2048:4096].bitcast(BF16)
    Vsb = MEGA[:, 4096:6144].bitcast(BF16)
    PT = [MEGA[:, 6144 + i * 256:6400 + i * 256].bitcast(BF16) for i in range(2)]
    lbc = MEGA[:, 6656:6912].bitcast(BF16)
    oh_t = MEGA[:, 6912:7168].bitcast(BF16)
    oh_res = MEGA[:].bitcast(BF16)

    wtmp = sb("wtmp", [128, 4096], F32)
    Bfb = wtmp[0:16, 0:2048].bitcast(BF16)
    Cfb = wtmp[32:48, 0:2048].bitcast(BF16)
    mid0b = wtmp[64:128, 0:2048].bitcast(BF16)
    lgf = wtmp[0:4, 2048:4096].bitcast(BF16)
    wB0_b = wtmp[64:128, 2048:4096].bitcast(BF16)
    wdtin_b = sb("wdtin_b", [128, NDF * 128], BF16)
    wbcr_b = sb("wbcr_b", [128, NDF * 68], BF16)
    wmid_b = sb("wmid_b", [128, NDF * 64], BF16)
    mid_res = sb("mid_res", [128, 8192], BF16)
    wdt_b = mid_res[:, 0:4096]
    dtin_f = mid_res[:, 4096:8192]
    wslab = [sb(f"wslab{i}", [128, 4096], BF16) for i in range(2)]
    sel0_b = sb("sel0_b", [4, 64], BF16)
    identb = sb("identb", [128, 128], BF16)
    dtb_t = sb("dtb_t", [128, NT], F32)
    Alog_s = sb("Alog_s", [128, NT * 16], F32)
    negA = sb("negA", [128, NT * 16], F32)
    Dcol = sb("Dcol", [128, NT], F32)
    gT_b = sb("gT_b", [128, 8 * NDF], BF16)
    g_fm1 = sb("g_fm1", [8, L], BF16)
    cos_s = sb("cos_s", [128, L], BF16)
    sin_s = sb("sin_s", [128, L], BF16)
    mask_s = sb("mask_s", [128, 4 * 512], BF16)
    pswap_s = sb("pswap_s", [128, 128], BF16)
    sel1_s = sb("sel1_s", [8, 512], BF16)

    onesc = sb("onesc", [128, 1], BF16)
    onesf = sb("onesf", [128, 1], F32)
    onesr = sb("onesr", [1, 128], BF16)
    ones1 = sb("ones1", [128, 1], F32)
    epsc = sb("epsc", [1, 1], F32)
    rows_a = sb("rows_a", [128, LC], F32)
    rows_b2 = sb("rows_b2", [128, LC], F32)
    rows_c = sb("rows_c", [1, LC], F32)
    rows_d = sb("rows_d", [1, LC], F32)
    srow = rows_a[0:1, :]
    sqrow = rows_b2[0:1, :]
    mrow = rows_c[0:1, :]
    qrow = rows_d[0:1, :]
    m2row = rows_a[0:1, :]
    vrow = rows_b2[0:1, :]
    sdrow = rows_d[0:1, :]
    rrow = rows_a[0:1, :]
    mrow_b = sb("mrow_b", [1, LC], BF16)
    rrow_b = sb("rrow_b", [1, LC], BF16)
    m_bc = sb("m_bc", [128, LC], F32)
    r_bc = sb("r_bc", [128, LC], F32)
    xn_t = rows_a
    abs_t = sb("abs_t", [128, LC], F32)
    exp_t = rows_a
    relu_t = rows_b2
    r8_t = wdtin_b
    ysc_t = m_bc
    ym_t = r_bc
    yh1 = sb("yh1", [128, LC], F32)
    dt_f = [sb(f"dt_f{i}", [128, LC], F32) for i in range(2)]
    dtx_b = [sb(f"dtx_b{i}", [128, LC], BF16) for i in range(2)]
    xr_t = [sb("xr_t0", [128, LC], F32)]
    x1t_ = [sb("x1t_0", [128, LC], F32)]
    xnA = [sb(f"xnA{i}", [128, LC], BF16) for i in range(2)]
    state = sb("state", [128, NT * 16], F32)
    lgT = sb("lgT", [128, 4], F32)
    g_m1 = sb("g_m1", [128, 1], F32)
    g_nm1 = sb("g_nm1", [128, 1], F32)
    g_t4 = sb("g_t4", [128, 8], F32)
    g_m2 = sb("g_m2", [128, 1], F32)
    g_e = sb("g_e", [128, 8], F32)
    g_ind = sb("g_ind", [128, 8], F32)
    g_raw = sb("g_raw", [128, 8], F32)
    g_s = sb("g_s", [128, 1], F32)
    g_sr = sb("g_sr", [128, 1], F32)
    g_tok4 = sb("g_tok4", [128, 4], BF16)
    gtok = sb("gtok", [128, 8], F32)
    g_tokb = sb("g_tokb", [128, 8], BF16)
    gbc_t = sb("gbc_t", [128, LC], BF16)
    qk_b = sb("qk_b", [128, LC], BF16)
    qk_b2 = [qk_b, sb("qk_b_1", [128, LC], BF16)]
    rot1 = [sb(f"rot1_{i}", [128, LC], BF16) for i in range(2)]
    rot2 = sb("rot2", [128, LC], BF16)
    vout_b = [sb(f"vout_b{i}", [128, 128], BF16) for i in range(2)]
    mid1_t = sb("mid1_t", [128, LC], BF16)
    lrow = rows_c[0:1, :]
    lrec = rows_d[0:1, :]
    lrec_b = sb("lrec_b", [1, LC], BF16)
    x2t = [sb("x2t0", [128, LC], F32), yh1]

    ps = nc.alloc_psum_tensor
    p_stats = ps("p_stats", [128, LC], F32)
    p_s1 = p_stats[0:1, :]
    p_s2 = p_stats[32:33, :]
    p_bc = ps("p_bc", [128, LC], F32)
    p_dtin = ps("p_dtin", [128, LC], F32)
    p_bcrF = ps("p_bcrF", [128, LC], F32)
    p_bcr = p_bcrF[0:68, :]
    p_midF = ps("p_midF", [128, LC], F32)
    p_mid = p_midF[0:64, :]
    p_dtr = [p_bcrF, p_midF]
    p_d0 = p_dtin
    p_g = p_bc[:, 0:2].bitcast(BF16)
    p_gb = p_bc[0:4, 64:128].bitcast(BF16)
    p_qkv = [p_bcrF, p_midF]
    p_sw = ps("p_sw", [128, LC], F32)
    p_o = ps("p_o", [128, LC], F32)
    p_l = ps("p_l", [128, LC], F32)
    p_g8 = p_stats[:, 0:8]
    p_gt8 = p_bc[0:8, 0:64].bitcast(BF16)
    p_s = p_sw
    p_s2nd = p_dtin
    p_y = [p_bcrF, p_midF]

    P = Prog(nc)
    V, S, T, G, Y = (P.eng["vector"], P.eng["scalar"], P.eng["tensor"],
                     P.eng["gpsimd"], P.eng["sync"])
    if True:
        ld_s, ld_c = P.dma_sem("ld")
        wl_s, wl_c = P.dma_sem("wl")
        wlg_s, wlg_c = P.dma_sem("wlg")
        st_s, st_c = P.dma_sem("st")
        br_s, br_c = P.dma_sem("br")
        xn_s, xn_c = P.dma_sem("xn")
        sp_s, sp_c = P.dma_sem("sp")

        Y.dma(dtb_t[:], dt_bias[:], wl_s, wl_c)
        Y.dma(Alog_s[:], A_log_t[:], wl_s, wl_c)
        Y.dma(Dcol[:], D_t[:], wl_s, wl_c)
        Y.dma(m_bc[0:4, 0:64], sel0[:], wl_s, wl_c)
        Y.dma(cos_s[:], cosb[:], wl_s, wl_c)
        Y.dma(sin_s[:], sinb[:], wl_s, wl_c)
        Y.dma(mask_s[:], masks[:], wl_s, wl_c)
        Y.dma(pswap_s[:], pswap[:], wl_s, wl_c)
        Y.dma(identb[:], identbi[:], wl_s, wl_c)
        Y.dma(sel1_s[:], sel1[:], wl_s, wl_c)
        G.dma(wtmp[:, 0:4096].rearrange("p (c k) -> p c k", k=128),
              w_dtin[:].rearrange("(c p) k -> p c k", p=128), wlg_s, wlg_c)
        V.wait(wl_s, wl_c["n"])
        V.wait(wlg_s, wlg_c["n"])
        V.op("memset", onesc[:], 1.0)
        V.op("memset", onesf[:], 1.0)
        V.op("memset", onesr[:], 1.0)
        V.op("memset", ones1[:], 1.0)
        V.op("memset", epsc[:], 1e-5)
        V.op("tensor_copy", out=sel0_b[:], in_=m_bc[0:4, 0:64])
        V.op("tensor_copy", out=wdtin_b[:], in_=wtmp[:, 0:4096])
        G.wait(P.sem["vector"], V.tok())
        G.dma(wtmp[:, 0:NDF * 68].rearrange("p (c k) -> p c k", k=68),
              w_bcr[:].rearrange("(c p) k -> p c k", p=128), wlg_s, wlg_c)
        V.wait(wlg_s, wlg_c["n"])
        V.op("tensor_copy", out=wbcr_b[:], in_=wtmp[:, 0:NDF * 68])
        G.wait(P.sem["vector"], V.tok())
        G.dma(wtmp[:, 0:NDF * 64].rearrange("p (c k) -> p c k", k=64),
              w_mid[:].rearrange("(c p) k -> p c k", p=128), wlg_s, wlg_c)
        V.wait(wlg_s, wlg_c["n"])
        V.op("tensor_copy", out=wmid_b[:], in_=wtmp[:, 0:NDF * 64])
        G.wait(P.sem["vector"], V.tok())
        G.dma(wtmp[:, 0:4096], w_dt[:], wlg_s, wlg_c)
        V.wait(wlg_s, wlg_c["n"])
        V.op("tensor_copy", out=wdt_b[:], in_=wtmp[:, 0:4096])
        G.wait(P.sem["vector"], V.tok())
        G.dma(wtmp[:, 0:8 * NDF].rearrange("p (c k) -> p c k", k=8),
              w_g[:].rearrange("(c p) k -> p c k", p=128), wlg_s, wlg_c)
        V.wait(wlg_s, wlg_c["n"])
        V.op("tensor_copy", out=gT_b[:], in_=wtmp[:, 0:8 * NDF])
        S.wait(wl_s, wl_c["n"])
        S.op("activation", out=negA[:], in_=Alog_s[:], func=AF.Exp)
        V.wait(P.sem["scalar"], S.tok())
        V.op("tensor_scalar", out=negA[:], in0=negA[:], scalar1=-1.0, scalar2=0.0,
             op0=ALU.mult, op1=BYP)

        def ln_phase(src, dst_sc, ev, with_proj):
            for lc in range(NLC):
                cs = slice(lc * LC, (lc + 1) * LC)
                T.wait(P.sem["scalar"], ev["srow"])
                for j in range(NDF):
                    xt = xr_t[0] if j % 2 == 0 else x1t_[0]
                    Y.wait(P.sem["tensor"], T.tok())
                    Y.wait(P.sem["scalar"], S.tok())
                    Y.dma(xt[:], src[j * 128:(j + 1) * 128, cs], ld_s, ld_c)
                    ldj = ld_c["n"]
                    T.wait(ld_s, ldj)
                    T.mm(p_s1, onesf[:], xt[:], start=(j == 0), stop=(j == NDF - 1))
                    if j == NDF - 1:
                        T.mark()
                    S.wait(ld_s, ldj)
                    S.op("activation", out=xsq_v[:, j * LC:(j + 1) * LC],
                         in_=xt[:], func=AF.Square)
                t_s1 = T.tok()
                t_sq = S.tok()
                T.wait(P.sem["scalar"], t_sq)
                for j in range(NDF):
                    T.mm(p_s2, onesc[:], xsq_v[:, j * LC:(j + 1) * LC],
                         start=(j == 0), stop=(j == NDF - 1))
                T.mark()
                t_s2 = T.tok()
                S.wait(P.sem["tensor"], t_s1)
                S.op("copy", out=srow[:], in_=p_s1)
                S.wait(P.sem["tensor"], t_s2)
                S.op("copy", out=sqrow[:], in_=p_s2)
                ev["srow"] = S.tok()
                V.wait(P.sem["scalar"], ev["srow"])
                V.op("tensor_scalar", out=mrow[:], in0=srow[:], scalar1=1.0 / DM,
                     scalar2=0.0, op0=ALU.mult, op1=BYP)
                V.op("tensor_scalar", out=qrow[:], in0=sqrow[:], scalar1=1.0 / DM,
                     scalar2=0.0, op0=ALU.mult, op1=BYP)
                V.op("tensor_tensor", out=m2row[:], in0=mrow[:], in1=mrow[:],
                     op=ALU.mult)
                V.op("tensor_tensor", out=vrow[:], in0=qrow[:], in1=m2row[:],
                     op=ALU.subtract)
                S.wait(P.sem["vector"], V.tok())
                S.op("activation", out=sdrow[:], in_=vrow[:], func=AF.Sqrt,
                     bias=epsc[:])
                V.wait(P.sem["scalar"], S.tok())
                V.op("reciprocal", out=rrow[:], in_=sdrow[:])
                V.op("tensor_copy", out=mrow_b[:], in_=mrow[:])
                V.op("tensor_copy", out=rrow_b[:], in_=rrow[:])
                t_rowsb = V.tok()
                T.wait(P.sem["vector"], t_rowsb)
                T.wait(P.sem["scalar"], ev["rbc"])
                T.mm(p_bc[:], onesr[:], mrow_b[:], start=True, stop=True)
                T.mark()
                S.wait(P.sem["tensor"], T.tok())
                S.op("copy", out=m_bc[:], in_=p_bc[:])
                T.wait(P.sem["scalar"], S.tok())
                T.mm(p_bc[:], onesr[:], rrow_b[:], start=True, stop=True)
                T.mark()
                S.wait(P.sem["tensor"], T.tok())
                S.op("copy", out=r_bc[:], in_=p_bc[:])
                ev["rbc"] = S.tok()
                V.wait(P.sem["scalar"], ev["rbc"])
                for j in range(NDF):
                    xt = xr_t[0] if j % 2 == 0 else x1t_[0]
                    Y.wait(P.sem["vector"], V.tok())
                    Y.dma(xt[:], src[j * 128:(j + 1) * 128, cs], ld_s, ld_c)
                    ldj = ld_c["n"]
                    V.wait(ld_s, ldj)
                    V.op("tensor_tensor", out=xn_t[:], in0=xt[:], in1=m_bc[:],
                         op=ALU.subtract)
                    V.op("tensor_tensor", out=xnb_v[:, j * LC:(j + 1) * LC],
                         in0=xn_t[:], in1=r_bc[:], op=ALU.mult)
                t_norm = V.tok()
                Y.wait(P.sem["vector"], t_norm)
                for jj in range(NDF):
                    Y.dma(dst_sc[jj * 128:(jj + 1) * 128, cs],
                          xnb_v[:, jj * LC:(jj + 1) * LC], xn_s, xn_c)
                if with_proj:
                    T.wait(P.sem["vector"], t_norm)
                    T.wait(P.sem["scalar"], ev["dtin"])
                    for j in range(NDF):
                        T.mm(p_dtin[:], wdtin_b[:, j * 128:(j + 1) * 128],
                             xnb_v[:, j * LC:(j + 1) * LC],
                             start=(j == 0), stop=(j == NDF - 1))
                    T.mark()
                    t_g1 = T.tok()
                    T.wait(P.sem["scalar"], ev["bcr"])
                    for j in range(NDF):
                        T.mm(p_bcr, wbcr_b[:, j * 68:(j + 1) * 68],
                             xnb_v[:, j * LC:(j + 1) * LC],
                             start=(j == 0), stop=(j == NDF - 1))
                    T.mark()
                    t_g2 = T.tok()
                    T.wait(P.sem["scalar"], ev["mid"])
                    for j in range(NDF):
                        T.mm(p_mid, wmid_b[:, j * 64:(j + 1) * 64],
                             xnb_v[:, j * LC:(j + 1) * LC],
                             start=(j == 0), stop=(j == NDF - 1))
                    T.mark()
                    t_g3 = T.tok()
                    S.wait(P.sem["tensor"], t_g1)
                    S.op("copy", out=dtin_f[:, cs], in_=p_dtin[:])
                    ev["dtin"] = S.tok()
                    S.wait(P.sem["tensor"], t_g2)
                    S.op("copy", out=Bfb[:, cs], in_=p_bcrF[0:16, :])
                    S.op("copy", out=Cfb[:, cs], in_=p_bcrF[32:48, :])
                    S.op("copy", out=lgf[:, cs], in_=p_bcrF[64:68, :])
                    ev["bcr"] = S.tok()
                    S.wait(P.sem["tensor"], t_g3)
                    S.op("copy", out=mid0b[:, cs], in_=p_midF[0:64, :])
                    ev["mid"] = S.tok()
                else:
                    V.wait(xn_s, xn_c["n"])

        t_prep = V.tok()
        T.wait(P.sem["vector"], t_prep)
        S.wait(P.sem["vector"], t_prep)
        ev = {"dtin": 0, "bcr": 0, "mid": 0, "srow": 0, "rbc": 0}
        ln_phase(x_fm, xn_sc, ev, True)

        Y.wait(P.sem["scalar"], S.tok())
        Y.dma(Bb_sc[:], Bfb[:], st_s, st_c)
        Y.dma(Cb_sc[:], Cfb[:], st_s, st_c)
        G.wait(P.sem["tensor"], T.tok())
        G.wait(P.sem["scalar"], S.tok())
        G.dma(wB0stg, w_B0[:], wlg_s, wlg_c)
        V.wait(wlg_s, wlg_c["n"])
        V.wait(P.sem["tensor"], T.tok())
        V.op("tensor_copy", out=wB0_b[:], in_=wB0stg)
        t_lgT_evac = 0
        t_gmath = 0
        t_gfm_evac = 0
        for q in range(L // 128):
            T.wait(P.sem["scalar"], S.tok() if q == 0 else t_lgT_evac)
            T.wait(P.sem["vector"], t_gmath if q > 0 else V.tok())
            T.raw("transpose", p_g, lgf[:, q * 128:(q + 1) * 128], identb[0:4, 0:4])
            T.mark()
            S.wait(P.sem["tensor"], T.tok())
            S.op("copy", out=lgT[:], in_=p_g)
            t_lgT_evac = S.tok()
            V.wait(P.sem["scalar"], t_lgT_evac)
            V.op("tensor_reduce", out=g_m1[:], in_=lgT[:], axis=AX.X, op=ALU.max)
            V.op("tensor_scalar", out=g_nm1[:], in0=g_m1[:], scalar1=-1.0,
                 scalar2=0.0, op0=ALU.mult, op1=BYP)
            V.op("tensor_scalar", out=g_ind[:, 0:4], in0=lgT[:], scalar1=g_m1[:],
                 scalar2=0.0, op0=ALU.is_ge, op1=BYP)
            V.op("scalar_tensor_tensor", out=g_t4[:, 0:4], in0=g_ind[:, 0:4],
                 scalar=-1e30, in1=lgT[:], op0=ALU.mult, op1=ALU.add)
            V.op("tensor_reduce", out=g_m2[:], in_=g_t4[:, 0:4], axis=AX.X, op=ALU.max)
            V.op("tensor_scalar", out=g_ind[:, 0:4], in0=lgT[:], scalar1=g_m2[:],
                 scalar2=0.0, op0=ALU.is_ge, op1=BYP)
            S.wait(P.sem["vector"], V.tok())
            S.op("activation", out=g_e[:, 0:4], in_=lgT[:], func=AF.Exp, bias=g_nm1[:])
            V.wait(P.sem["scalar"], S.tok())
            V.op("tensor_tensor", out=g_raw[:, 0:4], in0=g_e[:, 0:4],
                 in1=g_ind[:, 0:4], op=ALU.mult)
            V.op("tensor_reduce", out=g_s[:], in_=g_raw[:, 0:4], axis=AX.X, op=ALU.add)
            V.op("reciprocal", out=g_sr[:], in_=g_s[:])
            V.op("tensor_scalar", out=g_raw[:, 0:4], in0=g_raw[:, 0:4],
                 scalar1=g_sr[:], scalar2=0.0, op0=ALU.mult, op1=BYP)
            V.op("tensor_copy", out=g_tok4[:], in_=g_raw[:, 0:4])
            t_gmath = V.tok()
            T.wait(P.sem["vector"], t_gmath)
            T.wait(P.sem["scalar"], t_gfm_evac)
            T.raw("transpose", p_gb, g_tok4[:], identb[:])
            T.mark()
            S.wait(P.sem["tensor"], T.tok())
            S.op("copy", out=g_fm0[:, q * 128:(q + 1) * 128], in_=p_gb)
            t_gfm_evac = S.tok()
        for lc in range(NLC):
            cs = slice(lc * LC, (lc + 1) * LC)
            T.wait(P.sem["scalar"], S.tok())
            T.mm(p_bcrF[0:64, :], sel0_b[:], g_fm0[:, cs], start=True, stop=True)
            T.mark()
            S.wait(P.sem["tensor"], T.tok())
            S.op("activation", out=gbc0[:, cs], in_=p_bcrF[0:64, :], func=AF.Copy,
                 scale=4.0)
        V.wait(P.sem["scalar"], S.tok())
        V.op("tensor_tensor", out=mid0b[:], in0=mid0b[:], in1=gbc0[:], op=ALU.mult)
        t_mid0p = V.tok()
        mid0p = mid0b

        V.op("memset", state[:], 0.0)
        st_x1 = 0
        t_x1 = 0
        t_scan_done = 0
        t_scan_pp = [0, 0]
        t_dtx = [0, 0]
        for lc in range(NLC):
            cs = slice(lc * LC, (lc + 1) * LC)
            G.wait(st_s, 32)
            G.wait(P.sem["vector"], max(t_scan_done, t_mid0p))
            G.wait(P.sem["tensor"], T.tok())
            G.dma(BrepF_v[:].rearrange("p (n l) -> p n l", n=16),
                  Bb_sc[:, cs].unsqueeze(0).broadcast_to((128, 16, LC)), br_s, br_c)
            G.dma(wslab[0][:].rearrange("p (n l) -> p n l", n=8),
                  Cb_sc[0:8, cs].unsqueeze(0).broadcast_to((128, 8, LC)), br_s, br_c)
            G.dma(wslab[1][:].rearrange("p (n l) -> p n l", n=8),
                  Cb_sc[8:16, cs].unsqueeze(0).broadcast_to((128, 8, LC)), br_s, br_c)
            br_now = br_c["n"]
            if lc == 0:
                Y.wait(xn_s, xn_c["n"])
            for jj in range(NT):
                pp = jj % 2
                Y.wait(P.sem["vector"], V.tok())
                Y.dma(xnA[pp][:], xn_sc[jj * 128:(jj + 1) * 128, cs], ld_s, ld_c)
                ld_xn = ld_c["n"]
                T.wait(P.sem["vector"], t_dtx[pp])
                T.wait(P.sem["scalar"], S.tok())
                T.mm(p_dtr[pp][:], wdt_b[:, jj * 128:(jj + 1) * 128],
                     dtin_f[:, cs], start=True, stop=True)
                T.mark()
                t_dtr = T.tok()
                S.wait(P.sem["tensor"], t_dtr)
                S.wait(P.sem["vector"], t_dtx[pp])
                S.op("activation", out=abs_t[:], in_=p_dtr[pp][:], func=AF.Abs,
                     bias=dtb_t[:, jj:jj + 1])
                S.op("activation", out=exp_t[:], in_=abs_t[:], func=AF.Exp, scale=-1.0)
                S.op("activation", out=abs_t[:], in_=exp_t[:], func=AF.Ln,
                     bias=ones1[:])
                S.op("activation", out=relu_t[:], in_=p_dtr[pp][:], func=AF.Relu,
                     bias=dtb_t[:, jj:jj + 1])
                t_sp = S.tok()
                V.wait(P.sem["scalar"], t_sp)
                V.op("tensor_tensor", out=dt_f[pp][:], in0=relu_t[:], in1=abs_t[:],
                     op=ALU.add)
                V.wait(ld_s, ld_xn)
                V.op("tensor_tensor", out=dtx_b[pp][:], in0=dt_f[pp][:],
                     in1=xnA[pp][:], op=ALU.mult)
                t_dtx[pp] = V.tok()
                V.wait(br_s, br_now)
                for ng in range(2):
                    S.wait(P.sem["vector"], t_scan_pp[ng])
                    S.wait(P.sem["vector"], t_dtx[pp])
                    for n in range(8):
                        S.op("activation", out=dA_pp[ng][:, n * LC:(n + 1) * LC],
                             in_=dt_f[pp][:], func=AF.Exp,
                             scale=negA[:, jj * 16 + ng * 8 + n:jj * 16 + ng * 8 + n + 1])
                    t_dA = S.tok()
                    V.wait(P.sem["scalar"], t_dA)
                    V.op("tensor_tensor", out=u_v[:],
                         in0=dtx_b[pp][:].unsqueeze(1).broadcast_to((128, 8, LC)),
                         in1=BrepF_v[:, ng * 8 * LC:(ng + 1) * 8 * LC]
                         .rearrange("p (n l) -> p n l", n=8), op=ALU.mult)
                    for n in range(8):
                        cc = jj * 16 + ng * 8 + n
                        V.op("tensor_tensor_scan", out=h_v[:, n * LC:(n + 1) * LC],
                             data0=dA_pp[ng][:, n * LC:(n + 1) * LC],
                             data1=u_v[:, n * LC:(n + 1) * LC],
                             initial=state[:, cc:cc + 1], op0=ALU.mult, op1=ALU.add)
                    V.op("tensor_copy",
                         out=state[:, jj * 16 + ng * 8:jj * 16 + ng * 8 + 8],
                         in_=h_v[:].rearrange("p (n l) -> p n l", n=8)[:, :, LC - 1:LC])
                    V.op("tensor_tensor", out=hc_v[:], in0=h_v[:],
                         in1=wslab[ng][:], op=ALU.mult)
                    V.op("tensor_tensor", out=r8_t[:, 0:4 * LC], in0=hc_v[:, 0:4 * LC],
                         in1=hc_v[:, 4 * LC:8 * LC], op=ALU.add)
                    V.op("tensor_tensor", out=r8_t[:, 4 * LC:6 * LC],
                         in0=r8_t[:, 0:2 * LC], in1=r8_t[:, 2 * LC:4 * LC], op=ALU.add)
                    if ng == 0:
                        V.op("tensor_tensor", out=yh1[:], in0=r8_t[:, 4 * LC:5 * LC],
                             in1=r8_t[:, 5 * LC:6 * LC], op=ALU.add)
                    else:
                        V.op("tensor_tensor", out=ysc_t[:], in0=r8_t[:, 4 * LC:5 * LC],
                             in1=r8_t[:, 5 * LC:6 * LC], op=ALU.add)
                        V.op("tensor_tensor", out=yh1[:], in0=yh1[:], in1=ysc_t[:],
                             op=ALU.add)
                    t_scan_pp[ng] = V.tok()
                    t_scan_done = t_scan_pp[ng]
                V.op("scalar_tensor_tensor", out=ym_t[:], in0=xnA[pp][:],
                     scalar=Dcol[:, jj:jj + 1], in1=yh1[:], op0=ALU.mult, op1=ALU.add)
                t_scan_done = V.tok()
                T.wait(P.sem["vector"], t_mid0p)
                T.wait(P.sem["vector"], t_x1)
                T.mm(p_d0[:], wB0_b[:, jj * 128:(jj + 1) * 128], mid0p[:, cs],
                     start=True, stop=True)
                T.mark()
                t_d0 = T.tok()
                Y.wait(P.sem["vector"], t_scan_done)
                Y.wait(st_s, st_x1)
                Y.dma(xr_t[0][:], x_fm[jj * 128:(jj + 1) * 128, cs], ld_s, ld_c)
                ld_xr = ld_c["n"]
                V.wait(ld_s, ld_xr)
                V.op("tensor_tensor", out=ysc_t[:], in0=ym_t[:], in1=xr_t[0][:],
                     op=ALU.add)
                V.wait(P.sem["tensor"], t_d0)
                V.op("tensor_tensor", out=xr_t[0][:], in0=ysc_t[:], in1=p_d0[:],
                     op=ALU.add)
                t_x1 = V.tok()
                Y.wait(P.sem["vector"], t_x1)
                st_x1 = Y.dma(x1_sc[jj * 128:(jj + 1) * 128, cs], xr_t[0][:],
                              st_s, st_c)

        ev2 = {"dtin": 0, "bcr": 0, "mid": 0, "srow": S.tok(), "rbc": S.tok()}
        T.wait(P.sem["vector"], V.tok())
        S.wait(P.sem["vector"], V.tok())
        Y.wait(st_s, st_c["n"])
        ln_phase(x1_sc, xn1_sc, ev2, False)

        t_cast = [0, 0]
        t_rot = [0, 0]
        t_ve = [0, 0]
        t_middma = 0
        t_qkb_pp = [0, 0]
        t_swt_pp = [0, 0]
        t_vec = 0
        t_rv_pp = [0, 0]
        t_rv_last = 0
        G.wait(P.sem["vector"], V.tok())
        G.wait(P.sem["tensor"], T.tok())
        G.wait(P.sem["scalar"], S.tok())
        for lq in range(4):
            hs = slice(lq * LQ, (lq + 1) * LQ)
            Y.wait(xn_s, xn_c["n"])
            Y.wait(P.sem["tensor"], T.tok())
            Y.wait(P.sem["scalar"], S.tok())
            Y.wait(P.sem["vector"], V.tok())
            for j in range(NDF):
                Y.dma(xnh[:, j * LQ:(j + 1) * LQ],
                      xn1_sc[j * 128:(j + 1) * 128, hs], ld_s, ld_c)
            ld_xnh = ld_c["n"]
            T.wait(ld_s, ld_xnh)
            for w_in, spill in ((w_k, k_sp), (w_q, q_sp)):
                for mch in range(32):
                    wpp = mch % 2
                    G.wait(P.sem["vector"], max(t_cast[0], t_cast[1]))
                    G.dma(wtmp[:, 0:4096].rearrange("p (c k) -> p c k", k=128),
                          w_in[:, mch * 128:(mch + 1) * 128]
                          .rearrange("(c p) k -> p c k", p=128), wlg_s, wlg_c)
                    V.wait(wlg_s, wlg_c["n"])
                    V.wait(P.sem["tensor"], T.tok())
                    V.op("tensor_copy", out=wslab[wpp][:], in_=wtmp[:, 0:4096])
                    t_cast[wpp] = V.tok()
                    for lt in range(LQ // LC):
                        gcs = lq * LQ + lt * LC
                        pp = lt % 2
                        T.wait(P.sem["vector"], t_cast[wpp])
                        T.wait(P.sem["scalar"], S.tok())
                        T.wait(P.sem["vector"], V.tok())
                        for kc in range(NDF):
                            T.mm(p_qkv[pp][:], wslab[wpp][:, kc * 128:(kc + 1) * 128],
                                 xnh[:, kc * LQ + lt * LC:kc * LQ + (lt + 1) * LC],
                                 start=(kc == 0), stop=(kc == NDF - 1))
                        T.mark()
                        t_mm = T.tok()
                        S.wait(P.sem["tensor"], t_mm)
                        S.wait(P.sem["vector"], V.tok())
                        S.op("copy", out=qk_b[:], in_=p_qkv[pp][:])
                        t_qkb = S.tok()
                        T.wait(P.sem["scalar"], t_qkb)
                        T.mm(p_sw[:], pswap_s[:], qk_b[:], start=True, stop=True)
                        T.mark()
                        t_swt = T.tok()
                        V.wait(P.sem["tensor"], t_swt)
                        V.wait(sp_s, t_rot[pp])
                        V.op("tensor_tensor", out=rot1[pp][:], in0=p_sw[:],
                             in1=sin_s[:, gcs:gcs + LC], op=ALU.mult)
                        V.op("tensor_tensor", out=rot2[:], in0=qk_b[:],
                             in1=cos_s[:, gcs:gcs + LC], op=ALU.mult)
                        V.op("tensor_tensor", out=rot1[pp][:], in0=rot1[pp][:],
                             in1=rot2[:], op=ALU.add)
                        t_rv = V.tok()
                        Y.wait(P.sem["vector"], t_rv)
                        t_rot[pp] = Y.dma(
                            spill[mch * 128:(mch + 1) * 128, gcs:gcs + LC],
                            rot1[pp][:], sp_s, sp_c)
            for dvc in range(32):
                wpp = dvc % 2
                G.wait(P.sem["vector"], max(t_cast[0], t_cast[1]))
                G.dma(wtmp[:, 0:4096].rearrange("p (c k) -> p c k", k=128),
                      w_v[:, dvc * 128:(dvc + 1) * 128]
                      .rearrange("(c p) k -> p c k", p=128), wlg_s, wlg_c)
                V.wait(wlg_s, wlg_c["n"])
                V.wait(P.sem["tensor"], T.tok())
                V.op("tensor_copy", out=wslab[wpp][:], in_=wtmp[:, 0:4096])
                t_cast[wpp] = V.tok()
                for lt in range(LQ // LC):
                    gcs = lq * LQ + lt * LC
                    pp = lt % 2
                    T.wait(P.sem["vector"], t_cast[wpp])
                    T.wait(P.sem["scalar"], t_qkb_pp[pp])
                    for kc in range(NDF):
                        T.mm(p_qkv[pp][:], wslab[wpp][:, kc * 128:(kc + 1) * 128],
                             xnh[:, kc * LQ + lt * LC:kc * LQ + (lt + 1) * LC],
                             start=(kc == 0), stop=(kc == NDF - 1))
                    T.mark()
                    t_mm = T.tok()
                    S.wait(P.sem["tensor"], t_mm)
                    S.wait(P.sem["tensor"], t_swt_pp[pp])
                    S.op("copy", out=qk_b2[pp][:], in_=p_qkv[pp][:])
                    t_qkb_pp[pp] = S.tok()
                    for sub in range(4):
                        T.wait(P.sem["scalar"], t_qkb_pp[pp] if sub == 0 else t_vec)
                        T.raw("transpose", p_sw[:, 0:64].bitcast(BF16),
                              qk_b2[pp][:, sub * 128:(sub + 1) * 128], identb[:])
                        T.mark()
                        t_tr = T.tok()
                        if sub == 3:
                            t_swt_pp[pp] = t_tr
                        spp = sub % 2
                        S.wait(P.sem["tensor"], t_tr)
                        S.wait(sp_s, t_ve[spp])
                        S.op("copy", out=vout_b[spp][:], in_=p_sw[:, 0:64].bitcast(BF16))
                        t_vec = S.tok()
                        Y.wait(P.sem["scalar"], t_vec)
                        t_ve[spp] = Y.dma(
                            v_sp[gcs + sub * 128:gcs + (sub + 1) * 128,
                                 dvc * 128:(dvc + 1) * 128],
                            vout_b[spp][:], sp_s, sp_c)
            for ltt in range(LQ // 128):
                T.wait(P.sem["scalar"], S.tok())
                for kc in range(NDF):
                    T.mm(p_g8, xnh[:, kc * LQ + ltt * 128:kc * LQ + (ltt + 1) * 128],
                         gT_b[:, kc * 8:(kc + 1) * 8], start=(kc == 0),
                         stop=(kc == NDF - 1))
                T.mark()
                t_gmm = T.tok()
                S.wait(P.sem["tensor"], t_gmm)
                S.op("copy", out=gtok[:], in_=p_g8)
                t_ge = S.tok()
                V.wait(P.sem["scalar"], t_ge)
                V.op("tensor_reduce", out=g_m1[:], in_=gtok[:], axis=AX.X, op=ALU.max)
                V.op("tensor_scalar", out=g_nm1[:], in0=g_m1[:], scalar1=-1.0,
                     scalar2=0.0, op0=ALU.mult, op1=BYP)
                V.op("tensor_scalar", out=g_ind[:], in0=gtok[:], scalar1=g_m1[:],
                     scalar2=0.0, op0=ALU.is_ge, op1=BYP)
                V.op("scalar_tensor_tensor", out=g_t4[:], in0=g_ind[:], scalar=-1e30,
                     in1=gtok[:], op0=ALU.mult, op1=ALU.add)
                V.op("tensor_reduce", out=g_m2[:], in_=g_t4[:], axis=AX.X, op=ALU.max)
                V.op("tensor_scalar", out=g_ind[:], in0=gtok[:], scalar1=g_m2[:],
                     scalar2=0.0, op0=ALU.is_ge, op1=BYP)
                S.wait(P.sem["vector"], V.tok())
                S.op("activation", out=g_e[:], in_=gtok[:], func=AF.Exp, bias=g_nm1[:])
                V.wait(P.sem["scalar"], S.tok())
                V.op("tensor_tensor", out=g_raw[:], in0=g_e[:], in1=g_ind[:],
                     op=ALU.mult)
                V.op("tensor_reduce", out=g_s[:], in_=g_raw[:], axis=AX.X, op=ALU.add)
                V.op("reciprocal", out=g_sr[:], in_=g_s[:])
                V.op("tensor_scalar", out=g_raw[:], in0=g_raw[:], scalar1=g_sr[:],
                     scalar2=0.0, op0=ALU.mult, op1=BYP)
                V.op("tensor_copy", out=g_tokb[:], in_=g_raw[:])
                t_gmath = V.tok()
                T.wait(P.sem["vector"], t_gmath)
                T.wait(P.sem["scalar"], S.tok())
                T.raw("transpose", p_gt8, g_tokb[:], identb[:])
                T.mark()
                S.wait(P.sem["tensor"], T.tok())
                S.op("copy",
                     out=g_fm1[:, lq * LQ + ltt * 128:lq * LQ + (ltt + 1) * 128],
                     in_=p_gt8)
            for erc in range(4):
                wpp = erc % 2
                G.wait(P.sem["vector"], max(t_cast[0], t_cast[1]))
                G.dma(wtmp[:, 0:4096].rearrange("p (c k) -> p c k", k=128),
                      w_A1[:, erc * 128:(erc + 1) * 128]
                      .rearrange("(c p) k -> p c k", p=128), wlg_s, wlg_c)
                V.wait(wlg_s, wlg_c["n"])
                V.wait(P.sem["tensor"], T.tok())
                V.op("tensor_copy", out=wslab[wpp][:], in_=wtmp[:, 0:4096])
                t_cast[wpp] = V.tok()
                for lt in range(LQ // LC):
                    pp = lt % 2
                    gcs = lq * LQ + lt * LC
                    T.wait(P.sem["vector"], t_cast[wpp])
                    T.wait(P.sem["vector"], V.tok())
                    for kc in range(NDF):
                        T.mm(p_qkv[pp][:], wslab[wpp][:, kc * 128:(kc + 1) * 128],
                             xnh[:, kc * LQ + lt * LC:kc * LQ + (lt + 1) * LC],
                             start=(kc == 0), stop=(kc == NDF - 1))
                    T.mark()
                    t_mm = T.tok()
                    T.wait(P.sem["scalar"], S.tok())
                    T.mm(p_sw[:], sel1_s[:, erc * 128:(erc + 1) * 128],
                         g_fm1[:, gcs:gcs + LC], start=True, stop=True)
                    T.mark()
                    t_gbc = T.tok()
                    S.wait(P.sem["tensor"], t_gbc)
                    S.op("copy", out=gbc_t[:], in_=p_sw[:])
                    t_gbce = S.tok()
                    V.wait(P.sem["tensor"], t_mm)
                    V.wait(P.sem["scalar"], t_gbce)
                    V.wait(st_s, t_middma)
                    V.op("tensor_tensor", out=mid1_t[:], in0=p_qkv[pp][:],
                         in1=gbc_t[:], op=ALU.mult)
                    t_mid = V.tok()
                    Y.wait(P.sem["vector"], t_mid)
                    t_middma = Y.dma(mid_sc[erc * 128:(erc + 1) * 128, gcs:gcs + LC],
                                     mid1_t[:], st_s, st_c)

        t_oh_dma = 0
        t_oh_prev = 0
        t_lrow_prev = 0
        for h in range(32):
            Y.wait(sp_s, sp_c["n"])
            Y.wait(P.sem["tensor"], T.tok())
            Y.wait(P.sem["scalar"], S.tok())
            Y.wait(P.sem["vector"], V.tok())
            Y.dma(Kh[:], k_sp[h * 128:(h + 1) * 128, :], ld_s, ld_c)
            Y.dma(Qh[:], q_sp[h * 128:(h + 1) * 128, :], ld_s, ld_c)
            Y.dma(Vsb[:].rearrange("p (c k) -> p c k", k=128),
                  v_sp[:, h * 128:(h + 1) * 128].rearrange("(c p) k -> p c k", p=128),
                  ld_s, ld_c)
            ld_h = ld_c["n"]
            T.wait(ld_s, ld_h)
            for qb in range(NLC):
                nkt = 4 * (qb + 1)
                t_exp = [0, 0]
                t_avm = [0, 0]
                t_msk = [0, 0]
                for kt in range(nkt + 1):
                    if kt < nkt:
                        pp = kt % 2
                        T.wait(P.sem["scalar"], t_exp[pp])
                        T.mm(p_s[:] if pp == 0 else p_s2nd[:],
                             Kh[:, kt * 128:(kt + 1) * 128],
                             Qh[:, qb * LC:(qb + 1) * LC], start=True, stop=True)
                        T.mark()
                        t_sc = T.tok()
                    if kt >= 1:
                        km = kt - 1
                        pk = km % 2
                        if km >= nkt - 4:
                            T.wait(P.sem["vector"], t_msk[pk])
                        else:
                            T.wait(P.sem["scalar"], t_exp[pk])
                        if km == 0:
                            T.wait(P.sem["vector"], t_oh_prev)
                            T.wait(P.sem["scalar"], t_lrow_prev)
                        T.mm(p_o[:], Vsb[:, km * 128:(km + 1) * 128], PT[pk][:],
                             start=(km == 0), stop=(km == nkt - 1))
                        T.mm(p_l[0:1, :], onesc[:], PT[pk][:], start=(km == 0),
                             stop=(km == nkt - 1))
                        T.mark()
                        t_avm[pk] = T.tok()
                    if kt < nkt:
                        S.wait(P.sem["tensor"], t_sc)
                        S.wait(P.sem["tensor"], t_avm[pp])
                        if kt >= nkt - 4:
                            S.wait(P.sem["vector"], t_msk[pp])
                        S.op("activation", out=PT[pp][:],
                             in_=(p_s[:] if pp == 0 else p_s2nd[:]), func=AF.Exp,
                             scale=0.08838834764831845)
                        t_exp[pp] = S.tok()
                        if kt >= nkt - 4:
                            mj = kt - (nkt - 4)
                            V.wait(P.sem["scalar"], t_exp[pp])
                            V.op("tensor_tensor", out=PT[pp][:], in0=PT[pp][:],
                                 in1=mask_s[:, mj * 512:(mj + 1) * 512], op=ALU.mult)
                            t_msk[pp] = V.tok()
                t_av = T.tok()
                S.wait(P.sem["tensor"], t_av)
                S.op("copy", out=lrow[:], in_=p_l[0:1, :])
                t_lrow = S.tok()
                t_lrow_prev = t_lrow
                V.wait(P.sem["scalar"], t_lrow)
                V.op("reciprocal", out=lrec[:], in_=lrow[:])
                V.op("tensor_copy", out=lrec_b[:], in_=lrec[:])
                t_lr = V.tok()
                T.wait(P.sem["vector"], t_lr)
                T.wait(P.sem["scalar"], S.tok())
                T.mm(p_bc[:], onesr[:], lrec_b[:], start=True, stop=True)
                T.mark()
                t_lb = T.tok()
                S.wait(P.sem["tensor"], t_lb)
                S.op("copy", out=lbc[:], in_=p_bc[:])
                t_lbc = S.tok()
                V.wait(P.sem["scalar"], t_lbc)
                V.wait(st_s, t_oh_dma)
                V.op("tensor_tensor", out=oh_t[:], in0=p_o[:], in1=lbc[:], op=ALU.mult)
                t_oh = V.tok()
                t_oh_prev = t_oh
                Y.wait(P.sem["vector"], t_oh)
                t_oh_dma = Y.dma(oh_sc[h * 128:(h + 1) * 128, qb * LC:(qb + 1) * LC],
                                 oh_t[:], st_s, st_c)

        G.wait(P.sem["vector"], V.tok())
        G.wait(P.sem["tensor"], T.tok())
        G.wait(P.sem["scalar"], S.tok())
        for lq in range(4):
            hs = slice(lq * LQ, (lq + 1) * LQ)
            Y.wait(st_s, st_c["n"])
            Y.wait(P.sem["tensor"], T.tok())
            Y.wait(P.sem["scalar"], S.tok())
            Y.wait(P.sem["vector"], V.tok())
            for j in range(NDF):
                Y.dma(oh_res[:, j * LQ:(j + 1) * LQ],
                      oh_sc[j * 128:(j + 1) * 128, hs], ld_s, ld_c)
            for j in range(4):
                Y.dma(mid_res[:, j * LQ:(j + 1) * LQ],
                      mid_sc[j * 128:(j + 1) * 128, hs], ld_s, ld_c)
            ld_all = ld_c["n"]
            T.wait(ld_s, ld_all)
            t_x2 = [0, 0]
            st_t2 = [0, 0]
            for m in range(16):
                wpp = m % 2
                G.wait(P.sem["vector"], max(t_cast[0], t_cast[1]))
                G.dma(wtmp[:, 0:4096].rearrange("p (c k) -> p c k", k=128),
                      w_o[:, m * 128:(m + 1) * 128]
                      .rearrange("(c p) k -> p c k", p=128), wlg_s, wlg_c)
                G.dma(rows_b2[:].rearrange("p (c k) -> p c k", k=128),
                      w_B1[:, m * 128:(m + 1) * 128]
                      .rearrange("(c p) k -> p c k", p=128), wlg_s, wlg_c)
                V.wait(wlg_s, wlg_c["n"])
                V.wait(P.sem["tensor"], T.tok())
                V.op("tensor_copy", out=wslab[wpp][:], in_=wtmp[:, 0:4096])
                V.op("tensor_copy", out=qk_b[:], in_=rows_b2[:])
                t_cast[wpp] = V.tok()
                x1buf = [xr_t[0], x1t_[0]]
                for lt in range(LQ // LC):
                    pp = lt % 2
                    gcs = lq * LQ + lt * LC
                    Y.wait(P.sem["vector"], t_x2[pp])
                    Y.dma(x1buf[pp][:], x1_sc[m * 128:(m + 1) * 128, gcs:gcs + LC],
                          ld_s, ld_c)
                    ld_x1 = ld_c["n"]
                    T.wait(P.sem["vector"], t_cast[wpp])
                    T.wait(P.sem["vector"], t_x2[pp])
                    for kc in range(NDF):
                        T.mm(p_y[pp][:], wslab[wpp][:, kc * 128:(kc + 1) * 128],
                             oh_res[:, kc * LQ + lt * LC:kc * LQ + (lt + 1) * LC],
                             start=(kc == 0), stop=False)
                    for ec in range(4):
                        T.mm(p_y[pp][:], qk_b[:, ec * 128:(ec + 1) * 128],
                             mid_res[:, ec * LQ + lt * LC:ec * LQ + (lt + 1) * LC],
                             start=False, stop=(ec == 3))
                    T.mark()
                    t_mm = T.tok()
                    V.wait(P.sem["tensor"], t_mm)
                    V.wait(ld_s, ld_x1)
                    V.wait(st_s, st_t2[pp])
                    V.op("tensor_tensor", out=x2t[pp][:], in0=p_y[pp][:],
                         in1=x1buf[pp][:], op=ALU.add)
                    t_x2[pp] = V.tok()
                    Y.wait(P.sem["vector"], t_x2[pp])
                    st_t2[pp] = Y.dma(x2_out[m * 128:(m + 1) * 128, gcs:gcs + LC],
                                      x2t[pp][:], st_s, st_c)
        Y.wait(st_s, st_c["n"])
    with nc.Block() as blk:
        P.finalize(blk)
    return nc


def _prep_fused_inputs(inputs):
    x = inputs["L0_attn_x"]
    xpw = inputs["L0_attn_x_proj_w"]
    dtw = inputs["L0_attn_dt_proj_w"]
    dtb = inputs["L0_attn_dt_proj_b"]
    Alog = inputs["L0_attn_A_log"]
    Dv = inputs["L0_attn_D"]
    rw = inputs["L0_ffn_router_w"]
    A0 = inputs["L0_ffn_A"]
    B0 = inputs["L0_ffn_B"]
    wq = inputs["L1_attn_wq"]
    wk = inputs["L1_attn_wk"]
    wv = inputs["L1_attn_wv"]
    wo = inputs["L1_attn_wo"]
    gw = inputs["L1_ffn_gate_w"]
    lA = inputs["L1_ffn_lora_A"]
    lB = inputs["L1_ffn_lora_B"]

    w_dtin = np.ascontiguousarray(xpw[:128].T)
    w_bcr = np.zeros((DM, 68), np.float32)
    w_bcr[:, 0:16] = xpw[128:144].T
    w_bcr[:, 32:48] = xpw[144:160].T
    w_bcr[:, 64:68] = rw.T
    w_mid = np.ascontiguousarray(A0.reshape(64, DM).T)
    identv = np.eye(128, dtype=np.float32)
    sel0v = np.zeros((4, 64), np.float32)
    for e in range(4):
        sel0v[e, e * 16:(e + 1) * 16] = 1.0
    B0T = np.ascontiguousarray(B0.transpose(0, 2, 1).reshape(64, DM))
    t = np.arange(L, dtype=np.float64)
    cosr = np.cos(t).astype(bf16).astype(np.float32)
    sinr = np.sin(t).astype(bf16).astype(np.float32)
    cosbv = np.ascontiguousarray(np.broadcast_to(cosr, (128, L))).astype(bf16)
    sinbv = np.ascontiguousarray(np.broadcast_to(sinr, (128, L))).astype(bf16)
    mk = np.zeros((128, 4, 512), np.float32)
    for mj in range(4):
        for k in range(128):
            mk[k, mj, mj * 128 + k:] = 1.0
    masksv = mk.reshape(128, 2048).astype(bf16)
    psw = np.zeros((128, 128), np.float32)
    for i in range(64):
        psw[2 * i + 1, 2 * i] = -1.0
        psw[2 * i, 2 * i + 1] = 1.0
    psw = psw.astype(bf16)
    identbv = np.eye(128, dtype=np.float32).astype(bf16)
    sel1v = np.zeros((8, 512), np.float32)
    for e in range(8):
        sel1v[e, e * 64:(e + 1) * 64] = 1.0
    sel1v = sel1v.astype(bf16)
    wA1 = np.ascontiguousarray(lA.reshape(512, DM).T)
    woT = wo.T
    wB1 = lB.transpose(0, 2, 1).reshape(512, DM)

    maps = []
    for c in range(8):
        b, hp = c // 2, c % 2
        perm = np.concatenate([np.arange(hp * DH, (hp + 1) * DH),
                               np.arange((1 - hp) * DH, (2 - hp) * DH)])
        dperm = perm
        m = {
            "x_fm": np.ascontiguousarray(x[b].T[dperm]),
            "w_dtin": np.ascontiguousarray(w_dtin[dperm]),
            "w_bcr": np.ascontiguousarray(w_bcr[dperm]),
            "w_mid": np.ascontiguousarray(w_mid[dperm]),
            "w_dt": np.ascontiguousarray(dtw.T[:, dperm]),
            "dt_bias": np.ascontiguousarray(dtb[dperm].reshape(NDF, 128).T),
            "A_log_t": np.ascontiguousarray(
                Alog[dperm].reshape(NDF, 128, 16).transpose(1, 0, 2)
                .reshape(128, NDF * 16)),
            "D_t": np.ascontiguousarray(Dv[dperm].reshape(NDF, 128).T),
            "w_B0": np.ascontiguousarray(B0T[:, dperm]),
            "sel0": sel0v,
            "ident": identv,
            "w_q": np.ascontiguousarray(wq.T[dperm]),
            "w_k": np.ascontiguousarray(wk.T[dperm]),
            "w_v": np.ascontiguousarray(wv.T[dperm]),
            "w_g": np.ascontiguousarray(gw.T[dperm]),
            "w_A1": np.ascontiguousarray(wA1[dperm]),
            "w_o": np.ascontiguousarray(woT[:, dperm[0:DH]]),
            "w_B1": np.ascontiguousarray(wB1[:, dperm[0:DH]]),
            "cosb": cosbv,
            "sinb": sinbv,
            "masks": masksv,
            "pswap": psw,
            "identbi": identbv,
            "sel1": sel1v,
        }
        maps.append(m)
    return maps


def kernel(**inputs):
    maps = _prep_fused_inputs(inputs)
    nc = build_fused()
    res = run_bass_kernel_spmd(nc, maps, core_ids=list(range(8)))
    out = np.empty((NB, L, DM), np.float32)
    for c in range(8):
        b, hp = c // 2, c % 2
        x2 = np.asarray(res.results[c]["x2_out"])
        out[b, :, hp * DH:(hp + 1) * DH] = x2.T
    return out


# version 05
# speedup vs baseline: 1.8023x; 1.8023x over previous
import numpy as np
import ml_dtypes
import concourse.bass as bass
import concourse.mybir as mybir
from concourse.bass_utils import run_bass_kernel_spmd

F32 = mybir.dt.float32
BF16 = mybir.dt.bfloat16
AF = mybir.ActivationFunctionType
ALU = mybir.AluOpType
AX = mybir.AxisListType
bf16 = ml_dtypes.bfloat16

DM = 4096
NB = 4
L = 4096
DH = 2048
LC = 512
NLC = L // LC
NDF = DM // 128
NDH = DH // 128
NT = NDH
LQ = 1024
PAIRS = [[0, 1], [2, 3], [4, 5], [6, 7]]

BYP = ALU.bypass


class SEng:

    def __init__(self, sem, cnt):
        self.s = sem
        self.c = cnt
        self.q = []
        self.cell = {}

    def op(self, name, *a, **k):
        n0 = self.c["n"]
        s_ = self.s

        def run(e, n0=n0, name=name, a=a, k=k):
            if n0 > 0:
                e.wait_ge(s_, n0)
            getattr(e, name)(*a, **k).then_inc(s_, 1)
        self.q.append(run)
        self.c["n"] += 1
        return self.c["n"]

    def mm(self, out, lhsT, rhs, start, stop, **k):
        cell = self.cell

        def run(e, out=out, lhsT=lhsT, rhs=rhs, start=start, stop=stop, k=k):
            cell["last"] = e.matmul(out, lhsT, rhs, start=start, stop=stop, **k)
        self.q.append(run)

    def raw(self, name, *a, **k):
        cell = self.cell

        def run(e, name=name, a=a, k=k):
            cell["last"] = getattr(e, name)(*a, **k)
        self.q.append(run)

    def mark(self):
        cell = self.cell
        s_ = self.s

        def run(e):
            cell["last"].then_inc(s_, 1)
        self.q.append(run)
        self.c["n"] += 1
        return self.c["n"]

    def wait(self, sem, v):
        if v > 0:
            self.q.append(lambda e, sem=sem, v=v: e.wait_ge(sem, v))

    def dma(self, out, in_, sem, cnt, **k):
        def run(e, out=out, in_=in_, sem=sem, k=k):
            e.dma_start(out=out, in_=in_, **k).then_inc(sem, 16)
        self.q.append(run)
        cnt["n"] += 16
        return cnt["n"]

    def cc(self, kind, op, in_t, out_t, sem, cnt):
        def run(e, kind=kind, op=op, in_t=in_t, out_t=out_t, sem=sem):
            e.collective_compute(
                kind, op, replica_groups=PAIRS,
                ins=[in_t.ap().opt()], outs=[out_t.ap().opt()]).then_inc(sem, 1)
        self.q.append(run)
        cnt["n"] += 1
        return cnt["n"]

    def tok(self):
        return self.c["n"]


class Prog:
    ENGS = ("vector", "scalar", "tensor", "gpsimd", "sync")

    def __init__(self, nc):
        self.nc = nc
        self.sem = {e: nc.alloc_semaphore(f"es_{e}") for e in self.ENGS}
        self.cnt = {e: {"n": 0} for e in self.ENGS}
        self.dsem = {}
        self.dcnt = {}
        self.eng = {e: SEng(self.sem[e], self.cnt[e]) for e in self.ENGS}

    def dma_sem(self, name):
        if name not in self.dsem:
            self.dsem[name] = self.nc.alloc_semaphore(f"ds_{name}")
            self.dcnt[name] = {"n": 0}
        return self.dsem[name], self.dcnt[name]

    def finalize(self, blk):
        for name in self.ENGS:
            se = self.eng[name]

            def deco(raw, se=se):
                for fn in se.q:
                    fn(raw)
            getattr(blk, name)(deco)


def build_fused():
    nc = bass.Bass("TRN2", target_bir_lowering=False)

    x_fm = nc.dram_tensor("x_fm", [DH, L], F32, kind="ExternalInput")
    w_dtin = nc.dram_tensor("w_dtin", [DH, 128], F32, kind="ExternalInput")
    w_bcr = nc.dram_tensor("w_bcr", [DH, 68], F32, kind="ExternalInput")
    w_mid = nc.dram_tensor("w_mid", [DH, 64], F32, kind="ExternalInput")
    w_dt = nc.dram_tensor("w_dt", [128, DH], F32, kind="ExternalInput")
    dt_bias = nc.dram_tensor("dt_bias", [128, NT], F32, kind="ExternalInput")
    A_log_t = nc.dram_tensor("A_log_t", [128, NT * 16], F32, kind="ExternalInput")
    D_t = nc.dram_tensor("D_t", [128, NT], F32, kind="ExternalInput")
    w_B0 = nc.dram_tensor("w_B0", [64, DH], F32, kind="ExternalInput")
    sel0 = nc.dram_tensor("sel0", [4, 64], F32, kind="ExternalInput")
    w_q = nc.dram_tensor("w_q", [DM, DH], F32, kind="ExternalInput")
    w_k = nc.dram_tensor("w_k", [DM, DH], F32, kind="ExternalInput")
    w_v = nc.dram_tensor("w_v", [DM, DH], F32, kind="ExternalInput")
    w_g = nc.dram_tensor("w_g", [DM, 8], F32, kind="ExternalInput")
    w_A1 = nc.dram_tensor("w_A1", [DM, 256], F32, kind="ExternalInput")
    w_o = nc.dram_tensor("w_o", [DM, DH], F32, kind="ExternalInput")
    w_B1 = nc.dram_tensor("w_B1", [512, DH], F32, kind="ExternalInput")
    cosb = nc.dram_tensor("cosb", [128, L], BF16, kind="ExternalInput")
    sinb = nc.dram_tensor("sinb", [128, L], BF16, kind="ExternalInput")
    masks = nc.dram_tensor("masks", [128, 4 * 512], BF16, kind="ExternalInput")
    pswap = nc.dram_tensor("pswap", [128, 128], BF16, kind="ExternalInput")
    identbi = nc.dram_tensor("identbi", [128, 128], BF16, kind="ExternalInput")
    sel1 = nc.dram_tensor("sel1", [8, 256], BF16, kind="ExternalInput")
    x2_out = nc.dram_tensor("x2_out", [DH, L], F32, kind="ExternalOutput")

    st0_p = nc.dram_tensor("st0_p", [2, L], F32, kind="Internal")
    st0_f = nc.dram_tensor("st0_f", [2, L], F32, kind="Internal")
    st1_p = nc.dram_tensor("st1_p", [2, L], F32, kind="Internal")
    st1_f = nc.dram_tensor("st1_f", [2, L], F32, kind="Internal")
    proj_p = nc.dram_tensor("proj_p", [260, L], F32, kind="Internal")
    proj_f = nc.dram_tensor("proj_f", [260, L], F32, kind="Internal")
    xn_sc = nc.dram_tensor("xn_sc", [DH, L], BF16, kind="Internal")
    Bb_sc = nc.dram_tensor("Bb_sc", [16, L], BF16, kind="Internal")
    Cb_sc = nc.dram_tensor("Cb_sc", [16, L], BF16, kind="Internal")
    x1_sc = nc.dram_tensor("x1_sc", [DH, L], F32, kind="Internal")
    xn1h = nc.dram_tensor("xn1h", [DH, L], BF16, kind="Internal")
    xn1_f = nc.dram_tensor("xn1_f", [DM, L], BF16, kind="Internal")
    q_sp = nc.dram_tensor("q_sp", [DH, L], BF16, kind="Internal")
    k_sp = nc.dram_tensor("k_sp", [DH, L], BF16, kind="Internal")
    v_sp = nc.dram_tensor("v_sp", [L, DH], BF16, kind="Internal")
    ohh = nc.dram_tensor("ohh", [DH, L], BF16, kind="Internal")
    oh_f = nc.dram_tensor("oh_f", [DM, L], BF16, kind="Internal")
    midh = nc.dram_tensor("midh", [256, L], BF16, kind="Internal")
    mid_f = nc.dram_tensor("mid_f", [512, L], BF16, kind="Internal")

    sb = nc.alloc_sbuf_tensor
    MEGA = sb("MEGA", [128, 16384], F32)
    xnb_v = MEGA[:, 0:4096].bitcast(BF16)
    xsq_v = MEGA[:, 8192:12288].bitcast(BF16)
    stgA = MEGA[:, 12288:16384]
    stgB = MEGA[0:64, 4096:8192]
    s1_sb = MEGA[0:1, 4096:8192]
    s2_sb = MEGA[32:33, 4096:8192]
    g_fm0 = MEGA[0:4, 0:2048].bitcast(BF16)
    gbc0 = MEGA[64:128, 0:2048].bitcast(BF16)
    wB0stg = MEGA[0:64, 4096:6144]
    dA_pp = [MEGA[:, 0:4096], MEGA[:, 4096:8192]]
    u_v = MEGA[:, 8192:10240].bitcast(BF16)
    h_v = MEGA[:, 10240:12288].bitcast(BF16)
    hc_v = u_v
    BrepF_v = MEGA[:, 12288:16384].bitcast(BF16)
    xnh = MEGA[:].bitcast(BF16)
    Kh = MEGA[:, 0:2048].bitcast(BF16)
    Qh = MEGA[:, 2048:4096].bitcast(BF16)
    Vsb = MEGA[:, 4096:6144].bitcast(BF16)
    PT = [MEGA[:, 6144 + i * 256:6400 + i * 256].bitcast(BF16) for i in range(2)]
    lbc = MEGA[:, 6656:6912].bitcast(BF16)
    oh_t = MEGA[:, 6912:7168].bitcast(BF16)
    oh_res = MEGA[:].bitcast(BF16)

    wtmp = sb("wtmp", [128, 4096], F32)
    Bfb = wtmp[0:16, 0:2048].bitcast(BF16)
    Cfb = wtmp[32:48, 0:2048].bitcast(BF16)
    mid0b = wtmp[64:128, 0:2048].bitcast(BF16)
    lgf = wtmp[0:4, 2048:4096].bitcast(BF16)
    wB0_b = wtmp[64:128, 2048:3072].bitcast(BF16)
    wdtin_b = sb("wdtin_b", [128, NDH * 128], BF16)
    wbcr_b = sb("wbcr_b", [128, NDH * 68], BF16)
    wmid_b = sb("wmid_b", [128, NDH * 64], BF16)
    mid_res = sb("mid_res", [128, 8192], BF16)
    wdt_b = mid_res[:, 0:2048]
    dtin_f = mid_res[:, 4096:8192]
    wslab = [sb(f"wslab{i}", [128, 4096], BF16) for i in range(2)]
    sel0_b = sb("sel0_b", [4, 64], BF16)
    identb = sb("identb", [128, 128], BF16)
    dtb_t = sb("dtb_t", [128, NT], F32)
    Alog_s = sb("Alog_s", [128, NT * 16], F32)
    negA = sb("negA", [128, NT * 16], F32)
    Dcol = sb("Dcol", [128, NT], F32)
    gT_b = sb("gT_b", [128, 8 * NDF], BF16)
    g_fm1 = sb("g_fm1", [8, L], BF16)
    cos_s = sb("cos_s", [128, L], BF16)
    sin_s = sb("sin_s", [128, L], BF16)
    mask_s = sb("mask_s", [128, 4 * 512], BF16)
    pswap_s = sb("pswap_s", [128, 128], BF16)
    sel1_s = sb("sel1_s", [8, 256], BF16)
    pr_dt = sb("pr_dt", [128, LC], F32)
    pr_bc = sb("pr_bc", [68, LC], F32)
    pr_md = sb("pr_md", [64, LC], F32)

    onesc = sb("onesc", [128, 1], BF16)
    onesf = sb("onesf", [128, 1], F32)
    onesr = sb("onesr", [1, 128], BF16)
    ones1 = sb("ones1", [128, 1], F32)
    epsc = sb("epsc", [1, 1], F32)
    rows_a = sb("rows_a", [128, LC], F32)
    rows_b2 = sb("rows_b2", [128, LC], F32)
    rows_c = sb("rows_c", [1, LC], F32)
    rows_d = sb("rows_d", [1, LC], F32)
    mrow = rows_c[0:1, :]
    qrow = rows_d[0:1, :]
    m2row = rows_a[0:1, :]
    vrow = rows_b2[0:1, :]
    sdrow = rows_d[0:1, :]
    rrow = rows_a[0:1, :]
    mrow_b = sb("mrow_b", [1, LC], BF16)
    rrow_b = sb("rrow_b", [1, LC], BF16)
    m_bc = sb("m_bc", [128, LC], F32)
    r_bc = sb("r_bc", [128, LC], F32)
    xn_t = rows_a
    abs_t = sb("abs_t", [128, LC], F32)
    exp_t = rows_a
    relu_t = rows_b2
    r8_t = wdtin_b
    ysc_t = m_bc
    ym_t = r_bc
    yh1 = sb("yh1", [128, LC], F32)
    dt_f = [sb(f"dt_f{i}", [128, LC], F32) for i in range(2)]
    dtx_b = [sb(f"dtx_b{i}", [128, LC], BF16) for i in range(2)]
    xr_t = [sb("xr_t0", [128, LC], F32)]
    x1t_ = [sb("x1t_0", [128, LC], F32)]
    xnA = [sb(f"xnA{i}", [128, LC], BF16) for i in range(2)]
    state = sb("state", [128, NT * 16], F32)
    lgT = sb("lgT", [128, 4], F32)
    g_m1 = sb("g_m1", [128, 1], F32)
    g_nm1 = sb("g_nm1", [128, 1], F32)
    g_t4 = sb("g_t4", [128, 8], F32)
    g_m2 = sb("g_m2", [128, 1], F32)
    g_e = sb("g_e", [128, 8], F32)
    g_ind = sb("g_ind", [128, 8], F32)
    g_raw = sb("g_raw", [128, 8], F32)
    g_s = sb("g_s", [128, 1], F32)
    g_sr = sb("g_sr", [128, 1], F32)
    g_tok4 = sb("g_tok4", [128, 4], BF16)
    gtok = sb("gtok", [128, 8], F32)
    g_tokb = sb("g_tokb", [128, 8], BF16)
    gbc_t = sb("gbc_t", [128, LC], BF16)
    qk_b = sb("qk_b", [128, LC], BF16)
    qk_b2 = [qk_b, sb("qk_b_1", [128, LC], BF16)]
    rot1 = [sb(f"rot1_{i}", [128, LC], BF16) for i in range(2)]
    rot2 = sb("rot2", [128, LC], BF16)
    vout_b = [sb(f"vout_b{i}", [128, 128], BF16) for i in range(2)]
    mid1_t = sb("mid1_t", [128, LC], BF16)
    lrow = rows_c[0:1, :]
    lrec = rows_d[0:1, :]
    lrec_b = sb("lrec_b", [1, LC], BF16)
    x2t = [sb("x2t0", [128, LC], F32), yh1]

    ps = nc.alloc_psum_tensor
    p_stats = ps("p_stats", [128, LC], F32)
    p_s1 = p_stats[0:1, :]
    p_s2 = p_stats[32:33, :]
    p_bc = ps("p_bc", [128, LC], F32)
    p_dtin = ps("p_dtin", [128, LC], F32)
    p_bcrF = ps("p_bcrF", [128, LC], F32)
    p_bcr = p_bcrF[0:68, :]
    p_midF = ps("p_midF", [128, LC], F32)
    p_mid = p_midF[0:64, :]
    p_dtr = [p_bcrF, p_midF]
    p_d0 = p_dtin
    p_g = p_bc[:, 0:2].bitcast(BF16)
    p_gb = p_bc[0:4, 64:128].bitcast(BF16)
    p_qkv = [p_bcrF, p_midF]
    p_sw = ps("p_sw", [128, LC], F32)
    p_o = ps("p_o", [128, LC], F32)
    p_l = ps("p_l", [128, LC], F32)
    p_g8 = p_stats[:, 0:8]
    p_gt8 = p_bc[0:8, 0:64].bitcast(BF16)
    p_s = p_sw
    p_s2nd = p_dtin
    p_y = [p_bcrF, p_midF]

    P = Prog(nc)
    V, S, T, G, Y = (P.eng["vector"], P.eng["scalar"], P.eng["tensor"],
                     P.eng["gpsimd"], P.eng["sync"])
    if True:
        ld_s, ld_c = P.dma_sem("ld")
        wl_s, wl_c = P.dma_sem("wl")
        wlg_s, wlg_c = P.dma_sem("wlg")
        st_s, st_c = P.dma_sem("st")
        br_s, br_c = P.dma_sem("br")
        xn_s, xn_c = P.dma_sem("xn")
        sp_s, sp_c = P.dma_sem("sp")
        cc_s, cc_c = P.dma_sem("cc")

        Y.dma(dtb_t[:], dt_bias[:], wl_s, wl_c)
        Y.dma(Alog_s[:], A_log_t[:], wl_s, wl_c)
        Y.dma(Dcol[:], D_t[:], wl_s, wl_c)
        Y.dma(m_bc[0:4, 0:64], sel0[:], wl_s, wl_c)
        Y.dma(cos_s[:], cosb[:], wl_s, wl_c)
        Y.dma(sin_s[:], sinb[:], wl_s, wl_c)
        Y.dma(mask_s[:], masks[:], wl_s, wl_c)
        Y.dma(pswap_s[:], pswap[:], wl_s, wl_c)
        Y.dma(identb[:], identbi[:], wl_s, wl_c)
        Y.dma(sel1_s[:], sel1[:], wl_s, wl_c)
        G.dma(wtmp[:, 0:NDH * 128].rearrange("p (c k) -> p c k", k=128),
              w_dtin[:].rearrange("(c p) k -> p c k", p=128), wlg_s, wlg_c)
        V.wait(wl_s, wl_c["n"])
        V.wait(wlg_s, wlg_c["n"])
        V.op("memset", onesc[:], 1.0)
        V.op("memset", onesf[:], 1.0)
        V.op("memset", onesr[:], 1.0)
        V.op("memset", ones1[:], 1.0)
        V.op("memset", epsc[:], 1e-5)
        V.op("tensor_copy", out=sel0_b[:], in_=m_bc[0:4, 0:64])
        V.op("tensor_copy", out=wdtin_b[:], in_=wtmp[:, 0:NDH * 128])
        G.wait(P.sem["vector"], V.tok())
        G.dma(wtmp[:, 0:NDH * 68].rearrange("p (c k) -> p c k", k=68),
              w_bcr[:].rearrange("(c p) k -> p c k", p=128), wlg_s, wlg_c)
        V.wait(wlg_s, wlg_c["n"])
        V.op("tensor_copy", out=wbcr_b[:], in_=wtmp[:, 0:NDH * 68])
        G.wait(P.sem["vector"], V.tok())
        G.dma(wtmp[:, 0:NDH * 64].rearrange("p (c k) -> p c k", k=64),
              w_mid[:].rearrange("(c p) k -> p c k", p=128), wlg_s, wlg_c)
        V.wait(wlg_s, wlg_c["n"])
        V.op("tensor_copy", out=wmid_b[:], in_=wtmp[:, 0:NDH * 64])
        G.wait(P.sem["vector"], V.tok())
        G.dma(wtmp[:, 0:DH], w_dt[:], wlg_s, wlg_c)
        V.wait(wlg_s, wlg_c["n"])
        V.op("tensor_copy", out=wdt_b[:], in_=wtmp[:, 0:DH])
        G.wait(P.sem["vector"], V.tok())
        G.dma(wtmp[:, 0:8 * NDF].rearrange("p (c k) -> p c k", k=8),
              w_g[:].rearrange("(c p) k -> p c k", p=128), wlg_s, wlg_c)
        V.wait(wlg_s, wlg_c["n"])
        V.op("tensor_copy", out=gT_b[:], in_=wtmp[:, 0:8 * NDF])
        S.wait(wl_s, wl_c["n"])
        S.op("activation", out=negA[:], in_=Alog_s[:], func=AF.Exp)
        V.wait(P.sem["scalar"], S.tok())
        V.op("tensor_scalar", out=negA[:], in0=negA[:], scalar1=-1.0, scalar2=0.0,
             op0=ALU.mult, op1=BYP)

        def ln_stats(src, dst_p, ev):
            for lc in range(NLC):
                cs = slice(lc * LC, (lc + 1) * LC)
                T.wait(P.sem["scalar"], ev["srow"])
                for j in range(NDH):
                    xt = xr_t[0] if j % 2 == 0 else x1t_[0]
                    Y.wait(P.sem["tensor"], T.tok())
                    Y.wait(P.sem["scalar"], S.tok())
                    Y.dma(xt[:], src[j * 128:(j + 1) * 128, cs], ld_s, ld_c)
                    ldj = ld_c["n"]
                    T.wait(ld_s, ldj)
                    T.mm(p_s1, onesf[:], xt[:], start=(j == 0), stop=(j == NDH - 1))
                    if j == NDH - 1:
                        T.mark()
                    S.wait(ld_s, ldj)
                    S.op("activation", out=xsq_v[:, j * LC:(j + 1) * LC],
                         in_=xt[:], func=AF.Square)
                t_s1 = T.tok()
                t_sq = S.tok()
                T.wait(P.sem["scalar"], t_sq)
                for j in range(NDH):
                    T.mm(p_s2, onesc[:], xsq_v[:, j * LC:(j + 1) * LC],
                         start=(j == 0), stop=(j == NDH - 1))
                T.mark()
                t_s2 = T.tok()
                S.wait(P.sem["tensor"], t_s1)
                S.op("copy", out=s1_sb[0:1, cs], in_=p_s1)
                S.wait(P.sem["tensor"], t_s2)
                S.op("copy", out=s2_sb[0:1, cs], in_=p_s2)
                ev["srow"] = S.tok()
            Y.wait(P.sem["scalar"], S.tok())
            Y.dma(dst_p[0:1, :], s1_sb[:], st_s, st_c)
            Y.dma(dst_p[1:2, :], s2_sb[:], st_s, st_c)

        def ln_norm(src, dst_sc, ev, with_proj, evd):
            for lc in range(NLC):
                cs = slice(lc * LC, (lc + 1) * LC)
                V.wait(P.sem["scalar"], ev["rbc"])
                V.op("tensor_scalar", out=mrow[:], in0=s1_sb[0:1, cs],
                     scalar1=1.0 / DM, scalar2=0.0, op0=ALU.mult, op1=BYP)
                V.op("tensor_scalar", out=qrow[:], in0=s2_sb[0:1, cs],
                     scalar1=1.0 / DM, scalar2=0.0, op0=ALU.mult, op1=BYP)
                V.op("tensor_tensor", out=m2row[:], in0=mrow[:], in1=mrow[:],
                     op=ALU.mult)
                V.op("tensor_tensor", out=vrow[:], in0=qrow[:], in1=m2row[:],
                     op=ALU.subtract)
                S.wait(P.sem["vector"], V.tok())
                S.op("activation", out=sdrow[:], in_=vrow[:], func=AF.Sqrt,
                     bias=epsc[:])
                V.wait(P.sem["scalar"], S.tok())
                V.op("reciprocal", out=rrow[:], in_=sdrow[:])
                V.op("tensor_copy", out=mrow_b[:], in_=mrow[:])
                V.op("tensor_copy", out=rrow_b[:], in_=rrow[:])
                t_rowsb = V.tok()
                T.wait(P.sem["vector"], t_rowsb)
                T.wait(P.sem["scalar"], ev["rbc"])
                T.mm(p_bc[:], onesr[:], mrow_b[:], start=True, stop=True)
                T.mark()
                S.wait(P.sem["tensor"], T.tok())
                S.op("copy", out=m_bc[:], in_=p_bc[:])
                T.wait(P.sem["scalar"], S.tok())
                T.mm(p_bc[:], onesr[:], rrow_b[:], start=True, stop=True)
                T.mark()
                S.wait(P.sem["tensor"], T.tok())
                S.op("copy", out=r_bc[:], in_=p_bc[:])
                ev["rbc"] = S.tok()
                V.wait(P.sem["scalar"], ev["rbc"])
                for j in range(NDH):
                    xt = xr_t[0] if j % 2 == 0 else x1t_[0]
                    Y.wait(P.sem["vector"], V.tok())
                    Y.dma(xt[:], src[j * 128:(j + 1) * 128, cs], ld_s, ld_c)
                    ldj = ld_c["n"]
                    V.wait(ld_s, ldj)
                    V.op("tensor_tensor", out=xn_t[:], in0=xt[:], in1=m_bc[:],
                         op=ALU.subtract)
                    V.op("tensor_tensor", out=xnb_v[:, j * LC:(j + 1) * LC],
                         in0=xn_t[:], in1=r_bc[:], op=ALU.mult)
                t_norm = V.tok()
                Y.wait(P.sem["vector"], t_norm)
                for jj in range(NDH):
                    Y.dma(dst_sc[jj * 128:(jj + 1) * 128, cs],
                          xnb_v[:, jj * LC:(jj + 1) * LC], xn_s, xn_c)
                if with_proj:
                    T.wait(P.sem["vector"], t_norm)
                    T.wait(P.sem["scalar"], ev["dtin"])
                    for j in range(NDH):
                        T.mm(p_dtin[:], wdtin_b[:, j * 128:(j + 1) * 128],
                             xnb_v[:, j * LC:(j + 1) * LC],
                             start=(j == 0), stop=(j == NDH - 1))
                    T.mark()
                    t_g1 = T.tok()
                    T.wait(P.sem["scalar"], ev["bcr"])
                    for j in range(NDH):
                        T.mm(p_bcr, wbcr_b[:, j * 68:(j + 1) * 68],
                             xnb_v[:, j * LC:(j + 1) * LC],
                             start=(j == 0), stop=(j == NDH - 1))
                    T.mark()
                    t_g2 = T.tok()
                    T.wait(P.sem["scalar"], ev["mid"])
                    for j in range(NDH):
                        T.mm(p_mid, wmid_b[:, j * 64:(j + 1) * 64],
                             xnb_v[:, j * LC:(j + 1) * LC],
                             start=(j == 0), stop=(j == NDH - 1))
                    T.mark()
                    t_g3 = T.tok()
                    S.wait(P.sem["tensor"], t_g1)
                    S.wait(st_s, evd["dt"])
                    S.op("copy", out=pr_dt[:], in_=p_dtin[:])
                    ev["dtin"] = S.tok()
                    Y.wait(P.sem["scalar"], ev["dtin"])
                    evd["dt"] = Y.dma(proj_p[0:128, cs], pr_dt[:], st_s, st_c)
                    S.wait(P.sem["tensor"], t_g2)
                    S.wait(st_s, evd["bc"])
                    S.op("copy", out=pr_bc[:], in_=p_bcrF[0:68, :])
                    ev["bcr"] = S.tok()
                    Y.wait(P.sem["scalar"], ev["bcr"])
                    evd["bc"] = Y.dma(proj_p[128:196, cs], pr_bc[:], st_s, st_c)
                    S.wait(P.sem["tensor"], t_g3)
                    S.wait(st_s, evd["md"])
                    S.op("copy", out=pr_md[:], in_=p_midF[0:64, :])
                    ev["mid"] = S.tok()
                    Y.wait(P.sem["scalar"], ev["mid"])
                    evd["md"] = Y.dma(proj_p[196:260, cs], pr_md[:], st_s, st_c)
                else:
                    V.wait(xn_s, xn_c["n"])

        t_prep = V.tok()
        T.wait(P.sem["vector"], t_prep)
        S.wait(P.sem["vector"], t_prep)
        ev = {"dtin": 0, "bcr": 0, "mid": 0, "srow": 0, "rbc": 0}
        evd = {"dt": 0, "bc": 0, "md": 0}
        ln_stats(x_fm, st0_p, ev)
        G.wait(st_s, st_c["n"])
        cc1 = G.cc("AllReduce", ALU.add, st0_p, st0_f, cc_s, cc_c)
        Y.wait(cc_s, cc1)
        Y.dma(s1_sb[:], st0_f[0:1, :], ld_s, ld_c)
        Y.dma(s2_sb[:], st0_f[1:2, :], ld_s, ld_c)
        t_st0_rb = ld_c["n"]
        V.wait(ld_s, t_st0_rb)

        ln_norm(x_fm, xn_sc, ev, True, evd)
        G.wait(st_s, st_c["n"])
        cc2 = G.cc("AllReduce", ALU.add, proj_p, proj_f, cc_s, cc_c)

        Y.wait(cc_s, cc2)
        Y.wait(P.sem["tensor"], T.tok())
        Y.wait(xn_s, xn_c["n"])
        Y.dma(stgA[:, :], proj_f[0:128, :], ld_s, ld_c)
        ld_dt = ld_c["n"]
        V.wait(ld_s, ld_dt)
        V.op("tensor_copy", out=dtin_f[:], in_=stgA[:, :])
        t_dtin_cast = V.tok()
        Y.wait(P.sem["vector"], t_dtin_cast)
        Y.dma(stgA[0:16, :], proj_f[128:144, :], ld_s, ld_c)
        Y.dma(stgA[32:48, :], proj_f[160:176, :], ld_s, ld_c)
        Y.dma(stgA[64:68, :], proj_f[192:196, :], ld_s, ld_c)
        Y.dma(stgB[:, :], proj_f[196:260, :], ld_s, ld_c)
        ld_rest = ld_c["n"]
        V.wait(ld_s, ld_rest)
        V.op("tensor_copy", out=Bfb[:], in_=stgA[0:16, :])
        V.op("tensor_copy", out=Cfb[:], in_=stgA[32:48, :])
        V.op("tensor_copy", out=lgf[:], in_=stgA[64:68, :])
        V.op("tensor_copy", out=mid0b[:], in_=stgB[:, :])
        t_casts = V.tok()
        Y.wait(P.sem["vector"], t_casts)
        Y.dma(Bb_sc[:], Bfb[:], st_s, st_c)
        Y.dma(Cb_sc[:], Cfb[:], st_s, st_c)
        bc_st = st_c["n"]

        G.wait(P.sem["vector"], t_casts)
        G.dma(wB0stg, w_B0[:], wlg_s, wlg_c)
        V.wait(wlg_s, wlg_c["n"])
        V.op("tensor_copy", out=wB0_b[:], in_=wB0stg)
        t_lgT_evac = 0
        t_gmath = 0
        t_gfm_evac = 0
        for q in range(L // 128):
            T.wait(P.sem["scalar"], S.tok() if q == 0 else t_lgT_evac)
            T.wait(P.sem["vector"], t_gmath if q > 0 else V.tok())
            T.raw("transpose", p_g, lgf[:, q * 128:(q + 1) * 128], identb[0:4, 0:4])
            T.mark()
            S.wait(P.sem["tensor"], T.tok())
            S.op("copy", out=lgT[:], in_=p_g)
            t_lgT_evac = S.tok()
            V.wait(P.sem["scalar"], t_lgT_evac)
            V.op("tensor_reduce", out=g_m1[:], in_=lgT[:], axis=AX.X, op=ALU.max)
            V.op("tensor_scalar", out=g_nm1[:], in0=g_m1[:], scalar1=-1.0,
                 scalar2=0.0, op0=ALU.mult, op1=BYP)
            V.op("tensor_scalar", out=g_ind[:, 0:4], in0=lgT[:], scalar1=g_m1[:],
                 scalar2=0.0, op0=ALU.is_ge, op1=BYP)
            V.op("scalar_tensor_tensor", out=g_t4[:, 0:4], in0=g_ind[:, 0:4],
                 scalar=-1e30, in1=lgT[:], op0=ALU.mult, op1=ALU.add)
            V.op("tensor_reduce", out=g_m2[:], in_=g_t4[:, 0:4], axis=AX.X, op=ALU.max)
            V.op("tensor_scalar", out=g_ind[:, 0:4], in0=lgT[:], scalar1=g_m2[:],
                 scalar2=0.0, op0=ALU.is_ge, op1=BYP)
            S.wait(P.sem["vector"], V.tok())
            S.op("activation", out=g_e[:, 0:4], in_=lgT[:], func=AF.Exp, bias=g_nm1[:])
            V.wait(P.sem["scalar"], S.tok())
            V.op("tensor_tensor", out=g_raw[:, 0:4], in0=g_e[:, 0:4],
                 in1=g_ind[:, 0:4], op=ALU.mult)
            V.op("tensor_reduce", out=g_s[:], in_=g_raw[:, 0:4], axis=AX.X, op=ALU.add)
            V.op("reciprocal", out=g_sr[:], in_=g_s[:])
            V.op("tensor_scalar", out=g_raw[:, 0:4], in0=g_raw[:, 0:4],
                 scalar1=g_sr[:], scalar2=0.0, op0=ALU.mult, op1=BYP)
            V.op("tensor_copy", out=g_tok4[:], in_=g_raw[:, 0:4])
            t_gmath = V.tok()
            T.wait(P.sem["vector"], t_gmath)
            T.wait(P.sem["scalar"], t_gfm_evac)
            T.raw("transpose", p_gb, g_tok4[:], identb[:])
            T.mark()
            S.wait(P.sem["tensor"], T.tok())
            S.op("copy", out=g_fm0[:, q * 128:(q + 1) * 128], in_=p_gb)
            t_gfm_evac = S.tok()
        for lc in range(NLC):
            cs = slice(lc * LC, (lc + 1) * LC)
            T.wait(P.sem["scalar"], S.tok())
            T.mm(p_bcrF[0:64, :], sel0_b[:], g_fm0[:, cs], start=True, stop=True)
            T.mark()
            S.wait(P.sem["tensor"], T.tok())
            S.op("activation", out=gbc0[:, cs], in_=p_bcrF[0:64, :], func=AF.Copy,
                 scale=4.0)
        V.wait(P.sem["scalar"], S.tok())
        V.op("tensor_tensor", out=mid0b[:], in0=mid0b[:], in1=gbc0[:], op=ALU.mult)
        t_mid0p = V.tok()
        mid0p = mid0b

        V.op("memset", state[:], 0.0)
        st_x1 = 0
        t_x1 = 0
        t_scan_done = 0
        t_scan_pp = [0, 0]
        t_dtx = [0, 0]
        for lc in range(NLC):
            cs = slice(lc * LC, (lc + 1) * LC)
            G.wait(st_s, bc_st)
            G.wait(P.sem["vector"], max(t_scan_done, t_mid0p))
            G.wait(P.sem["tensor"], T.tok())
            G.dma(BrepF_v[:].rearrange("p (n l) -> p n l", n=16),
                  Bb_sc[:, cs].unsqueeze(0).broadcast_to((128, 16, LC)), br_s, br_c)
            G.dma(wslab[0][:].rearrange("p (n l) -> p n l", n=8),
                  Cb_sc[0:8, cs].unsqueeze(0).broadcast_to((128, 8, LC)), br_s, br_c)
            G.dma(wslab[1][:].rearrange("p (n l) -> p n l", n=8),
                  Cb_sc[8:16, cs].unsqueeze(0).broadcast_to((128, 8, LC)), br_s, br_c)
            br_now = br_c["n"]
            if lc == 0:
                Y.wait(xn_s, xn_c["n"])
            for jj in range(NT):
                pp = jj % 2
                Y.wait(P.sem["vector"], V.tok())
                Y.dma(xnA[pp][:], xn_sc[jj * 128:(jj + 1) * 128, cs], ld_s, ld_c)
                ld_xn = ld_c["n"]
                T.wait(P.sem["vector"], t_dtx[pp])
                T.wait(P.sem["scalar"], S.tok())
                T.mm(p_dtr[pp][:], wdt_b[:, jj * 128:(jj + 1) * 128],
                     dtin_f[:, cs], start=True, stop=True)
                T.mark()
                t_dtr = T.tok()
                S.wait(P.sem["tensor"], t_dtr)
                S.wait(P.sem["vector"], t_dtx[pp])
                S.op("activation", out=abs_t[:], in_=p_dtr[pp][:], func=AF.Abs,
                     bias=dtb_t[:, jj:jj + 1])
                S.op("activation", out=exp_t[:], in_=abs_t[:], func=AF.Exp, scale=-1.0)
                S.op("activation", out=abs_t[:], in_=exp_t[:], func=AF.Ln,
                     bias=ones1[:])
                S.op("activation", out=relu_t[:], in_=p_dtr[pp][:], func=AF.Relu,
                     bias=dtb_t[:, jj:jj + 1])
                t_sp = S.tok()
                V.wait(P.sem["scalar"], t_sp)
                V.op("tensor_tensor", out=dt_f[pp][:], in0=relu_t[:], in1=abs_t[:],
                     op=ALU.add)
                V.wait(ld_s, ld_xn)
                V.op("tensor_tensor", out=dtx_b[pp][:], in0=dt_f[pp][:],
                     in1=xnA[pp][:], op=ALU.mult)
                t_dtx[pp] = V.tok()
                V.wait(br_s, br_now)
                for ng in range(2):
                    S.wait(P.sem["vector"], t_scan_pp[ng])
                    S.wait(P.sem["vector"], t_dtx[pp])
                    for n in range(8):
                        S.op("activation", out=dA_pp[ng][:, n * LC:(n + 1) * LC],
                             in_=dt_f[pp][:], func=AF.Exp,
                             scale=negA[:, jj * 16 + ng * 8 + n:jj * 16 + ng * 8 + n + 1])
                    t_dA = S.tok()
                    V.wait(P.sem["scalar"], t_dA)
                    V.op("tensor_tensor", out=u_v[:],
                         in0=dtx_b[pp][:].unsqueeze(1).broadcast_to((128, 8, LC)),
                         in1=BrepF_v[:, ng * 8 * LC:(ng + 1) * 8 * LC]
                         .rearrange("p (n l) -> p n l", n=8), op=ALU.mult)
                    for n in range(8):
                        cc = jj * 16 + ng * 8 + n
                        V.op("tensor_tensor_scan", out=h_v[:, n * LC:(n + 1) * LC],
                             data0=dA_pp[ng][:, n * LC:(n + 1) * LC],
                             data1=u_v[:, n * LC:(n + 1) * LC],
                             initial=state[:, cc:cc + 1], op0=ALU.mult, op1=ALU.add)
                    V.op("tensor_copy",
                         out=state[:, jj * 16 + ng * 8:jj * 16 + ng * 8 + 8],
                         in_=h_v[:].rearrange("p (n l) -> p n l", n=8)[:, :, LC - 1:LC])
                    V.op("tensor_tensor", out=hc_v[:], in0=h_v[:],
                         in1=wslab[ng][:], op=ALU.mult)
                    V.op("tensor_tensor", out=r8_t[:, 0:4 * LC], in0=hc_v[:, 0:4 * LC],
                         in1=hc_v[:, 4 * LC:8 * LC], op=ALU.add)
                    V.op("tensor_tensor", out=r8_t[:, 0:2 * LC],
                         in0=r8_t[:, 0:2 * LC], in1=r8_t[:, 2 * LC:4 * LC], op=ALU.add)
                    if ng == 0:
                        V.op("tensor_tensor", out=yh1[:], in0=r8_t[:, 0:LC],
                             in1=r8_t[:, LC:2 * LC], op=ALU.add)
                    else:
                        V.op("tensor_tensor", out=ysc_t[:], in0=r8_t[:, 0:LC],
                             in1=r8_t[:, LC:2 * LC], op=ALU.add)
                        V.op("tensor_tensor", out=yh1[:], in0=yh1[:], in1=ysc_t[:],
                             op=ALU.add)
                    t_scan_pp[ng] = V.tok()
                    t_scan_done = t_scan_pp[ng]
                V.op("scalar_tensor_tensor", out=ym_t[:], in0=xnA[pp][:],
                     scalar=Dcol[:, jj:jj + 1], in1=yh1[:], op0=ALU.mult, op1=ALU.add)
                t_scan_done = V.tok()
                T.wait(P.sem["vector"], t_mid0p)
                T.wait(P.sem["vector"], t_x1)
                T.mm(p_d0[:], wB0_b[:, jj * 128:(jj + 1) * 128], mid0p[:, cs],
                     start=True, stop=True)
                T.mark()
                t_d0 = T.tok()
                Y.wait(P.sem["vector"], t_scan_done)
                Y.wait(st_s, st_x1)
                Y.dma(xr_t[0][:], x_fm[jj * 128:(jj + 1) * 128, cs], ld_s, ld_c)
                ld_xr = ld_c["n"]
                V.wait(ld_s, ld_xr)
                V.op("tensor_tensor", out=ysc_t[:], in0=ym_t[:], in1=xr_t[0][:],
                     op=ALU.add)
                V.wait(P.sem["tensor"], t_d0)
                V.op("tensor_tensor", out=xr_t[0][:], in0=ysc_t[:], in1=p_d0[:],
                     op=ALU.add)
                t_x1 = V.tok()
                Y.wait(P.sem["vector"], t_x1)
                st_x1 = Y.dma(x1_sc[jj * 128:(jj + 1) * 128, cs], xr_t[0][:],
                              st_s, st_c)

        ev2 = {"dtin": 0, "bcr": 0, "mid": 0, "srow": S.tok(), "rbc": S.tok()}
        T.wait(P.sem["vector"], V.tok())
        S.wait(P.sem["vector"], V.tok())
        S.wait(ld_s, ld_c["n"])
        Y.wait(st_s, st_c["n"])
        ln_stats(x1_sc, st1_p, ev2)
        G.wait(st_s, st_c["n"])
        cc3 = G.cc("AllReduce", ALU.add, st1_p, st1_f, cc_s, cc_c)
        Y.wait(cc_s, cc3)
        Y.dma(s1_sb[:], st1_f[0:1, :], ld_s, ld_c)
        Y.dma(s2_sb[:], st1_f[1:2, :], ld_s, ld_c)
        V.wait(ld_s, ld_c["n"])
        ln_norm(x1_sc, xn1h, ev2, False, evd)
        G.wait(xn_s, xn_c["n"])
        cc4 = G.cc("AllGather", BYP, xn1h, xn1_f, cc_s, cc_c)

        t_cast = [0, 0]
        t_rot = [0, 0]
        t_ve = [0, 0]
        t_middma = 0
        t_qkb_pp = [0, 0]
        t_swt_pp = [0, 0]
        t_vec = 0
        G.wait(P.sem["vector"], V.tok())
        G.wait(P.sem["tensor"], T.tok())
        G.wait(P.sem["scalar"], S.tok())
        for lq in range(4):
            hs = slice(lq * LQ, (lq + 1) * LQ)
            Y.wait(cc_s, cc4)
            Y.wait(xn_s, xn_c["n"])
            Y.wait(P.sem["tensor"], T.tok())
            Y.wait(P.sem["scalar"], S.tok())
            Y.wait(P.sem["vector"], V.tok())
            for j in range(NDF):
                Y.dma(xnh[:, j * LQ:(j + 1) * LQ],
                      xn1_f[j * 128:(j + 1) * 128, hs], ld_s, ld_c)
            ld_xnh = ld_c["n"]
            T.wait(ld_s, ld_xnh)
            for w_in, spill in ((w_k, k_sp), (w_q, q_sp)):
                for mch in range(16):
                    wpp = mch % 2
                    G.wait(P.sem["vector"], max(t_cast[0], t_cast[1]))
                    G.dma(wtmp[:, 0:4096].rearrange("p (c k) -> p c k", k=128),
                          w_in[:, mch * 128:(mch + 1) * 128]
                          .rearrange("(c p) k -> p c k", p=128), wlg_s, wlg_c)
                    V.wait(wlg_s, wlg_c["n"])
                    V.wait(P.sem["tensor"], T.tok())
                    V.op("tensor_copy", out=wslab[wpp][:], in_=wtmp[:, 0:4096])
                    t_cast[wpp] = V.tok()
                    for lt in range(LQ // LC):
                        gcs = lq * LQ + lt * LC
                        pp = lt % 2
                        T.wait(P.sem["vector"], t_cast[wpp])
                        T.wait(P.sem["scalar"], S.tok())
                        T.wait(P.sem["vector"], V.tok())
                        for kc in range(NDF):
                            T.mm(p_qkv[pp][:], wslab[wpp][:, kc * 128:(kc + 1) * 128],
                                 xnh[:, kc * LQ + lt * LC:kc * LQ + (lt + 1) * LC],
                                 start=(kc == 0), stop=(kc == NDF - 1))
                        T.mark()
                        t_mm = T.tok()
                        S.wait(P.sem["tensor"], t_mm)
                        S.wait(P.sem["vector"], V.tok())
                        S.op("copy", out=qk_b[:], in_=p_qkv[pp][:])
                        t_qkb = S.tok()
                        T.wait(P.sem["scalar"], t_qkb)
                        T.mm(p_sw[:], pswap_s[:], qk_b[:], start=True, stop=True)
                        T.mark()
                        t_swt = T.tok()
                        V.wait(P.sem["tensor"], t_swt)
                        V.wait(sp_s, t_rot[pp])
                        V.op("tensor_tensor", out=rot1[pp][:], in0=p_sw[:],
                             in1=sin_s[:, gcs:gcs + LC], op=ALU.mult)
                        V.op("tensor_tensor", out=rot2[:], in0=qk_b[:],
                             in1=cos_s[:, gcs:gcs + LC], op=ALU.mult)
                        V.op("tensor_tensor", out=rot1[pp][:], in0=rot1[pp][:],
                             in1=rot2[:], op=ALU.add)
                        t_rv = V.tok()
                        Y.wait(P.sem["vector"], t_rv)
                        t_rot[pp] = Y.dma(
                            spill[mch * 128:(mch + 1) * 128, gcs:gcs + LC],
                            rot1[pp][:], sp_s, sp_c)
            for dvc in range(16):
                wpp = dvc % 2
                G.wait(P.sem["vector"], max(t_cast[0], t_cast[1]))
                G.dma(wtmp[:, 0:4096].rearrange("p (c k) -> p c k", k=128),
                      w_v[:, dvc * 128:(dvc + 1) * 128]
                      .rearrange("(c p) k -> p c k", p=128), wlg_s, wlg_c)
                V.wait(wlg_s, wlg_c["n"])
                V.wait(P.sem["tensor"], T.tok())
                V.op("tensor_copy", out=wslab[wpp][:], in_=wtmp[:, 0:4096])
                t_cast[wpp] = V.tok()
                for lt in range(LQ // LC):
                    gcs = lq * LQ + lt * LC
                    pp = lt % 2
                    T.wait(P.sem["vector"], t_cast[wpp])
                    T.wait(P.sem["scalar"], t_qkb_pp[pp])
                    for kc in range(NDF):
                        T.mm(p_qkv[pp][:], wslab[wpp][:, kc * 128:(kc + 1) * 128],
                             xnh[:, kc * LQ + lt * LC:kc * LQ + (lt + 1) * LC],
                             start=(kc == 0), stop=(kc == NDF - 1))
                    T.mark()
                    t_mm = T.tok()
                    S.wait(P.sem["tensor"], t_mm)
                    S.wait(P.sem["tensor"], t_swt_pp[pp])
                    S.op("copy", out=qk_b2[pp][:], in_=p_qkv[pp][:])
                    t_qkb_pp[pp] = S.tok()
                    for sub in range(4):
                        T.wait(P.sem["scalar"], t_qkb_pp[pp] if sub == 0 else t_vec)
                        T.raw("transpose", p_sw[:, 0:64].bitcast(BF16),
                              qk_b2[pp][:, sub * 128:(sub + 1) * 128], identb[:])
                        T.mark()
                        t_tr = T.tok()
                        if sub == 3:
                            t_swt_pp[pp] = t_tr
                        spp = sub % 2
                        S.wait(P.sem["tensor"], t_tr)
                        S.wait(sp_s, t_ve[spp])
                        S.op("copy", out=vout_b[spp][:], in_=p_sw[:, 0:64].bitcast(BF16))
                        t_vec = S.tok()
                        Y.wait(P.sem["scalar"], t_vec)
                        t_ve[spp] = Y.dma(
                            v_sp[gcs + sub * 128:gcs + (sub + 1) * 128,
                                 dvc * 128:(dvc + 1) * 128],
                            vout_b[spp][:], sp_s, sp_c)
            for ltt in range(LQ // 128):
                T.wait(P.sem["scalar"], S.tok())
                for kc in range(NDF):
                    T.mm(p_g8, xnh[:, kc * LQ + ltt * 128:kc * LQ + (ltt + 1) * 128],
                         gT_b[:, kc * 8:(kc + 1) * 8], start=(kc == 0),
                         stop=(kc == NDF - 1))
                T.mark()
                t_gmm = T.tok()
                S.wait(P.sem["tensor"], t_gmm)
                S.op("copy", out=gtok[:], in_=p_g8)
                t_ge = S.tok()
                V.wait(P.sem["scalar"], t_ge)
                V.op("tensor_reduce", out=g_m1[:], in_=gtok[:], axis=AX.X, op=ALU.max)
                V.op("tensor_scalar", out=g_nm1[:], in0=g_m1[:], scalar1=-1.0,
                     scalar2=0.0, op0=ALU.mult, op1=BYP)
                V.op("tensor_scalar", out=g_ind[:], in0=gtok[:], scalar1=g_m1[:],
                     scalar2=0.0, op0=ALU.is_ge, op1=BYP)
                V.op("scalar_tensor_tensor", out=g_t4[:], in0=g_ind[:], scalar=-1e30,
                     in1=gtok[:], op0=ALU.mult, op1=ALU.add)
                V.op("tensor_reduce", out=g_m2[:], in_=g_t4[:], axis=AX.X, op=ALU.max)
                V.op("tensor_scalar", out=g_ind[:], in0=gtok[:], scalar1=g_m2[:],
                     scalar2=0.0, op0=ALU.is_ge, op1=BYP)
                S.wait(P.sem["vector"], V.tok())
                S.op("activation", out=g_e[:], in_=gtok[:], func=AF.Exp, bias=g_nm1[:])
                V.wait(P.sem["scalar"], S.tok())
                V.op("tensor_tensor", out=g_raw[:], in0=g_e[:], in1=g_ind[:],
                     op=ALU.mult)
                V.op("tensor_reduce", out=g_s[:], in_=g_raw[:], axis=AX.X, op=ALU.add)
                V.op("reciprocal", out=g_sr[:], in_=g_s[:])
                V.op("tensor_scalar", out=g_raw[:], in0=g_raw[:], scalar1=g_sr[:],
                     scalar2=0.0, op0=ALU.mult, op1=BYP)
                V.op("tensor_copy", out=g_tokb[:], in_=g_raw[:])
                t_gmath = V.tok()
                T.wait(P.sem["vector"], t_gmath)
                T.wait(P.sem["scalar"], S.tok())
                T.raw("transpose", p_gt8, g_tokb[:], identb[:])
                T.mark()
                S.wait(P.sem["tensor"], T.tok())
                S.op("copy",
                     out=g_fm1[:, lq * LQ + ltt * 128:lq * LQ + (ltt + 1) * 128],
                     in_=p_gt8)
            for erc in range(2):
                wpp = erc % 2
                G.wait(P.sem["vector"], max(t_cast[0], t_cast[1]))
                G.dma(wtmp[:, 0:4096].rearrange("p (c k) -> p c k", k=128),
                      w_A1[:, erc * 128:(erc + 1) * 128]
                      .rearrange("(c p) k -> p c k", p=128), wlg_s, wlg_c)
                V.wait(wlg_s, wlg_c["n"])
                V.wait(P.sem["tensor"], T.tok())
                V.op("tensor_copy", out=wslab[wpp][:], in_=wtmp[:, 0:4096])
                t_cast[wpp] = V.tok()
                for lt in range(LQ // LC):
                    pp = lt % 2
                    gcs = lq * LQ + lt * LC
                    T.wait(P.sem["vector"], t_cast[wpp])
                    T.wait(P.sem["vector"], V.tok())
                    for kc in range(NDF):
                        T.mm(p_qkv[pp][:], wslab[wpp][:, kc * 128:(kc + 1) * 128],
                             xnh[:, kc * LQ + lt * LC:kc * LQ + (lt + 1) * LC],
                             start=(kc == 0), stop=(kc == NDF - 1))
                    T.mark()
                    t_mm = T.tok()
                    T.wait(P.sem["scalar"], S.tok())
                    T.mm(p_sw[:], sel1_s[:, erc * 128:(erc + 1) * 128],
                         g_fm1[:, gcs:gcs + LC], start=True, stop=True)
                    T.mark()
                    t_gbc = T.tok()
                    S.wait(P.sem["tensor"], t_gbc)
                    S.op("copy", out=gbc_t[:], in_=p_sw[:])
                    t_gbce = S.tok()
                    V.wait(P.sem["tensor"], t_mm)
                    V.wait(P.sem["scalar"], t_gbce)
                    V.wait(st_s, t_middma)
                    V.op("tensor_tensor", out=mid1_t[:], in0=p_qkv[pp][:],
                         in1=gbc_t[:], op=ALU.mult)
                    t_mid = V.tok()
                    Y.wait(P.sem["vector"], t_mid)
                    t_middma = Y.dma(midh[erc * 128:(erc + 1) * 128, gcs:gcs + LC],
                                     mid1_t[:], st_s, st_c)
        G.wait(st_s, st_c["n"])
        cc5 = G.cc("AllGather", BYP, midh, mid_f, cc_s, cc_c)

        t_oh_dma = 0
        t_oh_prev = 0
        t_lrow_prev = 0
        for h in range(16):
            Y.wait(sp_s, sp_c["n"])
            Y.wait(P.sem["tensor"], T.tok())
            Y.wait(P.sem["scalar"], S.tok())
            Y.wait(P.sem["vector"], V.tok())
            Y.dma(Kh[:], k_sp[h * 128:(h + 1) * 128, :], ld_s, ld_c)
            Y.dma(Qh[:], q_sp[h * 128:(h + 1) * 128, :], ld_s, ld_c)
            Y.dma(Vsb[:].rearrange("p (c k) -> p c k", k=128),
                  v_sp[:, h * 128:(h + 1) * 128].rearrange("(c p) k -> p c k", p=128),
                  ld_s, ld_c)
            ld_h = ld_c["n"]
            T.wait(ld_s, ld_h)
            for qb in range(NLC):
                nkt = 4 * (qb + 1)
                t_exp = [0, 0]
                t_avm = [0, 0]
                t_msk = [0, 0]
                for kt in range(nkt + 1):
                    if kt < nkt:
                        pp = kt % 2
                        T.wait(P.sem["scalar"], t_exp[pp])
                        T.mm(p_s[:] if pp == 0 else p_s2nd[:],
                             Kh[:, kt * 128:(kt + 1) * 128],
                             Qh[:, qb * LC:(qb + 1) * LC], start=True, stop=True)
                        T.mark()
                        t_sc = T.tok()
                    if kt >= 1:
                        km = kt - 1
                        pk = km % 2
                        if km >= nkt - 4:
                            T.wait(P.sem["vector"], t_msk[pk])
                        else:
                            T.wait(P.sem["scalar"], t_exp[pk])
                        if km == 0:
                            T.wait(P.sem["vector"], t_oh_prev)
                            T.wait(P.sem["scalar"], t_lrow_prev)
                        T.mm(p_o[:], Vsb[:, km * 128:(km + 1) * 128], PT[pk][:],
                             start=(km == 0), stop=(km == nkt - 1))
                        T.mm(p_l[0:1, :], onesc[:], PT[pk][:], start=(km == 0),
                             stop=(km == nkt - 1))
                        T.mark()
                        t_avm[pk] = T.tok()
                    if kt < nkt:
                        S.wait(P.sem["tensor"], t_sc)
                        S.wait(P.sem["tensor"], t_avm[pp])
                        if kt >= nkt - 4:
                            S.wait(P.sem["vector"], t_msk[pp])
                        S.op("activation", out=PT[pp][:],
                             in_=(p_s[:] if pp == 0 else p_s2nd[:]), func=AF.Exp,
                             scale=0.08838834764831845)
                        t_exp[pp] = S.tok()
                        if kt >= nkt - 4:
                            mj = kt - (nkt - 4)
                            V.wait(P.sem["scalar"], t_exp[pp])
                            V.op("tensor_tensor", out=PT[pp][:], in0=PT[pp][:],
                                 in1=mask_s[:, mj * 512:(mj + 1) * 512], op=ALU.mult)
                            t_msk[pp] = V.tok()
                t_av = T.tok()
                S.wait(P.sem["tensor"], t_av)
                S.op("copy", out=lrow[:], in_=p_l[0:1, :])
                t_lrow = S.tok()
                t_lrow_prev = t_lrow
                V.wait(P.sem["scalar"], t_lrow)
                V.op("reciprocal", out=lrec[:], in_=lrow[:])
                V.op("tensor_copy", out=lrec_b[:], in_=lrec[:])
                t_lr = V.tok()
                T.wait(P.sem["vector"], t_lr)
                T.wait(P.sem["scalar"], S.tok())
                T.mm(p_bc[:], onesr[:], lrec_b[:], start=True, stop=True)
                T.mark()
                t_lb = T.tok()
                S.wait(P.sem["tensor"], t_lb)
                S.op("copy", out=lbc[:], in_=p_bc[:])
                t_lbc = S.tok()
                V.wait(P.sem["scalar"], t_lbc)
                V.wait(st_s, t_oh_dma)
                V.op("tensor_tensor", out=oh_t[:], in0=p_o[:], in1=lbc[:], op=ALU.mult)
                t_oh = V.tok()
                t_oh_prev = t_oh
                Y.wait(P.sem["vector"], t_oh)
                t_oh_dma = Y.dma(ohh[h * 128:(h + 1) * 128, qb * LC:(qb + 1) * LC],
                                 oh_t[:], st_s, st_c)
        G.wait(st_s, st_c["n"])
        cc6 = G.cc("AllGather", BYP, ohh, oh_f, cc_s, cc_c)

        G.wait(P.sem["vector"], V.tok())
        G.wait(P.sem["tensor"], T.tok())
        G.wait(P.sem["scalar"], S.tok())
        for lq in range(4):
            hs = slice(lq * LQ, (lq + 1) * LQ)
            Y.wait(cc_s, cc6)
            Y.wait(st_s, st_c["n"])
            Y.wait(P.sem["tensor"], T.tok())
            Y.wait(P.sem["scalar"], S.tok())
            Y.wait(P.sem["vector"], V.tok())
            for j in range(NDF):
                Y.dma(oh_res[:, j * LQ:(j + 1) * LQ],
                      oh_f[j * 128:(j + 1) * 128, hs], ld_s, ld_c)
            for j in range(4):
                Y.dma(mid_res[:, j * LQ:(j + 1) * LQ],
                      mid_f[j * 128:(j + 1) * 128, hs], ld_s, ld_c)
            ld_all = ld_c["n"]
            T.wait(ld_s, ld_all)
            t_x2 = [0, 0]
            st_t2 = [0, 0]
            for m in range(16):
                wpp = m % 2
                G.wait(P.sem["vector"], max(t_cast[0], t_cast[1]))
                G.dma(wtmp[:, 0:4096].rearrange("p (c k) -> p c k", k=128),
                      w_o[:, m * 128:(m + 1) * 128]
                      .rearrange("(c p) k -> p c k", p=128), wlg_s, wlg_c)
                G.dma(rows_b2[:].rearrange("p (c k) -> p c k", k=128),
                      w_B1[:, m * 128:(m + 1) * 128]
                      .rearrange("(c p) k -> p c k", p=128), wlg_s, wlg_c)
                V.wait(wlg_s, wlg_c["n"])
                V.wait(P.sem["tensor"], T.tok())
                V.op("tensor_copy", out=wslab[wpp][:], in_=wtmp[:, 0:4096])
                V.op("tensor_copy", out=qk_b[:], in_=rows_b2[:])
                t_cast[wpp] = V.tok()
                x1buf = [xr_t[0], x1t_[0]]
                for lt in range(LQ // LC):
                    pp = lt % 2
                    gcs = lq * LQ + lt * LC
                    Y.wait(P.sem["vector"], t_x2[pp])
                    Y.dma(x1buf[pp][:], x1_sc[m * 128:(m + 1) * 128, gcs:gcs + LC],
                          ld_s, ld_c)
                    ld_x1 = ld_c["n"]
                    T.wait(P.sem["vector"], t_cast[wpp])
                    T.wait(P.sem["vector"], t_x2[pp])
                    for kc in range(NDF):
                        T.mm(p_y[pp][:], wslab[wpp][:, kc * 128:(kc + 1) * 128],
                             oh_res[:, kc * LQ + lt * LC:kc * LQ + (lt + 1) * LC],
                             start=(kc == 0), stop=False)
                    for ec in range(4):
                        T.mm(p_y[pp][:], qk_b[:, ec * 128:(ec + 1) * 128],
                             mid_res[:, ec * LQ + lt * LC:ec * LQ + (lt + 1) * LC],
                             start=False, stop=(ec == 3))
                    T.mark()
                    t_mm = T.tok()
                    V.wait(P.sem["tensor"], t_mm)
                    V.wait(ld_s, ld_x1)
                    V.wait(st_s, st_t2[pp])
                    V.op("tensor_tensor", out=x2t[pp][:], in0=p_y[pp][:],
                         in1=x1buf[pp][:], op=ALU.add)
                    t_x2[pp] = V.tok()
                    Y.wait(P.sem["vector"], t_x2[pp])
                    st_t2[pp] = Y.dma(x2_out[m * 128:(m + 1) * 128, gcs:gcs + LC],
                                      x2t[pp][:], st_s, st_c)
        Y.wait(st_s, st_c["n"])
    with nc.Block() as blk:
        P.finalize(blk)
    return nc


def _prep_fused_inputs(inputs):
    x = inputs["L0_attn_x"]
    xpw = inputs["L0_attn_x_proj_w"]
    dtw = inputs["L0_attn_dt_proj_w"]
    dtb = inputs["L0_attn_dt_proj_b"]
    Alog = inputs["L0_attn_A_log"]
    Dv = inputs["L0_attn_D"]
    rw = inputs["L0_ffn_router_w"]
    A0 = inputs["L0_ffn_A"]
    B0 = inputs["L0_ffn_B"]
    wq = inputs["L1_attn_wq"]
    wk = inputs["L1_attn_wk"]
    wv = inputs["L1_attn_wv"]
    wo = inputs["L1_attn_wo"]
    gw = inputs["L1_ffn_gate_w"]
    lA = inputs["L1_ffn_lora_A"]
    lB = inputs["L1_ffn_lora_B"]

    w_dtin = xpw[:128].T
    w_bcr = np.zeros((DM, 68), np.float32)
    w_bcr[:, 0:16] = xpw[128:144].T
    w_bcr[:, 32:48] = xpw[144:160].T
    w_bcr[:, 64:68] = rw.T
    w_mid = A0.reshape(64, DM).T
    sel0v = np.zeros((4, 64), np.float32)
    for e in range(4):
        sel0v[e, e * 16:(e + 1) * 16] = 1.0
    B0T = B0.transpose(0, 2, 1).reshape(64, DM)
    t = np.arange(L, dtype=np.float64)
    cosr = np.cos(t).astype(bf16).astype(np.float32)
    sinr = np.sin(t).astype(bf16).astype(np.float32)
    cosbv = np.ascontiguousarray(np.broadcast_to(cosr, (128, L))).astype(bf16)
    sinbv = np.ascontiguousarray(np.broadcast_to(sinr, (128, L))).astype(bf16)
    mk = np.zeros((128, 4, 512), np.float32)
    for mj in range(4):
        for k in range(128):
            mk[k, mj, mj * 128 + k:] = 1.0
    masksv = mk.reshape(128, 2048).astype(bf16)
    psw = np.zeros((128, 128), np.float32)
    for i in range(64):
        psw[2 * i + 1, 2 * i] = -1.0
        psw[2 * i, 2 * i + 1] = 1.0
    psw = psw.astype(bf16)
    identbv = np.eye(128, dtype=np.float32).astype(bf16)
    sel1v = np.zeros((8, 512), np.float32)
    for e in range(8):
        sel1v[e, e * 64:(e + 1) * 64] = 1.0
    sel1v = sel1v.astype(bf16)
    wA1 = lA.reshape(512, DM).T
    woT = wo.T
    wB1 = lB.transpose(0, 2, 1).reshape(512, DM)
    dtwT = dtw.T

    maps = []
    for c in range(8):
        b, hp = c // 2, c % 2
        hs = slice(hp * DH, (hp + 1) * DH)
        rs = slice(hp * 256, (hp + 1) * 256)
        m = {
            "x_fm": np.ascontiguousarray(x[b].T[hs]),
            "w_dtin": np.ascontiguousarray(w_dtin[hs]),
            "w_bcr": np.ascontiguousarray(w_bcr[hs]),
            "w_mid": np.ascontiguousarray(w_mid[hs]),
            "w_dt": np.ascontiguousarray(dtwT[:, hs]),
            "dt_bias": np.ascontiguousarray(dtb[hs].reshape(NT, 128).T),
            "A_log_t": np.ascontiguousarray(
                Alog[hs].reshape(NT, 128, 16).transpose(1, 0, 2)
                .reshape(128, NT * 16)),
            "D_t": np.ascontiguousarray(Dv[hs].reshape(NT, 128).T),
            "w_B0": np.ascontiguousarray(B0T[:, hs]),
            "sel0": sel0v,
            "w_q": np.ascontiguousarray(wq.T[:, hs]),
            "w_k": np.ascontiguousarray(wk.T[:, hs]),
            "w_v": np.ascontiguousarray(wv.T[:, hs]),
            "w_g": np.ascontiguousarray(gw.T),
            "w_A1": np.ascontiguousarray(wA1[:, rs]),
            "w_o": np.ascontiguousarray(woT[:, hs]),
            "w_B1": np.ascontiguousarray(wB1[:, hs]),
            "cosb": cosbv,
            "sinb": sinbv,
            "masks": masksv,
            "pswap": psw,
            "identbi": identbv,
            "sel1": np.ascontiguousarray(sel1v[:, rs]),
        }
        maps.append(m)
    return maps


def kernel(**inputs):
    maps = _prep_fused_inputs(inputs)
    nc = build_fused()
    res = run_bass_kernel_spmd(nc, maps, core_ids=list(range(8)))
    out = np.empty((NB, L, DM), np.float32)
    for c in range(8):
        b, hp = c // 2, c % 2
        x2 = np.asarray(res.results[c]["x2_out"])
        out[b, :, hp * DH:(hp + 1) * DH] = x2.T
    return out


# version 12
# speedup vs baseline: 1.8827x; 1.0446x over previous
import numpy as np
import ml_dtypes
import concourse.bass as bass
import concourse.mybir as mybir
from concourse.bass_utils import run_bass_kernel_spmd

F32 = mybir.dt.float32
BF16 = mybir.dt.bfloat16
AF = mybir.ActivationFunctionType
ALU = mybir.AluOpType
AX = mybir.AxisListType
bf16 = ml_dtypes.bfloat16

DM = 4096
NB = 4
L = 4096
DH = 2048
LC = 512
NLC = L // LC
NDF = DM // 128
NDH = DH // 128
NT = NDH
LQ = 1024
PAIRS = [[0, 1], [2, 3], [4, 5], [6, 7]]

BYP = ALU.bypass


class SEng:

    def __init__(self, sem, cnt):
        self.s = sem
        self.c = cnt
        self.q = []
        self.cell = {}

    def op(self, name, *a, **k):
        n0 = self.c["n"]
        s_ = self.s

        def run(e, n0=n0, name=name, a=a, k=k):
            if n0 > 0:
                e.wait_ge(s_, n0)
            getattr(e, name)(*a, **k).then_inc(s_, 1)
        self.q.append(run)
        self.c["n"] += 1
        return self.c["n"]

    def mm(self, out, lhsT, rhs, start, stop, **k):
        cell = self.cell

        def run(e, out=out, lhsT=lhsT, rhs=rhs, start=start, stop=stop, k=k):
            cell["last"] = e.matmul(out, lhsT, rhs, start=start, stop=stop, **k)
        self.q.append(run)

    def raw(self, name, *a, **k):
        cell = self.cell

        def run(e, name=name, a=a, k=k):
            cell["last"] = getattr(e, name)(*a, **k)
        self.q.append(run)

    def mark(self):
        cell = self.cell
        s_ = self.s

        def run(e):
            cell["last"].then_inc(s_, 1)
        self.q.append(run)
        self.c["n"] += 1
        return self.c["n"]

    def wait(self, sem, v):
        if v > 0:
            self.q.append(lambda e, sem=sem, v=v: e.wait_ge(sem, v))

    def dma(self, out, in_, sem, cnt, **k):
        def run(e, out=out, in_=in_, sem=sem, k=k):
            e.dma_start(out=out, in_=in_, **k).then_inc(sem, 16)
        self.q.append(run)
        cnt["n"] += 16
        return cnt["n"]

    def cc(self, kind, op, in_t, out_t, sem, cnt):
        def run(e, kind=kind, op=op, in_t=in_t, out_t=out_t, sem=sem):
            e.collective_compute(
                kind, op, replica_groups=PAIRS,
                ins=[in_t.ap().opt()], outs=[out_t.ap().opt()]).then_inc(sem, 1)
        self.q.append(run)
        cnt["n"] += 1
        return cnt["n"]

    def tok(self):
        return self.c["n"]


class Prog:
    ENGS = ("vector", "scalar", "tensor", "gpsimd", "sync")

    def __init__(self, nc):
        self.nc = nc
        self.sem = {e: nc.alloc_semaphore(f"es_{e}") for e in self.ENGS}
        self.cnt = {e: {"n": 0} for e in self.ENGS}
        self.dsem = {}
        self.dcnt = {}
        self.eng = {e: SEng(self.sem[e], self.cnt[e]) for e in self.ENGS}

    def dma_sem(self, name):
        if name not in self.dsem:
            self.dsem[name] = self.nc.alloc_semaphore(f"ds_{name}")
            self.dcnt[name] = {"n": 0}
        return self.dsem[name], self.dcnt[name]

    def finalize(self, blk):
        for name in self.ENGS:
            se = self.eng[name]

            def deco(raw, se=se):
                for fn in se.q:
                    fn(raw)
            getattr(blk, name)(deco)


def build_fused():
    nc = bass.Bass("TRN2", target_bir_lowering=False)

    x_fm = nc.dram_tensor("x_fm", [DH, L], F32, kind="ExternalInput")
    w_dtin = nc.dram_tensor("w_dtin", [DH, 128], F32, kind="ExternalInput")
    w_bcr = nc.dram_tensor("w_bcr", [DH, 68], F32, kind="ExternalInput")
    w_mid = nc.dram_tensor("w_mid", [DH, 64], F32, kind="ExternalInput")
    w_dt = nc.dram_tensor("w_dt", [128, DH], F32, kind="ExternalInput")
    dt_bias = nc.dram_tensor("dt_bias", [128, NT], F32, kind="ExternalInput")
    A_log_t = nc.dram_tensor("A_log_t", [128, NT * 16], F32, kind="ExternalInput")
    D_t = nc.dram_tensor("D_t", [128, NT], F32, kind="ExternalInput")
    w_B0 = nc.dram_tensor("w_B0", [64, DH], F32, kind="ExternalInput")
    sel0 = nc.dram_tensor("sel0", [4, 64], F32, kind="ExternalInput")
    w_q = nc.dram_tensor("w_q", [DM, DH], F32, kind="ExternalInput")
    w_k = nc.dram_tensor("w_k", [DM, DH], F32, kind="ExternalInput")
    w_v = nc.dram_tensor("w_v", [DM, DH], F32, kind="ExternalInput")
    w_g = nc.dram_tensor("w_g", [DM, 8], F32, kind="ExternalInput")
    w_A1 = nc.dram_tensor("w_A1", [DM, 256], F32, kind="ExternalInput")
    w_o = nc.dram_tensor("w_o", [DM, DH], F32, kind="ExternalInput")
    w_B1 = nc.dram_tensor("w_B1", [512, DH], F32, kind="ExternalInput")
    cosb = nc.dram_tensor("cosb", [128, L], BF16, kind="ExternalInput")
    sinb = nc.dram_tensor("sinb", [128, L], BF16, kind="ExternalInput")
    masks = nc.dram_tensor("masks", [128, 4 * 512], BF16, kind="ExternalInput")
    pswap = nc.dram_tensor("pswap", [128, 128], BF16, kind="ExternalInput")
    identbi = nc.dram_tensor("identbi", [128, 128], BF16, kind="ExternalInput")
    sel1 = nc.dram_tensor("sel1", [8, 256], BF16, kind="ExternalInput")
    x2_out = nc.dram_tensor("x2_out", [DH, L], F32, kind="ExternalOutput")

    st0_p = nc.dram_tensor("st0_p", [2, L], F32, kind="Internal")
    st0_f = nc.dram_tensor("st0_f", [2, L], F32, kind="Internal")
    st1_p = nc.dram_tensor("st1_p", [2, L], F32, kind="Internal")
    st1_f = nc.dram_tensor("st1_f", [2, L], F32, kind="Internal")
    proj_p = nc.dram_tensor("proj_p", [260, L], F32, kind="Internal")
    proj_f = nc.dram_tensor("proj_f", [260, L], F32, kind="Internal")
    xn_sc = nc.dram_tensor("xn_sc", [DH, L], BF16, kind="Internal")
    Bb_sc = nc.dram_tensor("Bb_sc", [16, L], BF16, kind="Internal")
    Cb_sc = nc.dram_tensor("Cb_sc", [16, L], BF16, kind="Internal")
    x1_sc = nc.dram_tensor("x1_sc", [DH, L], F32, kind="Internal")
    xn1h = nc.dram_tensor("xn1h", [DH, L], BF16, kind="Internal")
    xn1_f = nc.dram_tensor("xn1_f", [DM, L], BF16, kind="Internal")
    q_sp = nc.dram_tensor("q_sp", [DH, L], BF16, kind="Internal")
    k_sp = nc.dram_tensor("k_sp", [DH, L], BF16, kind="Internal")
    v_sp = nc.dram_tensor("v_sp", [L, DH], BF16, kind="Internal")
    ohh = nc.dram_tensor("ohh", [DH, L], BF16, kind="Internal")
    oh_f = nc.dram_tensor("oh_f", [DM, L], BF16, kind="Internal")
    midh = nc.dram_tensor("midh", [256, L], BF16, kind="Internal")
    mid_f = nc.dram_tensor("mid_f", [512, L], BF16, kind="Internal")

    sb = nc.alloc_sbuf_tensor
    MEGA = sb("MEGA", [128, 16384], F32)
    xnb_v = MEGA[:, 0:4096].bitcast(BF16)
    xsq_v = MEGA[:, 8192:12288].bitcast(BF16)
    stgA = MEGA[:, 12288:16384]
    stgB = MEGA[0:64, 4096:8192]
    s1_sb = MEGA[0:1, 4096:8192]
    s2_sb = MEGA[32:33, 4096:8192]
    g_fm0 = MEGA[0:4, 0:2048].bitcast(BF16)
    gbc0 = MEGA[64:128, 0:2048].bitcast(BF16)
    wB0stg = MEGA[0:64, 4096:6144]
    dA_pp = [MEGA[:, 0:4096], MEGA[:, 4096:8192]]
    u_v = MEGA[:, 8192:10240].bitcast(BF16)
    h_v = MEGA[:, 10240:12288].bitcast(BF16)
    hc_v = u_v
    BrepF_v = MEGA[:, 12288:16384].bitcast(BF16)
    xnh = MEGA[:].bitcast(BF16)
    Kh_b = [MEGA[:, 0:2048].bitcast(BF16), MEGA[:, 8192:10240].bitcast(BF16)]
    Qh_b = [MEGA[:, 2048:4096].bitcast(BF16), MEGA[:, 10240:12288].bitcast(BF16)]
    Vsb_b = [MEGA[:, 4096:6144].bitcast(BF16), MEGA[:, 12288:14336].bitcast(BF16)]
    PT = [MEGA[:, 6144 + i * 256:6400 + i * 256].bitcast(BF16) for i in range(2)]
    lbc = MEGA[:, 6656:6912].bitcast(BF16)
    oh_t = MEGA[:, 6912:7168].bitcast(BF16)
    oh_res = MEGA[:].bitcast(BF16)

    wtmp = sb("wtmp", [128, 4096], F32)
    Bfb = wtmp[0:16, 0:2048].bitcast(BF16)
    Cfb = wtmp[32:48, 0:2048].bitcast(BF16)
    mid0b = wtmp[64:128, 0:2048].bitcast(BF16)
    lgf = wtmp[0:4, 2048:4096].bitcast(BF16)
    wB0_b = wtmp[64:128, 2048:3072].bitcast(BF16)
    wdtin_b = sb("wdtin_b", [128, NDH * 128], BF16)
    wbcr_b = sb("wbcr_b", [128, NDH * 68], BF16)
    wmid_b = sb("wmid_b", [128, NDH * 64], BF16)
    mid_res = sb("mid_res", [128, 8192], BF16)
    wdt_b = mid_res[:, 0:2048]
    dtin_f = mid_res[:, 4096:8192]
    wslab = [sb(f"wslab{i}", [128, 4096], BF16) for i in range(2)]
    sel0_b = sb("sel0_b", [4, 64], BF16)
    identb = sb("identb", [128, 128], BF16)
    dtb_t = sb("dtb_t", [128, NT], F32)
    Alog_s = sb("Alog_s", [128, NT * 16], F32)
    negA = sb("negA", [128, NT * 16], F32)
    Dcol = sb("Dcol", [128, NT], F32)
    gT_b = sb("gT_b", [128, 8 * NDF], BF16)
    g_fm1 = sb("g_fm1", [8, L], BF16)
    cos_s = sb("cos_s", [128, L], BF16)
    sin_s = sb("sin_s", [128, L], BF16)
    mask_s = sb("mask_s", [128, 4 * 512], BF16)
    pswap_s = sb("pswap_s", [128, 128], BF16)
    sel1_s = sb("sel1_s", [8, 256], BF16)
    pr_dt = sb("pr_dt", [128, LC], F32)
    pr_bc = sb("pr_bc", [68, LC], F32)
    pr_md = sb("pr_md", [64, LC], F32)

    onesc = sb("onesc", [128, 1], BF16)
    onesf = sb("onesf", [128, 1], F32)
    onesr = sb("onesr", [1, 128], BF16)
    ones1 = sb("ones1", [128, 1], F32)
    epsc = sb("epsc", [1, 1], F32)
    rows_a = sb("rows_a", [128, LC], F32)
    rows_b2 = sb("rows_b2", [128, LC], F32)
    rows_c = sb("rows_c", [1, LC], F32)
    rows_d = sb("rows_d", [1, LC], F32)
    mrow = rows_c[0:1, :]
    qrow = rows_d[0:1, :]
    m2row = rows_a[0:1, :]
    vrow = rows_b2[0:1, :]
    sdrow = rows_d[0:1, :]
    rrow = rows_a[0:1, :]
    mrow_b = sb("mrow_b", [1, LC], BF16)
    rrow_b = sb("rrow_b", [1, LC], BF16)
    m_bc = sb("m_bc", [128, LC], F32)
    r_bc = sb("r_bc", [128, LC], F32)
    xn_t = rows_a
    abs_t = sb("abs_t", [128, LC], F32)
    exp_t = rows_a
    relu_t = rows_b2
    r8_t = wdtin_b
    ysc_t = m_bc
    ym_t = r_bc
    yh1 = sb("yh1", [128, LC], F32)
    lsum = sb("lsum", [128, LC], F32)
    dt_f = [sb(f"dt_f{i}", [128, LC], F32) for i in range(2)]
    dtx_b = [sb(f"dtx_b{i}", [128, LC], BF16) for i in range(2)]
    xr_t = [sb("xr_t0", [128, LC], F32)]
    x1t_ = [sb("x1t_0", [128, LC], F32)]
    xnA = [sb(f"xnA{i}", [128, LC], BF16) for i in range(2)]
    state = sb("state", [128, NT * 16], F32)
    lgT = sb("lgT", [128, 4], F32)
    g_m1 = sb("g_m1", [128, 1], F32)
    g_nm1 = sb("g_nm1", [128, 1], F32)
    g_t4 = sb("g_t4", [128, 8], F32)
    g_m2 = sb("g_m2", [128, 1], F32)
    g_e = sb("g_e", [128, 8], F32)
    g_ind = sb("g_ind", [128, 8], F32)
    g_raw = sb("g_raw", [128, 8], F32)
    g_s = sb("g_s", [128, 1], F32)
    g_sr = sb("g_sr", [128, 1], F32)
    g_tok4 = sb("g_tok4", [128, 4], BF16)
    gtok = sb("gtok", [128, 8], F32)
    g_tokb = sb("g_tokb", [128, 8], BF16)
    gbc_t = sb("gbc_t", [128, LC], BF16)
    qk_b = sb("qk_b", [128, LC], BF16)
    qk_b2 = [qk_b, sb("qk_b_1", [128, LC], BF16)]
    qk_bx2 = sb("qk_bx2", [128, LC], BF16)
    qk_bx3 = sb("qk_bx3", [128, LC], BF16)
    rot1 = [sb(f"rot1_{i}", [128, LC], BF16) for i in range(2)]
    rot2 = sb("rot2", [128, LC], BF16)
    vout_b = [sb(f"vout_b{i}", [128, 128], BF16) for i in range(2)]
    vout_bx2 = sb("vout_bx2", [128, 128], BF16)
    vout_bx3 = sb("vout_bx3", [128, 128], BF16)
    mid1_t = sb("mid1_t", [128, LC], BF16)
    lrow = rows_c[0:1, :]
    lrec = rows_d[0:1, :]
    lrec_b = sb("lrec_b", [1, LC], BF16)
    x2t = [sb("x2t0", [128, LC], F32), yh1]

    ps = nc.alloc_psum_tensor
    p_stats = ps("p_stats", [128, LC], F32)
    p_s1 = p_stats[0:1, :]
    p_s2 = p_stats[32:33, :]
    p_bc = ps("p_bc", [128, LC], F32)
    p_dtin = ps("p_dtin", [128, LC], F32)
    p_bcrF = ps("p_bcrF", [128, LC], F32)
    p_bcr = p_bcrF[0:68, :]
    p_midF = ps("p_midF", [128, LC], F32)
    p_mid = p_midF[0:64, :]
    p_dtr = [p_bcrF, p_midF]
    p_d0 = p_dtin
    p_g = p_bc[:, 0:2].bitcast(BF16)
    p_gb = p_bc[0:4, 64:128].bitcast(BF16)
    p_qkv = [p_bcrF, p_midF]
    p_sw = ps("p_sw", [128, LC], F32)
    p_o = ps("p_o", [128, LC], F32)
    p_l = ps("p_l", [128, LC], F32)
    p_g8 = p_stats[:, 0:8]
    p_gt8 = p_bc[0:8, 0:64].bitcast(BF16)
    p_s = p_sw
    p_s2nd = p_dtin
    p_y = [p_bcrF, p_midF]

    P = Prog(nc)
    V, S, T, G, Y = (P.eng["vector"], P.eng["scalar"], P.eng["tensor"],
                     P.eng["gpsimd"], P.eng["sync"])
    if True:
        ld_s, ld_c = P.dma_sem("ld")
        wl_s, wl_c = P.dma_sem("wl")
        wlg_s, wlg_c = P.dma_sem("wlg")
        st_s, st_c = P.dma_sem("st")
        br_s, br_c = P.dma_sem("br")
        xn_s, xn_c = P.dma_sem("xn")
        sp_s, sp_c = P.dma_sem("sp")
        cc_s, cc_c = P.dma_sem("cc")

        Y.dma(dtb_t[:], dt_bias[:], wl_s, wl_c)
        Y.dma(Alog_s[:], A_log_t[:], wl_s, wl_c)
        Y.dma(Dcol[:], D_t[:], wl_s, wl_c)
        Y.dma(m_bc[0:4, 0:64], sel0[:], wl_s, wl_c)
        Y.dma(cos_s[:], cosb[:], wl_s, wl_c)
        Y.dma(sin_s[:], sinb[:], wl_s, wl_c)
        Y.dma(mask_s[:], masks[:], wl_s, wl_c)
        Y.dma(pswap_s[:], pswap[:], wl_s, wl_c)
        Y.dma(identb[:], identbi[:], wl_s, wl_c)
        Y.dma(sel1_s[:], sel1[:], wl_s, wl_c)
        G.dma(wtmp[:, 0:NDH * 128].rearrange("p (c k) -> p c k", k=128),
              w_dtin[:].rearrange("(c p) k -> p c k", p=128), wlg_s, wlg_c)
        V.wait(wl_s, wl_c["n"])
        V.wait(wlg_s, wlg_c["n"])
        V.op("memset", onesc[:], 1.0)
        V.op("memset", onesf[:], 1.0)
        V.op("memset", onesr[:], 1.0)
        V.op("memset", ones1[:], 1.0)
        V.op("memset", epsc[:], 1e-5)
        V.op("tensor_copy", out=sel0_b[:], in_=m_bc[0:4, 0:64])
        V.op("tensor_copy", out=wdtin_b[:], in_=wtmp[:, 0:NDH * 128])
        G.wait(P.sem["vector"], V.tok())
        G.dma(wtmp[:, 0:NDH * 68].rearrange("p (c k) -> p c k", k=68),
              w_bcr[:].rearrange("(c p) k -> p c k", p=128), wlg_s, wlg_c)
        V.wait(wlg_s, wlg_c["n"])
        V.op("tensor_copy", out=wbcr_b[:], in_=wtmp[:, 0:NDH * 68])
        G.wait(P.sem["vector"], V.tok())
        G.dma(wtmp[:, 0:NDH * 64].rearrange("p (c k) -> p c k", k=64),
              w_mid[:].rearrange("(c p) k -> p c k", p=128), wlg_s, wlg_c)
        V.wait(wlg_s, wlg_c["n"])
        V.op("tensor_copy", out=wmid_b[:], in_=wtmp[:, 0:NDH * 64])
        G.wait(P.sem["vector"], V.tok())
        G.dma(wtmp[:, 0:DH], w_dt[:], wlg_s, wlg_c)
        V.wait(wlg_s, wlg_c["n"])
        V.op("tensor_copy", out=wdt_b[:], in_=wtmp[:, 0:DH])
        G.wait(P.sem["vector"], V.tok())
        G.dma(wtmp[:, 0:8 * NDF].rearrange("p (c k) -> p c k", k=8),
              w_g[:].rearrange("(c p) k -> p c k", p=128), wlg_s, wlg_c)
        V.wait(wlg_s, wlg_c["n"])
        V.op("tensor_copy", out=gT_b[:], in_=wtmp[:, 0:8 * NDF])
        S.wait(wl_s, wl_c["n"])
        S.op("activation", out=negA[:], in_=Alog_s[:], func=AF.Exp)
        V.wait(P.sem["scalar"], S.tok())
        V.op("tensor_scalar", out=negA[:], in0=negA[:], scalar1=-1.0, scalar2=0.0,
             op0=ALU.mult, op1=BYP)

        def ln_stats(src, dst_p, ev):
            for lc in range(NLC):
                cs = slice(lc * LC, (lc + 1) * LC)
                T.wait(P.sem["scalar"], ev["srow"])
                for j in range(NDH):
                    xt = xr_t[0] if j % 2 == 0 else x1t_[0]
                    Y.wait(P.sem["tensor"], T.tok())
                    Y.wait(P.sem["scalar"], S.tok())
                    Y.dma(xt[:], src[j * 128:(j + 1) * 128, cs], ld_s, ld_c)
                    ldj = ld_c["n"]
                    T.wait(ld_s, ldj)
                    T.mm(p_s1, onesf[:], xt[:], start=(j == 0), stop=(j == NDH - 1))
                    if j == NDH - 1:
                        T.mark()
                    S.wait(ld_s, ldj)
                    S.op("activation", out=xsq_v[:, j * LC:(j + 1) * LC],
                         in_=xt[:], func=AF.Square)
                t_s1 = T.tok()
                t_sq = S.tok()
                T.wait(P.sem["scalar"], t_sq)
                for j in range(NDH):
                    T.mm(p_s2, onesc[:], xsq_v[:, j * LC:(j + 1) * LC],
                         start=(j == 0), stop=(j == NDH - 1))
                T.mark()
                t_s2 = T.tok()
                S.wait(P.sem["tensor"], t_s1)
                S.op("copy", out=s1_sb[0:1, cs], in_=p_s1)
                S.wait(P.sem["tensor"], t_s2)
                S.op("copy", out=s2_sb[0:1, cs], in_=p_s2)
                ev["srow"] = S.tok()
            Y.wait(P.sem["scalar"], S.tok())
            Y.dma(dst_p[0:1, :], s1_sb[:], st_s, st_c)
            Y.dma(dst_p[1:2, :], s2_sb[:], st_s, st_c)

        def ln_norm(src, dst_sc, ev, with_proj, evd):
            for lc in range(NLC):
                cs = slice(lc * LC, (lc + 1) * LC)
                V.wait(P.sem["scalar"], ev["rbc"])
                V.op("tensor_scalar", out=mrow[:], in0=s1_sb[0:1, cs],
                     scalar1=1.0 / DM, scalar2=0.0, op0=ALU.mult, op1=BYP)
                V.op("tensor_scalar", out=qrow[:], in0=s2_sb[0:1, cs],
                     scalar1=1.0 / DM, scalar2=0.0, op0=ALU.mult, op1=BYP)
                V.op("tensor_tensor", out=m2row[:], in0=mrow[:], in1=mrow[:],
                     op=ALU.mult)
                V.op("tensor_tensor", out=vrow[:], in0=qrow[:], in1=m2row[:],
                     op=ALU.subtract)
                S.wait(P.sem["vector"], V.tok())
                S.op("activation", out=sdrow[:], in_=vrow[:], func=AF.Sqrt,
                     bias=epsc[:])
                V.wait(P.sem["scalar"], S.tok())
                V.op("reciprocal", out=rrow[:], in_=sdrow[:])
                V.op("tensor_copy", out=mrow_b[:], in_=mrow[:])
                V.op("tensor_copy", out=rrow_b[:], in_=rrow[:])
                t_rowsb = V.tok()
                T.wait(P.sem["vector"], t_rowsb)
                T.wait(P.sem["scalar"], ev["rbc"])
                T.mm(p_bc[:], onesr[:], mrow_b[:], start=True, stop=True)
                T.mark()
                S.wait(P.sem["tensor"], T.tok())
                S.op("copy", out=m_bc[:], in_=p_bc[:])
                T.wait(P.sem["scalar"], S.tok())
                T.mm(p_bc[:], onesr[:], rrow_b[:], start=True, stop=True)
                T.mark()
                S.wait(P.sem["tensor"], T.tok())
                S.op("copy", out=r_bc[:], in_=p_bc[:])
                ev["rbc"] = S.tok()
                V.wait(P.sem["scalar"], ev["rbc"])
                for j in range(NDH):
                    xt = xr_t[0] if j % 2 == 0 else x1t_[0]
                    Y.wait(P.sem["vector"], V.tok())
                    Y.dma(xt[:], src[j * 128:(j + 1) * 128, cs], ld_s, ld_c)
                    ldj = ld_c["n"]
                    V.wait(ld_s, ldj)
                    V.op("tensor_tensor", out=xn_t[:], in0=xt[:], in1=m_bc[:],
                         op=ALU.subtract)
                    V.op("tensor_tensor", out=xnb_v[:, j * LC:(j + 1) * LC],
                         in0=xn_t[:], in1=r_bc[:], op=ALU.mult)
                t_norm = V.tok()
                Y.wait(P.sem["vector"], t_norm)
                for jj in range(NDH):
                    Y.dma(dst_sc[jj * 128:(jj + 1) * 128, cs],
                          xnb_v[:, jj * LC:(jj + 1) * LC], xn_s, xn_c)
                if with_proj:
                    T.wait(P.sem["vector"], t_norm)
                    T.wait(P.sem["scalar"], ev["dtin"])
                    for j in range(NDH):
                        T.mm(p_dtin[:], wdtin_b[:, j * 128:(j + 1) * 128],
                             xnb_v[:, j * LC:(j + 1) * LC],
                             start=(j == 0), stop=(j == NDH - 1))
                    T.mark()
                    t_g1 = T.tok()
                    T.wait(P.sem["scalar"], ev["bcr"])
                    for j in range(NDH):
                        T.mm(p_bcr, wbcr_b[:, j * 68:(j + 1) * 68],
                             xnb_v[:, j * LC:(j + 1) * LC],
                             start=(j == 0), stop=(j == NDH - 1))
                    T.mark()
                    t_g2 = T.tok()
                    T.wait(P.sem["scalar"], ev["mid"])
                    for j in range(NDH):
                        T.mm(p_mid, wmid_b[:, j * 64:(j + 1) * 64],
                             xnb_v[:, j * LC:(j + 1) * LC],
                             start=(j == 0), stop=(j == NDH - 1))
                    T.mark()
                    t_g3 = T.tok()
                    S.wait(P.sem["tensor"], t_g1)
                    S.wait(st_s, evd["dt"])
                    S.op("copy", out=pr_dt[:], in_=p_dtin[:])
                    ev["dtin"] = S.tok()
                    Y.wait(P.sem["scalar"], ev["dtin"])
                    evd["dt"] = Y.dma(proj_p[0:128, cs], pr_dt[:], st_s, st_c)
                    S.wait(P.sem["tensor"], t_g2)
                    S.wait(st_s, evd["bc"])
                    S.op("copy", out=pr_bc[:], in_=p_bcrF[0:68, :])
                    ev["bcr"] = S.tok()
                    Y.wait(P.sem["scalar"], ev["bcr"])
                    evd["bc"] = Y.dma(proj_p[128:196, cs], pr_bc[:], st_s, st_c)
                    S.wait(P.sem["tensor"], t_g3)
                    S.wait(st_s, evd["md"])
                    S.op("copy", out=pr_md[:], in_=p_midF[0:64, :])
                    ev["mid"] = S.tok()
                    Y.wait(P.sem["scalar"], ev["mid"])
                    evd["md"] = Y.dma(proj_p[196:260, cs], pr_md[:], st_s, st_c)
                else:
                    V.wait(xn_s, xn_c["n"])

        t_prep = V.tok()
        T.wait(P.sem["vector"], t_prep)
        S.wait(P.sem["vector"], t_prep)
        ev = {"dtin": 0, "bcr": 0, "mid": 0, "srow": 0, "rbc": 0}
        evd = {"dt": 0, "bc": 0, "md": 0}
        ln_stats(x_fm, st0_p, ev)
        G.wait(st_s, st_c["n"])
        cc1 = G.cc("AllReduce", ALU.add, st0_p, st0_f, cc_s, cc_c)
        Y.wait(cc_s, cc1)
        Y.dma(s1_sb[:], st0_f[0:1, :], ld_s, ld_c)
        Y.dma(s2_sb[:], st0_f[1:2, :], ld_s, ld_c)
        t_st0_rb = ld_c["n"]
        V.wait(ld_s, t_st0_rb)

        ln_norm(x_fm, xn_sc, ev, True, evd)
        G.wait(st_s, st_c["n"])
        cc2 = G.cc("AllReduce", ALU.add, proj_p, proj_f, cc_s, cc_c)

        Y.wait(cc_s, cc2)
        Y.wait(P.sem["tensor"], T.tok())
        Y.wait(xn_s, xn_c["n"])
        Y.dma(stgA[:, :], proj_f[0:128, :], ld_s, ld_c)
        ld_dt = ld_c["n"]
        V.wait(ld_s, ld_dt)
        V.op("tensor_copy", out=dtin_f[:], in_=stgA[:, :])
        t_dtin_cast = V.tok()
        Y.wait(P.sem["vector"], t_dtin_cast)
        Y.dma(stgA[0:16, :], proj_f[128:144, :], ld_s, ld_c)
        Y.dma(stgA[32:48, :], proj_f[160:176, :], ld_s, ld_c)
        Y.dma(stgA[64:68, :], proj_f[192:196, :], ld_s, ld_c)
        Y.dma(stgB[:, :], proj_f[196:260, :], ld_s, ld_c)
        ld_rest = ld_c["n"]
        V.wait(ld_s, ld_rest)
        V.op("tensor_copy", out=Bfb[:], in_=stgA[0:16, :])
        V.op("tensor_copy", out=Cfb[:], in_=stgA[32:48, :])
        V.op("tensor_copy", out=lgf[:], in_=stgA[64:68, :])
        V.op("tensor_copy", out=mid0b[:], in_=stgB[:, :])
        t_casts = V.tok()
        Y.wait(P.sem["vector"], t_casts)
        Y.dma(Bb_sc[:], Bfb[:], st_s, st_c)
        Y.dma(Cb_sc[:], Cfb[:], st_s, st_c)
        bc_st = st_c["n"]

        G.wait(P.sem["vector"], t_casts)
        G.dma(wB0stg, w_B0[:], wlg_s, wlg_c)
        V.wait(wlg_s, wlg_c["n"])
        V.op("tensor_copy", out=wB0_b[:], in_=wB0stg)
        t_lgT_evac = 0
        t_gmath = 0
        t_gfm_evac = 0
        for q in range(L // 128):
            T.wait(P.sem["scalar"], S.tok() if q == 0 else t_lgT_evac)
            T.wait(P.sem["vector"], t_gmath if q > 0 else V.tok())
            T.raw("transpose", p_g, lgf[:, q * 128:(q + 1) * 128], identb[0:4, 0:4])
            T.mark()
            S.wait(P.sem["tensor"], T.tok())
            S.op("copy", out=lgT[:], in_=p_g)
            t_lgT_evac = S.tok()
            V.wait(P.sem["scalar"], t_lgT_evac)
            V.op("tensor_reduce", out=g_m1[:], in_=lgT[:], axis=AX.X, op=ALU.max)
            V.op("tensor_scalar", out=g_nm1[:], in0=g_m1[:], scalar1=-1.0,
                 scalar2=0.0, op0=ALU.mult, op1=BYP)
            V.op("tensor_scalar", out=g_ind[:, 0:4], in0=lgT[:], scalar1=g_m1[:],
                 scalar2=0.0, op0=ALU.is_ge, op1=BYP)
            V.op("scalar_tensor_tensor", out=g_t4[:, 0:4], in0=g_ind[:, 0:4],
                 scalar=-1e30, in1=lgT[:], op0=ALU.mult, op1=ALU.add)
            V.op("tensor_reduce", out=g_m2[:], in_=g_t4[:, 0:4], axis=AX.X, op=ALU.max)
            V.op("tensor_scalar", out=g_ind[:, 0:4], in0=lgT[:], scalar1=g_m2[:],
                 scalar2=0.0, op0=ALU.is_ge, op1=BYP)
            S.wait(P.sem["vector"], V.tok())
            S.op("activation", out=g_e[:, 0:4], in_=lgT[:], func=AF.Exp, bias=g_nm1[:])
            V.wait(P.sem["scalar"], S.tok())
            V.op("tensor_tensor", out=g_raw[:, 0:4], in0=g_e[:, 0:4],
                 in1=g_ind[:, 0:4], op=ALU.mult)
            V.op("tensor_reduce", out=g_s[:], in_=g_raw[:, 0:4], axis=AX.X, op=ALU.add)
            V.op("reciprocal", out=g_sr[:], in_=g_s[:])
            V.op("tensor_scalar", out=g_raw[:, 0:4], in0=g_raw[:, 0:4],
                 scalar1=g_sr[:], scalar2=0.0, op0=ALU.mult, op1=BYP)
            V.op("tensor_copy", out=g_tok4[:], in_=g_raw[:, 0:4])
            t_gmath = V.tok()
            T.wait(P.sem["vector"], t_gmath)
            T.wait(P.sem["scalar"], t_gfm_evac)
            T.raw("transpose", p_gb, g_tok4[:], identb[:])
            T.mark()
            S.wait(P.sem["tensor"], T.tok())
            S.op("copy", out=g_fm0[:, q * 128:(q + 1) * 128], in_=p_gb)
            t_gfm_evac = S.tok()
        for lc in range(NLC):
            cs = slice(lc * LC, (lc + 1) * LC)
            T.wait(P.sem["scalar"], S.tok())
            T.mm(p_bcrF[0:64, :], sel0_b[:], g_fm0[:, cs], start=True, stop=True)
            T.mark()
            S.wait(P.sem["tensor"], T.tok())
            S.op("activation", out=gbc0[:, cs], in_=p_bcrF[0:64, :], func=AF.Copy,
                 scale=4.0)
        V.wait(P.sem["scalar"], S.tok())
        V.op("tensor_tensor", out=mid0b[:], in0=mid0b[:], in1=gbc0[:], op=ALU.mult)
        t_mid0p = V.tok()
        mid0p = mid0b

        V.op("memset", state[:], 0.0)
        st_x1 = 0
        t_x1 = 0
        t_scan_done = 0
        t_scan_pp = [0, 0]
        t_dtx = [0, 0]
        for lc in range(NLC):
            cs = slice(lc * LC, (lc + 1) * LC)
            G.wait(st_s, bc_st)
            G.wait(P.sem["vector"], max(t_scan_done, t_mid0p))
            G.wait(P.sem["tensor"], T.tok())
            G.dma(BrepF_v[:].rearrange("p (n l) -> p n l", n=16),
                  Bb_sc[:, cs].unsqueeze(0).broadcast_to((128, 16, LC)), br_s, br_c)
            G.dma(wslab[0][:].rearrange("p (n l) -> p n l", n=8),
                  Cb_sc[0:8, cs].unsqueeze(0).broadcast_to((128, 8, LC)), br_s, br_c)
            G.dma(wslab[1][:].rearrange("p (n l) -> p n l", n=8),
                  Cb_sc[8:16, cs].unsqueeze(0).broadcast_to((128, 8, LC)), br_s, br_c)
            br_now = br_c["n"]
            if lc == 0:
                Y.wait(xn_s, xn_c["n"])
            for jj in range(NT):
                pp = jj % 2
                Y.wait(P.sem["vector"], V.tok())
                Y.dma(xnA[pp][:], xn_sc[jj * 128:(jj + 1) * 128, cs], ld_s, ld_c)
                ld_xn = ld_c["n"]
                T.wait(P.sem["vector"], t_dtx[pp])
                T.wait(P.sem["scalar"], S.tok())
                T.mm(p_dtr[pp][:], wdt_b[:, jj * 128:(jj + 1) * 128],
                     dtin_f[:, cs], start=True, stop=True)
                T.mark()
                t_dtr = T.tok()
                S.wait(P.sem["tensor"], t_dtr)
                S.wait(P.sem["vector"], t_dtx[pp])
                S.op("activation", out=abs_t[:], in_=p_dtr[pp][:], func=AF.Abs,
                     bias=dtb_t[:, jj:jj + 1])
                S.op("activation", out=exp_t[:], in_=abs_t[:], func=AF.Exp, scale=-1.0)
                S.op("activation", out=abs_t[:], in_=exp_t[:], func=AF.Ln,
                     bias=ones1[:])
                S.op("activation", out=relu_t[:], in_=p_dtr[pp][:], func=AF.Relu,
                     bias=dtb_t[:, jj:jj + 1])
                t_sp = S.tok()
                V.wait(P.sem["scalar"], t_sp)
                V.op("tensor_tensor", out=dt_f[pp][:], in0=relu_t[:], in1=abs_t[:],
                     op=ALU.add)
                V.wait(ld_s, ld_xn)
                V.op("tensor_tensor", out=dtx_b[pp][:], in0=dt_f[pp][:],
                     in1=xnA[pp][:], op=ALU.mult)
                t_dtx[pp] = V.tok()
                V.wait(br_s, br_now)
                for ng in range(2):
                    S.wait(P.sem["vector"], t_scan_pp[ng])
                    S.wait(P.sem["vector"], t_dtx[pp])
                    for n in range(8):
                        S.op("activation", out=dA_pp[ng][:, n * LC:(n + 1) * LC],
                             in_=dt_f[pp][:], func=AF.Exp,
                             scale=negA[:, jj * 16 + ng * 8 + n:jj * 16 + ng * 8 + n + 1])
                    t_dA = S.tok()
                    V.wait(P.sem["scalar"], t_dA)
                    V.op("tensor_tensor", out=u_v[:],
                         in0=dtx_b[pp][:].unsqueeze(1).broadcast_to((128, 8, LC)),
                         in1=BrepF_v[:, ng * 8 * LC:(ng + 1) * 8 * LC]
                         .rearrange("p (n l) -> p n l", n=8), op=ALU.mult)
                    for n in range(8):
                        cc = jj * 16 + ng * 8 + n
                        V.op("tensor_tensor_scan", out=h_v[:, n * LC:(n + 1) * LC],
                             data0=dA_pp[ng][:, n * LC:(n + 1) * LC],
                             data1=u_v[:, n * LC:(n + 1) * LC],
                             initial=state[:, cc:cc + 1], op0=ALU.mult, op1=ALU.add)
                    V.op("tensor_copy",
                         out=state[:, jj * 16 + ng * 8:jj * 16 + ng * 8 + 8],
                         in_=h_v[:].rearrange("p (n l) -> p n l", n=8)[:, :, LC - 1:LC])
                    V.op("tensor_tensor", out=hc_v[:], in0=h_v[:],
                         in1=wslab[ng][:], op=ALU.mult)
                    V.op("tensor_tensor", out=r8_t[:, 0:4 * LC], in0=hc_v[:, 0:4 * LC],
                         in1=hc_v[:, 4 * LC:8 * LC], op=ALU.add)
                    V.op("tensor_tensor", out=r8_t[:, 0:2 * LC],
                         in0=r8_t[:, 0:2 * LC], in1=r8_t[:, 2 * LC:4 * LC], op=ALU.add)
                    if ng == 0:
                        V.op("tensor_tensor", out=yh1[:], in0=r8_t[:, 0:LC],
                             in1=r8_t[:, LC:2 * LC], op=ALU.add)
                    else:
                        V.op("tensor_tensor", out=ysc_t[:], in0=r8_t[:, 0:LC],
                             in1=r8_t[:, LC:2 * LC], op=ALU.add)
                        V.op("tensor_tensor", out=yh1[:], in0=yh1[:], in1=ysc_t[:],
                             op=ALU.add)
                    t_scan_pp[ng] = V.tok()
                    t_scan_done = t_scan_pp[ng]
                V.op("scalar_tensor_tensor", out=ym_t[:], in0=xnA[pp][:],
                     scalar=Dcol[:, jj:jj + 1], in1=yh1[:], op0=ALU.mult, op1=ALU.add)
                t_scan_done = V.tok()
                T.wait(P.sem["vector"], t_mid0p)
                T.wait(P.sem["vector"], t_x1)
                T.mm(p_d0[:], wB0_b[:, jj * 128:(jj + 1) * 128], mid0p[:, cs],
                     start=True, stop=True)
                T.mark()
                t_d0 = T.tok()
                Y.wait(P.sem["vector"], t_scan_done)
                Y.wait(st_s, st_x1)
                Y.dma(xr_t[0][:], x_fm[jj * 128:(jj + 1) * 128, cs], ld_s, ld_c)
                ld_xr = ld_c["n"]
                V.wait(ld_s, ld_xr)
                V.op("tensor_tensor", out=ysc_t[:], in0=ym_t[:], in1=xr_t[0][:],
                     op=ALU.add)
                V.wait(P.sem["tensor"], t_d0)
                V.op("tensor_tensor", out=xr_t[0][:], in0=ysc_t[:], in1=p_d0[:],
                     op=ALU.add)
                t_x1 = V.tok()
                Y.wait(P.sem["vector"], t_x1)
                st_x1 = Y.dma(x1_sc[jj * 128:(jj + 1) * 128, cs], xr_t[0][:],
                              st_s, st_c)

        ev2 = {"dtin": 0, "bcr": 0, "mid": 0, "srow": S.tok(), "rbc": S.tok()}
        T.wait(P.sem["vector"], V.tok())
        S.wait(P.sem["vector"], V.tok())
        S.wait(ld_s, ld_c["n"])
        Y.wait(st_s, st_c["n"])
        ln_stats(x1_sc, st1_p, ev2)
        G.wait(st_s, st_c["n"])
        cc3 = G.cc("AllReduce", ALU.add, st1_p, st1_f, cc_s, cc_c)
        Y.wait(cc_s, cc3)
        Y.dma(s1_sb[:], st1_f[0:1, :], ld_s, ld_c)
        Y.dma(s2_sb[:], st1_f[1:2, :], ld_s, ld_c)
        V.wait(ld_s, ld_c["n"])
        ln_norm(x1_sc, xn1h, ev2, False, evd)
        G.wait(xn_s, xn_c["n"])
        cc4 = G.cc("AllGather", BYP, xn1h, xn1_f, cc_s, cc_c)

        t_cast = [0, 0]
        t_rot = [0, 0]
        t_ve = [0, 0]
        t_middma = 0
        t_qkb_pp = [0, 0]
        t_swt_pp = [0, 0]
        t_vec = 0
        p_qkv4 = [p_bcrF, p_midF, p_o, p_l]
        p_sw2 = [p_sw, p_stats]
        qk_b4 = [qk_b, qk_b2[1], qk_bx2, qk_bx3]
        qg = [0]
        t_qevac = [S.tok()] * 4
        t_mmk = [0, 0, 0, 0]
        t_rotv = [0, 0, 0, 0]
        t_rot1sw = [0, 0]
        t_swk = [0, 0]
        vg = [0]
        vout4 = [vout_b[0], vout_b[1], vout_bx2, vout_bx3]
        t_ve4 = [0, 0, 0, 0]
        t_vsw = [0, 0]
        G.wait(P.sem["vector"], V.tok())
        G.wait(P.sem["tensor"], T.tok())
        G.wait(P.sem["scalar"], S.tok())
        for lq in range(4):
            hs = slice(lq * LQ, (lq + 1) * LQ)
            Y.wait(cc_s, cc4)
            Y.wait(xn_s, xn_c["n"])
            Y.wait(P.sem["tensor"], T.tok())
            Y.wait(P.sem["scalar"], S.tok())
            Y.wait(P.sem["vector"], V.tok())
            for j in range(NDF):
                Y.dma(xnh[:, j * LQ:(j + 1) * LQ],
                      xn1_f[j * 128:(j + 1) * 128, hs], ld_s, ld_c)
            ld_xnh = ld_c["n"]
            T.wait(ld_s, ld_xnh)
            def kq_epilogue(st):
                pb, ps, mch_e, gcs_e, spill_e = st
                rpp = pb % 2
                T.wait(P.sem["scalar"], t_qevac[pb])
                T.wait(P.sem["vector"], t_rot1sw[ps])
                T.mm(p_sw2[ps][:], pswap_s[:], qk_b4[pb][:], start=True, stop=True)
                T.mark()
                t_swk[ps] = T.tok()
                V.wait(P.sem["tensor"], t_swk[ps])
                V.wait(sp_s, t_rot[rpp])
                V.op("tensor_tensor", out=rot1[rpp][:], in0=p_sw2[ps][:],
                     in1=sin_s[:, gcs_e:gcs_e + LC], op=ALU.mult)
                V.op("tensor_tensor", out=rot2[:], in0=qk_b4[pb][:],
                     in1=cos_s[:, gcs_e:gcs_e + LC], op=ALU.mult)
                V.op("tensor_tensor", out=rot1[rpp][:], in0=rot1[rpp][:],
                     in1=rot2[:], op=ALU.add)
                t_rotv[pb] = V.tok()
                t_rot1sw[ps] = t_rotv[pb]
                Y.wait(P.sem["vector"], t_rotv[pb])
                t_rot[rpp] = Y.dma(
                    spill_e[mch_e * 128:(mch_e + 1) * 128, gcs_e:gcs_e + LC],
                    rot1[rpp][:], sp_s, sp_c)

            pend = [None]
            for w_in, spill in ((w_k, k_sp), (w_q, q_sp)):
                for mch in range(16):
                    wpp = mch % 2
                    G.wait(P.sem["vector"], max(t_cast[0], t_cast[1]))
                    G.dma(wtmp[:, 0:4096].rearrange("p (c k) -> p c k", k=128),
                          w_in[:, mch * 128:(mch + 1) * 128]
                          .rearrange("(c p) k -> p c k", p=128), wlg_s, wlg_c)
                    V.wait(wlg_s, wlg_c["n"])
                    V.wait(P.sem["tensor"], T.tok())
                    V.op("tensor_copy", out=wslab[wpp][:], in_=wtmp[:, 0:4096])
                    t_cast[wpp] = V.tok()
                    for lt in range(LQ // LC):
                        gcs = lq * LQ + lt * LC
                        b4 = qg[0] % 4
                        T.wait(P.sem["vector"], t_cast[wpp])
                        T.wait(P.sem["scalar"], t_qevac[b4])
                        for kc in range(NDF):
                            T.mm(p_qkv4[b4][:], wslab[wpp][:, kc * 128:(kc + 1) * 128],
                                 xnh[:, kc * LQ + lt * LC:kc * LQ + (lt + 1) * LC],
                                 start=(kc == 0), stop=(kc == NDF - 1))
                        T.mark()
                        t_mmk[b4] = T.tok()
                        if pend[0] is not None:
                            kq_epilogue(pend[0])
                        S.wait(P.sem["tensor"], t_mmk[b4])
                        S.wait(P.sem["vector"], t_rotv[b4])
                        S.op("copy", out=qk_b4[b4][:], in_=p_qkv4[b4][:])
                        t_qevac[b4] = S.tok()
                        pend[0] = (b4, qg[0] % 2, mch, gcs, spill)
                        qg[0] += 1
            kq_epilogue(pend[0])
            pend[0] = None
            T.wait(P.sem["vector"], t_rot1sw[0])
            T.wait(P.sem["vector"], t_rot1sw[1])
            for dvc in range(16):
                wpp = dvc % 2
                G.wait(P.sem["vector"], max(t_cast[0], t_cast[1]))
                G.dma(wtmp[:, 0:4096].rearrange("p (c k) -> p c k", k=128),
                      w_v[:, dvc * 128:(dvc + 1) * 128]
                      .rearrange("(c p) k -> p c k", p=128), wlg_s, wlg_c)
                V.wait(wlg_s, wlg_c["n"])
                V.wait(P.sem["tensor"], T.tok())
                V.op("tensor_copy", out=wslab[wpp][:], in_=wtmp[:, 0:4096])
                t_cast[wpp] = V.tok()
                for lt in range(LQ // LC):
                    gcs = lq * LQ + lt * LC
                    pp = lt % 2
                    T.wait(P.sem["vector"], t_cast[wpp])
                    T.wait(P.sem["scalar"], t_qkb_pp[pp])
                    T.wait(P.sem["scalar"], t_qevac[pp])
                    for kc in range(NDF):
                        T.mm(p_qkv[pp][:], wslab[wpp][:, kc * 128:(kc + 1) * 128],
                             xnh[:, kc * LQ + lt * LC:kc * LQ + (lt + 1) * LC],
                             start=(kc == 0), stop=(kc == NDF - 1))
                    T.mark()
                    t_mm = T.tok()
                    S.wait(P.sem["tensor"], t_mm)
                    S.wait(P.sem["tensor"], t_swt_pp[pp])
                    S.op("copy", out=qk_b2[pp][:], in_=p_qkv[pp][:])
                    t_qkb_pp[pp] = S.tok()
                    for sub in range(4):
                        tb = vg[0] % 2
                        vb = vg[0] % 4
                        T.wait(P.sem["scalar"],
                               t_qkb_pp[pp] if sub == 0 else t_vsw[tb])
                        T.raw("transpose", p_sw2[tb][:, 0:64].bitcast(BF16),
                              qk_b2[pp][:, sub * 128:(sub + 1) * 128], identb[:])
                        T.mark()
                        t_tr = T.tok()
                        if sub == 3:
                            t_swt_pp[pp] = t_tr
                        S.wait(P.sem["tensor"], t_tr)
                        S.wait(sp_s, t_ve4[vb])
                        S.op("copy", out=vout4[vb][:],
                             in_=p_sw2[tb][:, 0:64].bitcast(BF16))
                        t_vsw[tb] = S.tok()
                        Y.wait(P.sem["scalar"], t_vsw[tb])
                        t_ve4[vb] = Y.dma(
                            v_sp[gcs + sub * 128:gcs + (sub + 1) * 128,
                                 dvc * 128:(dvc + 1) * 128],
                            vout4[vb][:], sp_s, sp_c)
                        vg[0] += 1
            t_qevac[0] = t_qkb_pp[0]
            t_qevac[1] = t_qkb_pp[1]
            T.wait(P.sem["vector"], t_rot1sw[1])
            for ltt in range(LQ // 128):
                T.wait(P.sem["scalar"], S.tok())
                for kc in range(NDF):
                    T.mm(p_g8, xnh[:, kc * LQ + ltt * 128:kc * LQ + (ltt + 1) * 128],
                         gT_b[:, kc * 8:(kc + 1) * 8], start=(kc == 0),
                         stop=(kc == NDF - 1))
                T.mark()
                t_gmm = T.tok()
                S.wait(P.sem["tensor"], t_gmm)
                S.op("copy", out=gtok[:], in_=p_g8)
                t_ge = S.tok()
                V.wait(P.sem["scalar"], t_ge)
                V.op("tensor_reduce", out=g_m1[:], in_=gtok[:], axis=AX.X, op=ALU.max)
                V.op("tensor_scalar", out=g_nm1[:], in0=g_m1[:], scalar1=-1.0,
                     scalar2=0.0, op0=ALU.mult, op1=BYP)
                V.op("tensor_scalar", out=g_ind[:], in0=gtok[:], scalar1=g_m1[:],
                     scalar2=0.0, op0=ALU.is_ge, op1=BYP)
                V.op("scalar_tensor_tensor", out=g_t4[:], in0=g_ind[:], scalar=-1e30,
                     in1=gtok[:], op0=ALU.mult, op1=ALU.add)
                V.op("tensor_reduce", out=g_m2[:], in_=g_t4[:], axis=AX.X, op=ALU.max)
                V.op("tensor_scalar", out=g_ind[:], in0=gtok[:], scalar1=g_m2[:],
                     scalar2=0.0, op0=ALU.is_ge, op1=BYP)
                S.wait(P.sem["vector"], V.tok())
                S.op("activation", out=g_e[:], in_=gtok[:], func=AF.Exp, bias=g_nm1[:])
                V.wait(P.sem["scalar"], S.tok())
                V.op("tensor_tensor", out=g_raw[:], in0=g_e[:], in1=g_ind[:],
                     op=ALU.mult)
                V.op("tensor_reduce", out=g_s[:], in_=g_raw[:], axis=AX.X, op=ALU.add)
                V.op("reciprocal", out=g_sr[:], in_=g_s[:])
                V.op("tensor_scalar", out=g_raw[:], in0=g_raw[:], scalar1=g_sr[:],
                     scalar2=0.0, op0=ALU.mult, op1=BYP)
                V.op("tensor_copy", out=g_tokb[:], in_=g_raw[:])
                t_gmath = V.tok()
                T.wait(P.sem["vector"], t_gmath)
                T.wait(P.sem["scalar"], S.tok())
                T.raw("transpose", p_gt8, g_tokb[:], identb[:])
                T.mark()
                S.wait(P.sem["tensor"], T.tok())
                S.op("copy",
                     out=g_fm1[:, lq * LQ + ltt * 128:lq * LQ + (ltt + 1) * 128],
                     in_=p_gt8)
            for erc in range(2):
                wpp = erc % 2
                G.wait(P.sem["vector"], max(t_cast[0], t_cast[1]))
                G.dma(wtmp[:, 0:4096].rearrange("p (c k) -> p c k", k=128),
                      w_A1[:, erc * 128:(erc + 1) * 128]
                      .rearrange("(c p) k -> p c k", p=128), wlg_s, wlg_c)
                V.wait(wlg_s, wlg_c["n"])
                V.wait(P.sem["tensor"], T.tok())
                V.op("tensor_copy", out=wslab[wpp][:], in_=wtmp[:, 0:4096])
                t_cast[wpp] = V.tok()
                for lt in range(LQ // LC):
                    pp = lt % 2
                    gcs = lq * LQ + lt * LC
                    T.wait(P.sem["vector"], t_cast[wpp])
                    T.wait(P.sem["vector"], V.tok())
                    for kc in range(NDF):
                        T.mm(p_qkv[pp][:], wslab[wpp][:, kc * 128:(kc + 1) * 128],
                             xnh[:, kc * LQ + lt * LC:kc * LQ + (lt + 1) * LC],
                             start=(kc == 0), stop=(kc == NDF - 1))
                    T.mark()
                    t_mm = T.tok()
                    T.wait(P.sem["scalar"], S.tok())
                    T.mm(p_sw[:], sel1_s[:, erc * 128:(erc + 1) * 128],
                         g_fm1[:, gcs:gcs + LC], start=True, stop=True)
                    T.mark()
                    t_gbc = T.tok()
                    S.wait(P.sem["tensor"], t_gbc)
                    S.op("copy", out=gbc_t[:], in_=p_sw[:])
                    t_gbce = S.tok()
                    V.wait(P.sem["tensor"], t_mm)
                    V.wait(P.sem["scalar"], t_gbce)
                    V.wait(st_s, t_middma)
                    V.op("tensor_tensor", out=mid1_t[:], in0=p_qkv[pp][:],
                         in1=gbc_t[:], op=ALU.mult)
                    t_mid = V.tok()
                    Y.wait(P.sem["vector"], t_mid)
                    t_middma = Y.dma(midh[erc * 128:(erc + 1) * 128, gcs:gcs + LC],
                                     mid1_t[:], st_s, st_c)
        G.wait(st_s, st_c["n"])
        cc5 = G.cc("AllGather", BYP, midh, mid_f, cc_s, cc_c)

        p_o2 = [p_o, p_bcrF]
        p_l2 = [p_l, p_midF]
        t_oh_dma = 0
        t_ohv = [0, 0]
        t_lbe = [0, 0]
        t_mbc_prev = 0
        t_lsum = [0, 0]
        t_lmm_prev = [0]
        t_headT = [T.tok(), T.tok()]
        g3 = max(T.tok(), 1)
        S.wait(P.sem["tensor"], g3)
        V.wait(P.sem["tensor"], g3)
        ld_h2 = [0, 0]
        for h in range(16):
            hb = h % 2
            if h < 2:
                Y.wait(sp_s, sp_c["n"])
                Y.wait(P.sem["scalar"], S.tok())
                Y.wait(P.sem["vector"], V.tok())
            Y.wait(P.sem["tensor"], t_headT[hb])
            Y.dma(Kh_b[hb][:], k_sp[h * 128:(h + 1) * 128, :], ld_s, ld_c)
            Y.dma(Qh_b[hb][:], q_sp[h * 128:(h + 1) * 128, :], ld_s, ld_c)
            Y.dma(Vsb_b[hb][:].rearrange("p (c k) -> p c k", k=128),
                  v_sp[:, h * 128:(h + 1) * 128].rearrange("(c p) k -> p c k", p=128),
                  ld_s, ld_c)
            ld_h2[hb] = ld_c["n"]
            T.wait(ld_s, ld_h2[hb])
            Kh, Qh, Vsb = Kh_b[hb], Qh_b[hb], Vsb_b[hb]
            for qb in range(NLC):
                qp = qb % 2
                nkt = 4 * (qb + 1)
                t_exp = [0, 0]
                t_avm = [0, 0]
                t_msk = [0, 0]
                for kt in range(nkt + 1):
                    if kt < nkt:
                        pp = kt % 2
                        T.wait(P.sem["scalar"], t_exp[pp])
                        T.mm(p_s[:] if pp == 0 else p_s2nd[:],
                             Kh[:, kt * 128:(kt + 1) * 128],
                             Qh[:, qb * LC:(qb + 1) * LC], start=True, stop=True)
                        T.mark()
                        t_sc = T.tok()
                    if kt >= 1:
                        km = kt - 1
                        pk = km % 2
                        if km >= nkt - 4:
                            T.wait(P.sem["vector"], t_msk[pk])
                        else:
                            T.wait(P.sem["scalar"], t_exp[pk])
                        if km == 0:
                            T.wait(P.sem["vector"], t_ohv[qp])
                            T.wait(P.sem["scalar"], t_lbe[qp])
                        T.mm(p_o2[qp][:], Vsb[:, km * 128:(km + 1) * 128],
                             PT[pk][:], start=(km == 0), stop=(km == nkt - 1))
                        T.mark()
                        t_avm[pk] = T.tok()
                    if kt < nkt:
                        S.wait(P.sem["tensor"], t_sc)
                        S.wait(P.sem["tensor"], t_avm[pp])
                        S.wait(P.sem["vector"], t_lsum[pp])
                        if kt >= nkt - 4:
                            S.wait(P.sem["vector"], t_msk[pp])
                        S.op("activation", out=PT[pp][:],
                             in_=(p_s[:] if pp == 0 else p_s2nd[:]), func=AF.Exp,
                             scale=0.08838834764831845)
                        t_exp[pp] = S.tok()
                        if kt >= nkt - 4:
                            mj = kt - (nkt - 4)
                            V.wait(P.sem["scalar"], t_exp[pp])
                            V.op("tensor_tensor", out=PT[pp][:], in0=PT[pp][:],
                                 in1=mask_s[:, mj * 512:(mj + 1) * 512], op=ALU.mult)
                            t_msk[pp] = V.tok()
                        else:
                            V.wait(P.sem["scalar"], t_exp[pp])
                        if kt == 0:
                            V.wait(P.sem["tensor"], t_lmm_prev[0])
                            V.op("tensor_copy", out=lsum[:], in_=PT[pp][:])
                        else:
                            V.op("tensor_tensor", out=lsum[:], in0=lsum[:],
                                 in1=PT[pp][:], op=ALU.add)
                        t_lsum[pp] = V.tok()
                t_av = T.tok()
                V.op("tensor_copy", out=qk_bx2[:], in_=lsum[:])
                t_lsb = V.tok()
                T.wait(P.sem["vector"], t_lsb)
                T.wait(P.sem["scalar"], t_lbe[qp])
                T.mm(p_l2[qp][0:1, :], onesc[:], qk_bx2[:], start=True, stop=True)
                T.mark()
                t_lmm_prev[0] = T.tok()
                S.wait(P.sem["tensor"], t_lmm_prev[0])
                S.wait(P.sem["vector"], t_ohv[1 - qp])
                S.op("copy", out=lrec_b[:], in_=p_l2[qp][0:1, :])
                t_lbe[qp] = S.tok()
                T.wait(P.sem["scalar"], t_lbe[qp])
                T.wait(P.sem["scalar"], t_mbc_prev)
                T.mm(p_bc[:], onesr[:], lrec_b[:], start=True, stop=True)
                T.mark()
                t_lb = T.tok()
                S.wait(P.sem["tensor"], t_lb)
                S.op("copy", out=m_bc[:], in_=p_bc[:])
                t_mbc_prev = S.tok()
                V.wait(P.sem["scalar"], t_mbc_prev)
                V.op("reciprocal", out=r_bc[:], in_=m_bc[:])
                V.wait(st_s, t_oh_dma)
                V.op("tensor_tensor", out=oh_t[:], in0=p_o2[qp][:], in1=r_bc[:],
                     op=ALU.mult)
                t_ohv[qp] = V.tok()
                Y.wait(P.sem["vector"], t_ohv[qp])
                t_oh_dma = Y.dma(ohh[h * 128:(h + 1) * 128, qb * LC:(qb + 1) * LC],
                                 oh_t[:], st_s, st_c)
            t_headT[hb] = T.tok()
        G.wait(st_s, st_c["n"])
        cc6 = G.cc("AllGather", BYP, ohh, oh_f, cc_s, cc_c)

        G.wait(P.sem["vector"], V.tok())
        G.wait(P.sem["tensor"], T.tok())
        G.wait(P.sem["scalar"], S.tok())
        T.wait(P.sem["scalar"], S.tok())
        for lq in range(4):
            hs = slice(lq * LQ, (lq + 1) * LQ)
            Y.wait(cc_s, cc6)
            Y.wait(st_s, st_c["n"])
            Y.wait(P.sem["tensor"], T.tok())
            Y.wait(P.sem["scalar"], S.tok())
            Y.wait(P.sem["vector"], V.tok())
            for j in range(NDF):
                Y.dma(oh_res[:, j * LQ:(j + 1) * LQ],
                      oh_f[j * 128:(j + 1) * 128, hs], ld_s, ld_c)
            for j in range(4):
                Y.dma(mid_res[:, j * LQ:(j + 1) * LQ],
                      mid_f[j * 128:(j + 1) * 128, hs], ld_s, ld_c)
            ld_all = ld_c["n"]
            T.wait(ld_s, ld_all)
            t_x2 = [0, 0]
            st_t2 = [0, 0]
            for m in range(16):
                wpp = m % 2
                G.wait(P.sem["vector"], max(t_cast[0], t_cast[1]))
                G.dma(wtmp[:, 0:4096].rearrange("p (c k) -> p c k", k=128),
                      w_o[:, m * 128:(m + 1) * 128]
                      .rearrange("(c p) k -> p c k", p=128), wlg_s, wlg_c)
                G.dma(rows_b2[:].rearrange("p (c k) -> p c k", k=128),
                      w_B1[:, m * 128:(m + 1) * 128]
                      .rearrange("(c p) k -> p c k", p=128), wlg_s, wlg_c)
                V.wait(wlg_s, wlg_c["n"])
                V.wait(P.sem["tensor"], T.tok())
                V.op("tensor_copy", out=wslab[wpp][:], in_=wtmp[:, 0:4096])
                V.op("tensor_copy", out=qk_b[:], in_=rows_b2[:])
                t_cast[wpp] = V.tok()
                x1buf = [xr_t[0], x1t_[0]]
                for lt in range(LQ // LC):
                    pp = lt % 2
                    gcs = lq * LQ + lt * LC
                    Y.wait(P.sem["vector"], t_x2[pp])
                    Y.dma(x1buf[pp][:], x1_sc[m * 128:(m + 1) * 128, gcs:gcs + LC],
                          ld_s, ld_c)
                    ld_x1 = ld_c["n"]
                    T.wait(P.sem["vector"], t_cast[wpp])
                    T.wait(P.sem["vector"], t_x2[pp])
                    for kc in range(NDF):
                        T.mm(p_y[pp][:], wslab[wpp][:, kc * 128:(kc + 1) * 128],
                             oh_res[:, kc * LQ + lt * LC:kc * LQ + (lt + 1) * LC],
                             start=(kc == 0), stop=False)
                    for ec in range(4):
                        T.mm(p_y[pp][:], qk_b[:, ec * 128:(ec + 1) * 128],
                             mid_res[:, ec * LQ + lt * LC:ec * LQ + (lt + 1) * LC],
                             start=False, stop=(ec == 3))
                    T.mark()
                    t_mm = T.tok()
                    V.wait(P.sem["tensor"], t_mm)
                    V.wait(ld_s, ld_x1)
                    V.wait(st_s, st_t2[pp])
                    V.op("tensor_tensor", out=x2t[pp][:], in0=p_y[pp][:],
                         in1=x1buf[pp][:], op=ALU.add)
                    t_x2[pp] = V.tok()
                    Y.wait(P.sem["vector"], t_x2[pp])
                    st_t2[pp] = Y.dma(x2_out[m * 128:(m + 1) * 128, gcs:gcs + LC],
                                      x2t[pp][:], st_s, st_c)
        Y.wait(st_s, st_c["n"])
    with nc.Block() as blk:
        P.finalize(blk)
    return nc


def _prep_fused_inputs(inputs):
    x = inputs["L0_attn_x"]
    xpw = inputs["L0_attn_x_proj_w"]
    dtw = inputs["L0_attn_dt_proj_w"]
    dtb = inputs["L0_attn_dt_proj_b"]
    Alog = inputs["L0_attn_A_log"]
    Dv = inputs["L0_attn_D"]
    rw = inputs["L0_ffn_router_w"]
    A0 = inputs["L0_ffn_A"]
    B0 = inputs["L0_ffn_B"]
    wq = inputs["L1_attn_wq"]
    wk = inputs["L1_attn_wk"]
    wv = inputs["L1_attn_wv"]
    wo = inputs["L1_attn_wo"]
    gw = inputs["L1_ffn_gate_w"]
    lA = inputs["L1_ffn_lora_A"]
    lB = inputs["L1_ffn_lora_B"]

    w_dtin = xpw[:128].T
    w_bcr = np.zeros((DM, 68), np.float32)
    w_bcr[:, 0:16] = xpw[128:144].T
    w_bcr[:, 32:48] = xpw[144:160].T
    w_bcr[:, 64:68] = rw.T
    w_mid = A0.reshape(64, DM).T
    sel0v = np.zeros((4, 64), np.float32)
    for e in range(4):
        sel0v[e, e * 16:(e + 1) * 16] = 1.0
    B0T = B0.transpose(0, 2, 1).reshape(64, DM)
    t = np.arange(L, dtype=np.float64)
    cosr = np.cos(t).astype(bf16).astype(np.float32)
    sinr = np.sin(t).astype(bf16).astype(np.float32)
    cosbv = np.ascontiguousarray(np.broadcast_to(cosr, (128, L))).astype(bf16)
    sinbv = np.ascontiguousarray(np.broadcast_to(sinr, (128, L))).astype(bf16)
    mk = np.zeros((128, 4, 512), np.float32)
    for mj in range(4):
        for k in range(128):
            mk[k, mj, mj * 128 + k:] = 1.0
    masksv = mk.reshape(128, 2048).astype(bf16)
    psw = np.zeros((128, 128), np.float32)
    for i in range(64):
        psw[2 * i + 1, 2 * i] = -1.0
        psw[2 * i, 2 * i + 1] = 1.0
    psw = psw.astype(bf16)
    identbv = np.eye(128, dtype=np.float32).astype(bf16)
    sel1v = np.zeros((8, 512), np.float32)
    for e in range(8):
        sel1v[e, e * 64:(e + 1) * 64] = 1.0
    sel1v = sel1v.astype(bf16)
    wA1 = lA.reshape(512, DM).T
    woT = wo.T
    wB1 = lB.transpose(0, 2, 1).reshape(512, DM)
    dtwT = dtw.T

    maps = []
    for c in range(8):
        b, hp = c // 2, c % 2
        hs = slice(hp * DH, (hp + 1) * DH)
        rs = slice(hp * 256, (hp + 1) * 256)
        m = {
            "x_fm": np.ascontiguousarray(x[b].T[hs]),
            "w_dtin": np.ascontiguousarray(w_dtin[hs]),
            "w_bcr": np.ascontiguousarray(w_bcr[hs]),
            "w_mid": np.ascontiguousarray(w_mid[hs]),
            "w_dt": np.ascontiguousarray(dtwT[:, hs]),
            "dt_bias": np.ascontiguousarray(dtb[hs].reshape(NT, 128).T),
            "A_log_t": np.ascontiguousarray(
                Alog[hs].reshape(NT, 128, 16).transpose(1, 0, 2)
                .reshape(128, NT * 16)),
            "D_t": np.ascontiguousarray(Dv[hs].reshape(NT, 128).T),
            "w_B0": np.ascontiguousarray(B0T[:, hs]),
            "sel0": sel0v,
            "w_q": np.ascontiguousarray(wq.T[:, hs]),
            "w_k": np.ascontiguousarray(wk.T[:, hs]),
            "w_v": np.ascontiguousarray(wv.T[:, hs]),
            "w_g": np.ascontiguousarray(gw.T),
            "w_A1": np.ascontiguousarray(wA1[:, rs]),
            "w_o": np.ascontiguousarray(woT[:, hs]),
            "w_B1": np.ascontiguousarray(wB1[:, hs]),
            "cosb": cosbv,
            "sinb": sinbv,
            "masks": masksv,
            "pswap": psw,
            "identbi": identbv,
            "sel1": np.ascontiguousarray(sel1v[:, rs]),
        }
        maps.append(m)
    return maps


def kernel(**inputs):
    maps = _prep_fused_inputs(inputs)
    nc = build_fused()
    res = run_bass_kernel_spmd(nc, maps, core_ids=list(range(8)))
    out = np.empty((NB, L, DM), np.float32)
    for c in range(8):
        b, hp = c // 2, c % 2
        x2 = np.asarray(res.results[c]["x2_out"])
        out[b, :, hp * DH:(hp + 1) * DH] = x2.T
    return out


# version 13
# speedup vs baseline: 1.9322x; 1.0263x over previous
import numpy as np
import ml_dtypes
import concourse.bass as bass
import concourse.mybir as mybir
from concourse.bass_utils import run_bass_kernel_spmd

F32 = mybir.dt.float32
BF16 = mybir.dt.bfloat16
AF = mybir.ActivationFunctionType
ALU = mybir.AluOpType
AX = mybir.AxisListType
bf16 = ml_dtypes.bfloat16

DM = 4096
NB = 4
L = 4096
DH = 2048
LC = 512
NLC = L // LC
NDF = DM // 128
NDH = DH // 128
NT = NDH
LQ = 1024
PAIRS = [[0, 1], [2, 3], [4, 5], [6, 7]]

BYP = ALU.bypass


class SEng:

    def __init__(self, sem, cnt):
        self.s = sem
        self.c = cnt
        self.q = []
        self.cell = {}

    def op(self, name, *a, **k):
        n0 = self.c["n"]
        s_ = self.s

        def run(e, n0=n0, name=name, a=a, k=k):
            if n0 > 0:
                e.wait_ge(s_, n0)
            getattr(e, name)(*a, **k).then_inc(s_, 1)
        self.q.append(run)
        self.c["n"] += 1
        return self.c["n"]

    def mm(self, out, lhsT, rhs, start, stop, **k):
        cell = self.cell

        def run(e, out=out, lhsT=lhsT, rhs=rhs, start=start, stop=stop, k=k):
            cell["last"] = e.matmul(out, lhsT, rhs, start=start, stop=stop, **k)
        self.q.append(run)

    def raw(self, name, *a, **k):
        cell = self.cell

        def run(e, name=name, a=a, k=k):
            cell["last"] = getattr(e, name)(*a, **k)
        self.q.append(run)

    def mark(self):
        cell = self.cell
        s_ = self.s

        def run(e):
            cell["last"].then_inc(s_, 1)
        self.q.append(run)
        self.c["n"] += 1
        return self.c["n"]

    def wait(self, sem, v):
        if v > 0:
            self.q.append(lambda e, sem=sem, v=v: e.wait_ge(sem, v))

    def dma(self, out, in_, sem, cnt, **k):
        def run(e, out=out, in_=in_, sem=sem, k=k):
            e.dma_start(out=out, in_=in_, **k).then_inc(sem, 16)
        self.q.append(run)
        cnt["n"] += 16
        return cnt["n"]

    def cc(self, kind, op, in_t, out_t, sem, cnt):
        def run(e, kind=kind, op=op, in_t=in_t, out_t=out_t, sem=sem):
            e.collective_compute(
                kind, op, replica_groups=PAIRS,
                ins=[in_t.ap().opt()], outs=[out_t.ap().opt()]).then_inc(sem, 1)
        self.q.append(run)
        cnt["n"] += 1
        return cnt["n"]

    def tok(self):
        return self.c["n"]


class Prog:
    ENGS = ("vector", "scalar", "tensor", "gpsimd", "sync")

    def __init__(self, nc):
        self.nc = nc
        self.sem = {e: nc.alloc_semaphore(f"es_{e}") for e in self.ENGS}
        self.cnt = {e: {"n": 0} for e in self.ENGS}
        self.dsem = {}
        self.dcnt = {}
        self.eng = {e: SEng(self.sem[e], self.cnt[e]) for e in self.ENGS}

    def dma_sem(self, name):
        if name not in self.dsem:
            self.dsem[name] = self.nc.alloc_semaphore(f"ds_{name}")
            self.dcnt[name] = {"n": 0}
        return self.dsem[name], self.dcnt[name]

    def finalize(self, blk):
        for name in self.ENGS:
            se = self.eng[name]

            def deco(raw, se=se):
                for fn in se.q:
                    fn(raw)
            getattr(blk, name)(deco)


def build_fused():
    nc = bass.Bass("TRN2", target_bir_lowering=False)

    x_fm = nc.dram_tensor("x_fm", [DH, L], F32, kind="ExternalInput")
    w_dtin = nc.dram_tensor("w_dtin", [DH, 128], F32, kind="ExternalInput")
    w_bcr = nc.dram_tensor("w_bcr", [DH, 68], F32, kind="ExternalInput")
    w_mid = nc.dram_tensor("w_mid", [DH, 64], F32, kind="ExternalInput")
    w_dt = nc.dram_tensor("w_dt", [128, DH], F32, kind="ExternalInput")
    dt_bias = nc.dram_tensor("dt_bias", [128, NT], F32, kind="ExternalInput")
    A_log_t = nc.dram_tensor("A_log_t", [128, NT * 16], F32, kind="ExternalInput")
    D_t = nc.dram_tensor("D_t", [128, NT], F32, kind="ExternalInput")
    w_B0 = nc.dram_tensor("w_B0", [64, DH], F32, kind="ExternalInput")
    sel0 = nc.dram_tensor("sel0", [4, 64], F32, kind="ExternalInput")
    w_q = nc.dram_tensor("w_q", [DM, DH], F32, kind="ExternalInput")
    w_k = nc.dram_tensor("w_k", [DM, DH], F32, kind="ExternalInput")
    w_v = nc.dram_tensor("w_v", [DM, DH], F32, kind="ExternalInput")
    w_g = nc.dram_tensor("w_g", [DM, 8], F32, kind="ExternalInput")
    w_A1 = nc.dram_tensor("w_A1", [DM, 256], F32, kind="ExternalInput")
    w_o = nc.dram_tensor("w_o", [DM, DH], F32, kind="ExternalInput")
    w_B1 = nc.dram_tensor("w_B1", [512, DH], F32, kind="ExternalInput")
    cosb = nc.dram_tensor("cosb", [128, L], BF16, kind="ExternalInput")
    sinb = nc.dram_tensor("sinb", [128, L], BF16, kind="ExternalInput")
    masks = nc.dram_tensor("masks", [128, 4 * 512], BF16, kind="ExternalInput")
    pswap = nc.dram_tensor("pswap", [128, 128], BF16, kind="ExternalInput")
    identbi = nc.dram_tensor("identbi", [128, 128], BF16, kind="ExternalInput")
    sel1 = nc.dram_tensor("sel1", [8, 256], BF16, kind="ExternalInput")
    x2_out = nc.dram_tensor("x2_out", [DH, L], F32, kind="ExternalOutput")

    st0_p = nc.dram_tensor("st0_p", [2, L], F32, kind="Internal")
    st0_f = nc.dram_tensor("st0_f", [2, L], F32, kind="Internal")
    st1_p = nc.dram_tensor("st1_p", [2, L], F32, kind="Internal")
    st1_f = nc.dram_tensor("st1_f", [2, L], F32, kind="Internal")
    proj_p = nc.dram_tensor("proj_p", [260, L], F32, kind="Internal")
    proj_f = nc.dram_tensor("proj_f", [260, L], F32, kind="Internal")
    xn_sc = nc.dram_tensor("xn_sc", [DH, L], BF16, kind="Internal")
    Bb_sc = nc.dram_tensor("Bb_sc", [16, L], BF16, kind="Internal")
    Cb_sc = nc.dram_tensor("Cb_sc", [16, L], BF16, kind="Internal")
    x1_sc = nc.dram_tensor("x1_sc", [DH, L], F32, kind="Internal")
    xn1h = nc.dram_tensor("xn1h", [DH, L], BF16, kind="Internal")
    xn1_f = nc.dram_tensor("xn1_f", [DM, L], BF16, kind="Internal")
    q_sp = nc.dram_tensor("q_sp", [DH, L], BF16, kind="Internal")
    k_sp = nc.dram_tensor("k_sp", [DH, L], BF16, kind="Internal")
    v_sp = nc.dram_tensor("v_sp", [L, DH], BF16, kind="Internal")
    ohh = nc.dram_tensor("ohh", [DH, L], BF16, kind="Internal")
    oh_f = nc.dram_tensor("oh_f", [DM, L], BF16, kind="Internal")
    midh = nc.dram_tensor("midh", [256, L], BF16, kind="Internal")
    mid_f = nc.dram_tensor("mid_f", [512, L], BF16, kind="Internal")

    sb = nc.alloc_sbuf_tensor
    MEGA = sb("MEGA", [128, 16384], F32)
    xnb_v = MEGA[:, 0:4096].bitcast(BF16)
    xsq_v = MEGA[:, 8192:12288].bitcast(BF16)
    stgA = MEGA[:, 12288:16384]
    stgB = MEGA[0:64, 4096:8192]
    s1_sb = MEGA[0:1, 4096:8192]
    s2_sb = MEGA[32:33, 4096:8192]
    g_fm0 = MEGA[0:4, 0:2048].bitcast(BF16)
    gbc0 = MEGA[64:128, 0:2048].bitcast(BF16)
    wB0stg = MEGA[0:64, 4096:6144]
    dA_pp = [MEGA[:, 0:4096], MEGA[:, 4096:8192]]
    u_v = MEGA[:, 8192:10240].bitcast(BF16)
    h_v = MEGA[:, 10240:12288].bitcast(BF16)
    hc_v = u_v
    BrepF_v = MEGA[:, 12288:16384].bitcast(BF16)
    xnh = MEGA[:].bitcast(BF16)
    Kh_b = [MEGA[:, 0:2048].bitcast(BF16), MEGA[:, 8192:10240].bitcast(BF16)]
    Qh_b = [MEGA[:, 2048:4096].bitcast(BF16), MEGA[:, 10240:12288].bitcast(BF16)]
    Vsb_b = [MEGA[:, 4096:6144].bitcast(BF16), MEGA[:, 12288:14336].bitcast(BF16)]
    PT = [MEGA[:, 6144 + i * 256:6400 + i * 256].bitcast(BF16) for i in range(2)]
    lbc = MEGA[:, 6656:6912].bitcast(BF16)
    oh_t = MEGA[:, 6912:7168].bitcast(BF16)
    oh_res = MEGA[:].bitcast(BF16)

    wtmp = sb("wtmp", [128, 4096], F32)
    Bfb = wtmp[0:16, 0:2048].bitcast(BF16)
    Cfb = wtmp[32:48, 0:2048].bitcast(BF16)
    mid0b = wtmp[64:128, 0:2048].bitcast(BF16)
    lgf = wtmp[0:4, 2048:4096].bitcast(BF16)
    wB0_b = wtmp[64:128, 2048:3072].bitcast(BF16)
    wdtin_b = sb("wdtin_b", [128, NDH * 128], BF16)
    wbcr_b = sb("wbcr_b", [128, NDH * 68], BF16)
    wmid_b = sb("wmid_b", [128, NDH * 64], BF16)
    mid_res = sb("mid_res", [128, 8192], BF16)
    wdt_b = mid_res[:, 0:2048]
    dtin_f = mid_res[:, 4096:8192]
    wslab = [sb(f"wslab{i}", [128, 4096], BF16) for i in range(2)]
    sel0_b = sb("sel0_b", [4, 64], BF16)
    identb = sb("identb", [128, 128], BF16)
    dtb_t = sb("dtb_t", [128, NT], F32)
    Alog_s = sb("Alog_s", [128, NT * 16], F32)
    negA = sb("negA", [128, NT * 16], F32)
    Dcol = sb("Dcol", [128, NT], F32)
    gT_b = sb("gT_b", [128, 8 * NDF], BF16)
    g_fm1 = sb("g_fm1", [8, L], BF16)
    cos_s = sb("cos_s", [128, L], BF16)
    sin_s = sb("sin_s", [128, L], BF16)
    mask_s = sb("mask_s", [128, 4 * 512], BF16)
    pswap_s = sb("pswap_s", [128, 128], BF16)
    sel1_s = sb("sel1_s", [8, 256], BF16)
    pr_dt = sb("pr_dt", [128, LC], F32)
    pr_bc = sb("pr_bc", [68, LC], F32)
    pr_md = sb("pr_md", [64, LC], F32)

    onesc = sb("onesc", [128, 1], BF16)
    onesf = sb("onesf", [128, 1], F32)
    onesr = sb("onesr", [1, 128], BF16)
    ones1 = sb("ones1", [128, 1], F32)
    epsc = sb("epsc", [1, 1], F32)
    rows_a = sb("rows_a", [128, LC], F32)
    rows_b2 = sb("rows_b2", [128, LC], F32)
    rows_c = sb("rows_c", [1, LC], F32)
    rows_d = sb("rows_d", [1, LC], F32)
    mrow = rows_c[0:1, :]
    qrow = rows_d[0:1, :]
    m2row = rows_a[0:1, :]
    vrow = rows_b2[0:1, :]
    sdrow = rows_d[0:1, :]
    rrow = rows_a[0:1, :]
    mrow_b = sb("mrow_b", [1, LC], BF16)
    rrow_b = sb("rrow_b", [1, LC], BF16)
    m_bc = sb("m_bc", [128, LC], F32)
    r_bc = sb("r_bc", [128, LC], F32)
    xn_t = rows_a
    abs_t = sb("abs_t", [128, LC], F32)
    exp_t = rows_a
    relu_t = rows_b2
    r8_t = wdtin_b
    ysc_t = m_bc
    ym_t = r_bc
    yh1 = sb("yh1", [128, LC], F32)
    dt_f = [sb(f"dt_f{i}", [128, LC], F32) for i in range(2)]
    dtx_b = [sb(f"dtx_b{i}", [128, LC], BF16) for i in range(2)]
    xr_t = [sb("xr_t0", [128, LC], F32)]
    x1t_ = [sb("x1t_0", [128, LC], F32)]
    xnA = [sb(f"xnA{i}", [128, LC], BF16) for i in range(2)]
    state = sb("state", [128, NT * 16], F32)
    lgT = sb("lgT", [128, 4], F32)
    g_m1 = sb("g_m1", [128, 1], F32)
    g_nm1 = sb("g_nm1", [128, 1], F32)
    g_t4 = sb("g_t4", [128, 8], F32)
    g_m2 = sb("g_m2", [128, 1], F32)
    g_e = sb("g_e", [128, 8], F32)
    g_ind = sb("g_ind", [128, 8], F32)
    g_raw = sb("g_raw", [128, 8], F32)
    g_s = sb("g_s", [128, 1], F32)
    g_sr = sb("g_sr", [128, 1], F32)
    g_tok4 = sb("g_tok4", [128, 4], BF16)
    gtok = sb("gtok", [128, 8], F32)
    g_tokb = sb("g_tokb", [128, 8], BF16)
    gbc_t = sb("gbc_t", [128, LC], BF16)
    qk_b = sb("qk_b", [128, LC], BF16)
    qk_b2 = [qk_b, sb("qk_b_1", [128, LC], BF16)]
    qk_bx2 = sb("qk_bx2", [128, LC], BF16)
    qk_bx3 = sb("qk_bx3", [128, LC], BF16)
    rot1 = [sb(f"rot1_{i}", [128, LC], BF16) for i in range(2)]
    rot2 = sb("rot2", [128, LC], BF16)
    vout_b = [sb(f"vout_b{i}", [128, 128], BF16) for i in range(2)]
    vout_bx2 = sb("vout_bx2", [128, 128], BF16)
    vout_bx3 = sb("vout_bx3", [128, 128], BF16)
    mid1_t = sb("mid1_t", [128, LC], BF16)
    lrow = rows_c[0:1, :]
    lrec = rows_d[0:1, :]
    lrec_b = sb("lrec_b", [1, LC], BF16)
    x2t = [sb("x2t0", [128, LC], F32), yh1]

    ps = nc.alloc_psum_tensor
    p_stats = ps("p_stats", [128, LC], F32)
    p_s1 = p_stats[0:1, :]
    p_s2 = p_stats[32:33, :]
    p_bc = ps("p_bc", [128, LC], F32)
    p_dtin = ps("p_dtin", [128, LC], F32)
    p_bcrF = ps("p_bcrF", [128, LC], F32)
    p_bcr = p_bcrF[0:68, :]
    p_midF = ps("p_midF", [128, LC], F32)
    p_mid = p_midF[0:64, :]
    p_dtr = [p_bcrF, p_midF]
    p_d0 = p_dtin
    p_g = p_bc[:, 0:2].bitcast(BF16)
    p_gb = p_bc[0:4, 64:128].bitcast(BF16)
    p_qkv = [p_bcrF, p_midF]
    p_sw = ps("p_sw", [128, LC], F32)
    p_o = ps("p_o", [128, LC], F32)
    p_l = ps("p_l", [128, LC], F32)
    p_g8 = p_stats[:, 0:8]
    p_gt8 = p_bc[0:8, 0:64].bitcast(BF16)
    p_s = p_sw
    p_s2nd = p_dtin
    p_y = [p_bcrF, p_midF]

    P = Prog(nc)
    V, S, T, G, Y = (P.eng["vector"], P.eng["scalar"], P.eng["tensor"],
                     P.eng["gpsimd"], P.eng["sync"])
    if True:
        ld_s, ld_c = P.dma_sem("ld")
        wl_s, wl_c = P.dma_sem("wl")
        wlg_s, wlg_c = P.dma_sem("wlg")
        st_s, st_c = P.dma_sem("st")
        br_s, br_c = P.dma_sem("br")
        xn_s, xn_c = P.dma_sem("xn")
        sp_s, sp_c = P.dma_sem("sp")
        cc_s, cc_c = P.dma_sem("cc")

        Y.dma(dtb_t[:], dt_bias[:], wl_s, wl_c)
        Y.dma(Alog_s[:], A_log_t[:], wl_s, wl_c)
        Y.dma(Dcol[:], D_t[:], wl_s, wl_c)
        Y.dma(m_bc[0:4, 0:64], sel0[:], wl_s, wl_c)
        Y.dma(cos_s[:], cosb[:], wl_s, wl_c)
        Y.dma(sin_s[:], sinb[:], wl_s, wl_c)
        Y.dma(mask_s[:], masks[:], wl_s, wl_c)
        Y.dma(pswap_s[:], pswap[:], wl_s, wl_c)
        Y.dma(identb[:], identbi[:], wl_s, wl_c)
        Y.dma(sel1_s[:], sel1[:], wl_s, wl_c)
        G.dma(wtmp[:, 0:NDH * 128].rearrange("p (c k) -> p c k", k=128),
              w_dtin[:].rearrange("(c p) k -> p c k", p=128), wlg_s, wlg_c)
        V.wait(wl_s, wl_c["n"])
        V.wait(wlg_s, wlg_c["n"])
        V.op("memset", onesc[:], 1.0)
        V.op("memset", onesf[:], 1.0)
        V.op("memset", onesr[:], 1.0)
        V.op("memset", ones1[:], 1.0)
        V.op("memset", epsc[:], 1e-5)
        V.op("tensor_copy", out=sel0_b[:], in_=m_bc[0:4, 0:64])
        V.op("tensor_copy", out=wdtin_b[:], in_=wtmp[:, 0:NDH * 128])
        G.wait(P.sem["vector"], V.tok())
        G.dma(wtmp[:, 0:NDH * 68].rearrange("p (c k) -> p c k", k=68),
              w_bcr[:].rearrange("(c p) k -> p c k", p=128), wlg_s, wlg_c)
        V.wait(wlg_s, wlg_c["n"])
        V.op("tensor_copy", out=wbcr_b[:], in_=wtmp[:, 0:NDH * 68])
        G.wait(P.sem["vector"], V.tok())
        G.dma(wtmp[:, 0:NDH * 64].rearrange("p (c k) -> p c k", k=64),
              w_mid[:].rearrange("(c p) k -> p c k", p=128), wlg_s, wlg_c)
        V.wait(wlg_s, wlg_c["n"])
        V.op("tensor_copy", out=wmid_b[:], in_=wtmp[:, 0:NDH * 64])
        G.wait(P.sem["vector"], V.tok())
        G.dma(wtmp[:, 0:DH], w_dt[:], wlg_s, wlg_c)
        V.wait(wlg_s, wlg_c["n"])
        V.op("tensor_copy", out=wdt_b[:], in_=wtmp[:, 0:DH])
        G.wait(P.sem["vector"], V.tok())
        G.dma(wtmp[:, 0:8 * NDF].rearrange("p (c k) -> p c k", k=8),
              w_g[:].rearrange("(c p) k -> p c k", p=128), wlg_s, wlg_c)
        V.wait(wlg_s, wlg_c["n"])
        V.op("tensor_copy", out=gT_b[:], in_=wtmp[:, 0:8 * NDF])
        S.wait(wl_s, wl_c["n"])
        S.op("activation", out=negA[:], in_=Alog_s[:], func=AF.Exp)
        V.wait(P.sem["scalar"], S.tok())
        V.op("tensor_scalar", out=negA[:], in0=negA[:], scalar1=-1.0, scalar2=0.0,
             op0=ALU.mult, op1=BYP)

        def ln_stats(src, dst_p, ev):
            for lc in range(NLC):
                cs = slice(lc * LC, (lc + 1) * LC)
                T.wait(P.sem["scalar"], ev["srow"])
                for j in range(NDH):
                    xt = xr_t[0] if j % 2 == 0 else x1t_[0]
                    Y.wait(P.sem["tensor"], T.tok())
                    Y.wait(P.sem["scalar"], S.tok())
                    Y.dma(xt[:], src[j * 128:(j + 1) * 128, cs], ld_s, ld_c)
                    ldj = ld_c["n"]
                    T.wait(ld_s, ldj)
                    T.mm(p_s1, onesf[:], xt[:], start=(j == 0), stop=(j == NDH - 1))
                    if j == NDH - 1:
                        T.mark()
                    S.wait(ld_s, ldj)
                    S.op("activation", out=xsq_v[:, j * LC:(j + 1) * LC],
                         in_=xt[:], func=AF.Square)
                t_s1 = T.tok()
                t_sq = S.tok()
                T.wait(P.sem["scalar"], t_sq)
                for j in range(NDH):
                    T.mm(p_s2, onesc[:], xsq_v[:, j * LC:(j + 1) * LC],
                         start=(j == 0), stop=(j == NDH - 1))
                T.mark()
                t_s2 = T.tok()
                S.wait(P.sem["tensor"], t_s1)
                S.op("copy", out=s1_sb[0:1, cs], in_=p_s1)
                S.wait(P.sem["tensor"], t_s2)
                S.op("copy", out=s2_sb[0:1, cs], in_=p_s2)
                ev["srow"] = S.tok()
            Y.wait(P.sem["scalar"], S.tok())
            Y.dma(dst_p[0:1, :], s1_sb[:], st_s, st_c)
            Y.dma(dst_p[1:2, :], s2_sb[:], st_s, st_c)

        def ln_norm(src, dst_sc, ev, with_proj, evd):
            for lc in range(NLC):
                cs = slice(lc * LC, (lc + 1) * LC)
                V.wait(P.sem["scalar"], ev["rbc"])
                V.op("tensor_scalar", out=mrow[:], in0=s1_sb[0:1, cs],
                     scalar1=1.0 / DM, scalar2=0.0, op0=ALU.mult, op1=BYP)
                V.op("tensor_scalar", out=qrow[:], in0=s2_sb[0:1, cs],
                     scalar1=1.0 / DM, scalar2=0.0, op0=ALU.mult, op1=BYP)
                V.op("tensor_tensor", out=m2row[:], in0=mrow[:], in1=mrow[:],
                     op=ALU.mult)
                V.op("tensor_tensor", out=vrow[:], in0=qrow[:], in1=m2row[:],
                     op=ALU.subtract)
                S.wait(P.sem["vector"], V.tok())
                S.op("activation", out=sdrow[:], in_=vrow[:], func=AF.Sqrt,
                     bias=epsc[:])
                V.wait(P.sem["scalar"], S.tok())
                V.op("reciprocal", out=rrow[:], in_=sdrow[:])
                V.op("tensor_copy", out=mrow_b[:], in_=mrow[:])
                V.op("tensor_copy", out=rrow_b[:], in_=rrow[:])
                t_rowsb = V.tok()
                T.wait(P.sem["vector"], t_rowsb)
                T.wait(P.sem["scalar"], ev["rbc"])
                T.mm(p_bc[:], onesr[:], mrow_b[:], start=True, stop=True)
                T.mark()
                S.wait(P.sem["tensor"], T.tok())
                S.op("copy", out=m_bc[:], in_=p_bc[:])
                T.wait(P.sem["scalar"], S.tok())
                T.mm(p_bc[:], onesr[:], rrow_b[:], start=True, stop=True)
                T.mark()
                S.wait(P.sem["tensor"], T.tok())
                S.op("copy", out=r_bc[:], in_=p_bc[:])
                ev["rbc"] = S.tok()
                V.wait(P.sem["scalar"], ev["rbc"])
                for j in range(NDH):
                    xt = xr_t[0] if j % 2 == 0 else x1t_[0]
                    Y.wait(P.sem["vector"], V.tok())
                    Y.dma(xt[:], src[j * 128:(j + 1) * 128, cs], ld_s, ld_c)
                    ldj = ld_c["n"]
                    V.wait(ld_s, ldj)
                    V.op("tensor_tensor", out=xn_t[:], in0=xt[:], in1=m_bc[:],
                         op=ALU.subtract)
                    V.op("tensor_tensor", out=xnb_v[:, j * LC:(j + 1) * LC],
                         in0=xn_t[:], in1=r_bc[:], op=ALU.mult)
                t_norm = V.tok()
                Y.wait(P.sem["vector"], t_norm)
                for jj in range(NDH):
                    Y.dma(dst_sc[jj * 128:(jj + 1) * 128, cs],
                          xnb_v[:, jj * LC:(jj + 1) * LC], xn_s, xn_c)
                if with_proj:
                    T.wait(P.sem["vector"], t_norm)
                    T.wait(P.sem["scalar"], ev["dtin"])
                    for j in range(NDH):
                        T.mm(p_dtin[:], wdtin_b[:, j * 128:(j + 1) * 128],
                             xnb_v[:, j * LC:(j + 1) * LC],
                             start=(j == 0), stop=(j == NDH - 1))
                    T.mark()
                    t_g1 = T.tok()
                    T.wait(P.sem["scalar"], ev["bcr"])
                    for j in range(NDH):
                        T.mm(p_bcr, wbcr_b[:, j * 68:(j + 1) * 68],
                             xnb_v[:, j * LC:(j + 1) * LC],
                             start=(j == 0), stop=(j == NDH - 1))
                    T.mark()
                    t_g2 = T.tok()
                    T.wait(P.sem["scalar"], ev["mid"])
                    for j in range(NDH):
                        T.mm(p_mid, wmid_b[:, j * 64:(j + 1) * 64],
                             xnb_v[:, j * LC:(j + 1) * LC],
                             start=(j == 0), stop=(j == NDH - 1))
                    T.mark()
                    t_g3 = T.tok()
                    S.wait(P.sem["tensor"], t_g1)
                    S.wait(st_s, evd["dt"])
                    S.op("copy", out=pr_dt[:], in_=p_dtin[:])
                    ev["dtin"] = S.tok()
                    Y.wait(P.sem["scalar"], ev["dtin"])
                    evd["dt"] = Y.dma(proj_p[0:128, cs], pr_dt[:], st_s, st_c)
                    S.wait(P.sem["tensor"], t_g2)
                    S.wait(st_s, evd["bc"])
                    S.op("copy", out=pr_bc[:], in_=p_bcrF[0:68, :])
                    ev["bcr"] = S.tok()
                    Y.wait(P.sem["scalar"], ev["bcr"])
                    evd["bc"] = Y.dma(proj_p[128:196, cs], pr_bc[:], st_s, st_c)
                    S.wait(P.sem["tensor"], t_g3)
                    S.wait(st_s, evd["md"])
                    S.op("copy", out=pr_md[:], in_=p_midF[0:64, :])
                    ev["mid"] = S.tok()
                    Y.wait(P.sem["scalar"], ev["mid"])
                    evd["md"] = Y.dma(proj_p[196:260, cs], pr_md[:], st_s, st_c)
                else:
                    V.wait(xn_s, xn_c["n"])

        t_prep = V.tok()
        T.wait(P.sem["vector"], t_prep)
        S.wait(P.sem["vector"], t_prep)
        ev = {"dtin": 0, "bcr": 0, "mid": 0, "srow": 0, "rbc": 0}
        evd = {"dt": 0, "bc": 0, "md": 0}
        ln_stats(x_fm, st0_p, ev)
        G.wait(st_s, st_c["n"])
        cc1 = G.cc("AllReduce", ALU.add, st0_p, st0_f, cc_s, cc_c)
        Y.wait(cc_s, cc1)
        Y.dma(s1_sb[:], st0_f[0:1, :], ld_s, ld_c)
        Y.dma(s2_sb[:], st0_f[1:2, :], ld_s, ld_c)
        t_st0_rb = ld_c["n"]
        V.wait(ld_s, t_st0_rb)

        ln_norm(x_fm, xn_sc, ev, True, evd)
        G.wait(st_s, st_c["n"])
        cc2 = G.cc("AllReduce", ALU.add, proj_p, proj_f, cc_s, cc_c)

        Y.wait(cc_s, cc2)
        Y.wait(P.sem["tensor"], T.tok())
        Y.wait(xn_s, xn_c["n"])
        Y.dma(stgA[:, :], proj_f[0:128, :], ld_s, ld_c)
        ld_dt = ld_c["n"]
        V.wait(ld_s, ld_dt)
        V.op("tensor_copy", out=dtin_f[:], in_=stgA[:, :])
        t_dtin_cast = V.tok()
        Y.wait(P.sem["vector"], t_dtin_cast)
        Y.dma(stgA[0:16, :], proj_f[128:144, :], ld_s, ld_c)
        Y.dma(stgA[32:48, :], proj_f[160:176, :], ld_s, ld_c)
        Y.dma(stgA[64:68, :], proj_f[192:196, :], ld_s, ld_c)
        Y.dma(stgB[:, :], proj_f[196:260, :], ld_s, ld_c)
        ld_rest = ld_c["n"]
        V.wait(ld_s, ld_rest)
        V.op("tensor_copy", out=Bfb[:], in_=stgA[0:16, :])
        V.op("tensor_copy", out=Cfb[:], in_=stgA[32:48, :])
        V.op("tensor_copy", out=lgf[:], in_=stgA[64:68, :])
        V.op("tensor_copy", out=mid0b[:], in_=stgB[:, :])
        t_casts = V.tok()
        Y.wait(P.sem["vector"], t_casts)
        Y.dma(Bb_sc[:], Bfb[:], st_s, st_c)
        Y.dma(Cb_sc[:], Cfb[:], st_s, st_c)
        bc_st = st_c["n"]

        G.wait(P.sem["vector"], t_casts)
        G.dma(wB0stg, w_B0[:], wlg_s, wlg_c)
        V.wait(wlg_s, wlg_c["n"])
        V.op("tensor_copy", out=wB0_b[:], in_=wB0stg)
        t_lgT_evac = 0
        t_gmath = 0
        t_gfm_evac = 0
        for q in range(L // 128):
            T.wait(P.sem["scalar"], S.tok() if q == 0 else t_lgT_evac)
            T.wait(P.sem["vector"], t_gmath if q > 0 else V.tok())
            T.raw("transpose", p_g, lgf[:, q * 128:(q + 1) * 128], identb[0:4, 0:4])
            T.mark()
            S.wait(P.sem["tensor"], T.tok())
            S.op("copy", out=lgT[:], in_=p_g)
            t_lgT_evac = S.tok()
            V.wait(P.sem["scalar"], t_lgT_evac)
            V.op("tensor_reduce", out=g_m1[:], in_=lgT[:], axis=AX.X, op=ALU.max)
            V.op("tensor_scalar", out=g_nm1[:], in0=g_m1[:], scalar1=-1.0,
                 scalar2=0.0, op0=ALU.mult, op1=BYP)
            V.op("tensor_scalar", out=g_ind[:, 0:4], in0=lgT[:], scalar1=g_m1[:],
                 scalar2=0.0, op0=ALU.is_ge, op1=BYP)
            V.op("scalar_tensor_tensor", out=g_t4[:, 0:4], in0=g_ind[:, 0:4],
                 scalar=-1e30, in1=lgT[:], op0=ALU.mult, op1=ALU.add)
            V.op("tensor_reduce", out=g_m2[:], in_=g_t4[:, 0:4], axis=AX.X, op=ALU.max)
            V.op("tensor_scalar", out=g_ind[:, 0:4], in0=lgT[:], scalar1=g_m2[:],
                 scalar2=0.0, op0=ALU.is_ge, op1=BYP)
            S.wait(P.sem["vector"], V.tok())
            S.op("activation", out=g_e[:, 0:4], in_=lgT[:], func=AF.Exp, bias=g_nm1[:])
            V.wait(P.sem["scalar"], S.tok())
            V.op("tensor_tensor", out=g_raw[:, 0:4], in0=g_e[:, 0:4],
                 in1=g_ind[:, 0:4], op=ALU.mult)
            V.op("tensor_reduce", out=g_s[:], in_=g_raw[:, 0:4], axis=AX.X, op=ALU.add)
            V.op("reciprocal", out=g_sr[:], in_=g_s[:])
            V.op("tensor_scalar", out=g_raw[:, 0:4], in0=g_raw[:, 0:4],
                 scalar1=g_sr[:], scalar2=0.0, op0=ALU.mult, op1=BYP)
            V.op("tensor_copy", out=g_tok4[:], in_=g_raw[:, 0:4])
            t_gmath = V.tok()
            T.wait(P.sem["vector"], t_gmath)
            T.wait(P.sem["scalar"], t_gfm_evac)
            T.raw("transpose", p_gb, g_tok4[:], identb[:])
            T.mark()
            S.wait(P.sem["tensor"], T.tok())
            S.op("copy", out=g_fm0[:, q * 128:(q + 1) * 128], in_=p_gb)
            t_gfm_evac = S.tok()
        for lc in range(NLC):
            cs = slice(lc * LC, (lc + 1) * LC)
            T.wait(P.sem["scalar"], S.tok())
            T.mm(p_bcrF[0:64, :], sel0_b[:], g_fm0[:, cs], start=True, stop=True)
            T.mark()
            S.wait(P.sem["tensor"], T.tok())
            S.op("activation", out=gbc0[:, cs], in_=p_bcrF[0:64, :], func=AF.Copy,
                 scale=4.0)
        V.wait(P.sem["scalar"], S.tok())
        V.op("tensor_tensor", out=mid0b[:], in0=mid0b[:], in1=gbc0[:], op=ALU.mult)
        t_mid0p = V.tok()
        mid0p = mid0b

        V.op("memset", state[:], 0.0)
        st_x1 = 0
        t_x1 = 0
        t_scan_done = 0
        t_scan_pp = [0, 0]
        t_dtx = [0, 0]
        for lc in range(NLC):
            cs = slice(lc * LC, (lc + 1) * LC)
            G.wait(st_s, bc_st)
            G.wait(P.sem["vector"], max(t_scan_done, t_mid0p))
            G.wait(P.sem["tensor"], T.tok())
            G.dma(BrepF_v[:].rearrange("p (n l) -> p n l", n=16),
                  Bb_sc[:, cs].unsqueeze(0).broadcast_to((128, 16, LC)), br_s, br_c)
            G.dma(wslab[0][:].rearrange("p (n l) -> p n l", n=8),
                  Cb_sc[0:8, cs].unsqueeze(0).broadcast_to((128, 8, LC)), br_s, br_c)
            G.dma(wslab[1][:].rearrange("p (n l) -> p n l", n=8),
                  Cb_sc[8:16, cs].unsqueeze(0).broadcast_to((128, 8, LC)), br_s, br_c)
            br_now = br_c["n"]
            if lc == 0:
                Y.wait(xn_s, xn_c["n"])
            for jj in range(NT):
                pp = jj % 2
                Y.wait(P.sem["vector"], V.tok())
                Y.dma(xnA[pp][:], xn_sc[jj * 128:(jj + 1) * 128, cs], ld_s, ld_c)
                ld_xn = ld_c["n"]
                T.wait(P.sem["vector"], t_dtx[pp])
                T.wait(P.sem["scalar"], S.tok())
                T.mm(p_dtr[pp][:], wdt_b[:, jj * 128:(jj + 1) * 128],
                     dtin_f[:, cs], start=True, stop=True)
                T.mark()
                t_dtr = T.tok()
                S.wait(P.sem["tensor"], t_dtr)
                S.wait(P.sem["vector"], t_dtx[pp])
                S.op("activation", out=abs_t[:], in_=p_dtr[pp][:], func=AF.Abs,
                     bias=dtb_t[:, jj:jj + 1])
                S.op("activation", out=exp_t[:], in_=abs_t[:], func=AF.Exp, scale=-1.0)
                S.op("activation", out=abs_t[:], in_=exp_t[:], func=AF.Ln,
                     bias=ones1[:])
                S.op("activation", out=relu_t[:], in_=p_dtr[pp][:], func=AF.Relu,
                     bias=dtb_t[:, jj:jj + 1])
                t_sp = S.tok()
                V.wait(P.sem["scalar"], t_sp)
                V.op("tensor_tensor", out=dt_f[pp][:], in0=relu_t[:], in1=abs_t[:],
                     op=ALU.add)
                V.wait(ld_s, ld_xn)
                V.op("tensor_tensor", out=dtx_b[pp][:], in0=dt_f[pp][:],
                     in1=xnA[pp][:], op=ALU.mult)
                t_dtx[pp] = V.tok()
                V.wait(br_s, br_now)
                for ng in range(2):
                    S.wait(P.sem["vector"], t_scan_pp[ng])
                    S.wait(P.sem["vector"], t_dtx[pp])
                    for n in range(8):
                        S.op("activation", out=dA_pp[ng][:, n * LC:(n + 1) * LC],
                             in_=dt_f[pp][:], func=AF.Exp,
                             scale=negA[:, jj * 16 + ng * 8 + n:jj * 16 + ng * 8 + n + 1])
                    t_dA = S.tok()
                    V.wait(P.sem["scalar"], t_dA)
                    V.op("tensor_tensor", out=u_v[:],
                         in0=dtx_b[pp][:].unsqueeze(1).broadcast_to((128, 8, LC)),
                         in1=BrepF_v[:, ng * 8 * LC:(ng + 1) * 8 * LC]
                         .rearrange("p (n l) -> p n l", n=8), op=ALU.mult)
                    for n in range(8):
                        cc = jj * 16 + ng * 8 + n
                        V.op("tensor_tensor_scan", out=h_v[:, n * LC:(n + 1) * LC],
                             data0=dA_pp[ng][:, n * LC:(n + 1) * LC],
                             data1=u_v[:, n * LC:(n + 1) * LC],
                             initial=state[:, cc:cc + 1], op0=ALU.mult, op1=ALU.add)
                    V.op("tensor_copy",
                         out=state[:, jj * 16 + ng * 8:jj * 16 + ng * 8 + 8],
                         in_=h_v[:].rearrange("p (n l) -> p n l", n=8)[:, :, LC - 1:LC])
                    V.op("tensor_tensor", out=hc_v[:], in0=h_v[:],
                         in1=wslab[ng][:], op=ALU.mult)
                    V.op("tensor_tensor", out=r8_t[:, 0:4 * LC], in0=hc_v[:, 0:4 * LC],
                         in1=hc_v[:, 4 * LC:8 * LC], op=ALU.add)
                    V.op("tensor_tensor", out=r8_t[:, 0:2 * LC],
                         in0=r8_t[:, 0:2 * LC], in1=r8_t[:, 2 * LC:4 * LC], op=ALU.add)
                    if ng == 0:
                        V.op("tensor_tensor", out=yh1[:], in0=r8_t[:, 0:LC],
                             in1=r8_t[:, LC:2 * LC], op=ALU.add)
                    else:
                        V.op("tensor_tensor", out=ysc_t[:], in0=r8_t[:, 0:LC],
                             in1=r8_t[:, LC:2 * LC], op=ALU.add)
                        V.op("tensor_tensor", out=yh1[:], in0=yh1[:], in1=ysc_t[:],
                             op=ALU.add)
                    t_scan_pp[ng] = V.tok()
                    t_scan_done = t_scan_pp[ng]
                V.op("scalar_tensor_tensor", out=ym_t[:], in0=xnA[pp][:],
                     scalar=Dcol[:, jj:jj + 1], in1=yh1[:], op0=ALU.mult, op1=ALU.add)
                t_scan_done = V.tok()
                T.wait(P.sem["vector"], t_mid0p)
                T.wait(P.sem["vector"], t_x1)
                T.mm(p_d0[:], wB0_b[:, jj * 128:(jj + 1) * 128], mid0p[:, cs],
                     start=True, stop=True)
                T.mark()
                t_d0 = T.tok()
                Y.wait(P.sem["vector"], t_scan_done)
                Y.wait(st_s, st_x1)
                Y.dma(xr_t[0][:], x_fm[jj * 128:(jj + 1) * 128, cs], ld_s, ld_c)
                ld_xr = ld_c["n"]
                V.wait(ld_s, ld_xr)
                V.op("tensor_tensor", out=ysc_t[:], in0=ym_t[:], in1=xr_t[0][:],
                     op=ALU.add)
                V.wait(P.sem["tensor"], t_d0)
                V.op("tensor_tensor", out=xr_t[0][:], in0=ysc_t[:], in1=p_d0[:],
                     op=ALU.add)
                t_x1 = V.tok()
                Y.wait(P.sem["vector"], t_x1)
                st_x1 = Y.dma(x1_sc[jj * 128:(jj + 1) * 128, cs], xr_t[0][:],
                              st_s, st_c)

        ev2 = {"dtin": 0, "bcr": 0, "mid": 0, "srow": S.tok(), "rbc": S.tok()}
        T.wait(P.sem["vector"], V.tok())
        S.wait(P.sem["vector"], V.tok())
        S.wait(ld_s, ld_c["n"])
        Y.wait(st_s, st_c["n"])
        ln_stats(x1_sc, st1_p, ev2)
        G.wait(st_s, st_c["n"])
        cc3 = G.cc("AllReduce", ALU.add, st1_p, st1_f, cc_s, cc_c)
        Y.wait(cc_s, cc3)
        Y.dma(s1_sb[:], st1_f[0:1, :], ld_s, ld_c)
        Y.dma(s2_sb[:], st1_f[1:2, :], ld_s, ld_c)
        V.wait(ld_s, ld_c["n"])
        ln_norm(x1_sc, xn1h, ev2, False, evd)
        G.wait(xn_s, xn_c["n"])
        cc4 = G.cc("AllGather", BYP, xn1h, xn1_f, cc_s, cc_c)

        t_cast = [0, 0]
        t_rot = [0, 0]
        t_ve = [0, 0]
        t_middma = 0
        t_qkb_pp = [0, 0]
        t_swt_pp = [0, 0]
        t_vec = 0
        p_qkv4 = [p_bcrF, p_midF, p_o, p_l]
        p_sw2 = [p_sw, p_stats]
        qk_b4 = [qk_b, qk_b2[1], qk_bx2, qk_bx3]
        qg = [0]
        t_qevac = [S.tok()] * 4
        t_mmk = [0, 0, 0, 0]
        t_rotv = [0, 0, 0, 0]
        t_rot1sw = [0, 0]
        t_swk = [0, 0]
        vg = [0]
        vout4 = [vout_b[0], vout_b[1], vout_bx2, vout_bx3]
        t_ve4 = [0, 0, 0, 0]
        t_vsw = [0, 0]
        G.wait(P.sem["vector"], V.tok())
        G.wait(P.sem["tensor"], T.tok())
        G.wait(P.sem["scalar"], S.tok())
        for lq in range(4):
            hs = slice(lq * LQ, (lq + 1) * LQ)
            Y.wait(cc_s, cc4)
            Y.wait(xn_s, xn_c["n"])
            Y.wait(P.sem["tensor"], T.tok())
            Y.wait(P.sem["scalar"], S.tok())
            Y.wait(P.sem["vector"], V.tok())
            for j in range(NDF):
                Y.dma(xnh[:, j * LQ:(j + 1) * LQ],
                      xn1_f[j * 128:(j + 1) * 128, hs], ld_s, ld_c)
            ld_xnh = ld_c["n"]
            T.wait(ld_s, ld_xnh)
            def kq_epilogue(st):
                pb, ps, mch_e, gcs_e, spill_e = st
                rpp = pb % 2
                T.wait(P.sem["scalar"], t_qevac[pb])
                T.wait(P.sem["vector"], t_rot1sw[ps])
                T.mm(p_sw2[ps][:], pswap_s[:], qk_b4[pb][:], start=True, stop=True)
                T.mark()
                t_swk[ps] = T.tok()
                V.wait(P.sem["tensor"], t_swk[ps])
                V.wait(sp_s, t_rot[rpp])
                V.op("tensor_tensor", out=rot1[rpp][:], in0=p_sw2[ps][:],
                     in1=sin_s[:, gcs_e:gcs_e + LC], op=ALU.mult)
                V.op("tensor_tensor", out=rot2[:], in0=qk_b4[pb][:],
                     in1=cos_s[:, gcs_e:gcs_e + LC], op=ALU.mult)
                V.op("tensor_tensor", out=rot1[rpp][:], in0=rot1[rpp][:],
                     in1=rot2[:], op=ALU.add)
                t_rotv[pb] = V.tok()
                t_rot1sw[ps] = t_rotv[pb]
                Y.wait(P.sem["vector"], t_rotv[pb])
                t_rot[rpp] = Y.dma(
                    spill_e[mch_e * 128:(mch_e + 1) * 128, gcs_e:gcs_e + LC],
                    rot1[rpp][:], sp_s, sp_c)

            pend = [None]
            for w_in, spill in ((w_k, k_sp), (w_q, q_sp)):
                for mch in range(16):
                    wpp = mch % 2
                    G.wait(P.sem["vector"], max(t_cast[0], t_cast[1]))
                    G.dma(wtmp[:, 0:4096].rearrange("p (c k) -> p c k", k=128),
                          w_in[:, mch * 128:(mch + 1) * 128]
                          .rearrange("(c p) k -> p c k", p=128), wlg_s, wlg_c)
                    V.wait(wlg_s, wlg_c["n"])
                    V.wait(P.sem["tensor"], T.tok())
                    V.op("tensor_copy", out=wslab[wpp][:], in_=wtmp[:, 0:4096])
                    t_cast[wpp] = V.tok()
                    for lt in range(LQ // LC):
                        gcs = lq * LQ + lt * LC
                        b4 = qg[0] % 4
                        T.wait(P.sem["vector"], t_cast[wpp])
                        T.wait(P.sem["scalar"], t_qevac[b4])
                        for kc in range(NDF):
                            T.mm(p_qkv4[b4][:], wslab[wpp][:, kc * 128:(kc + 1) * 128],
                                 xnh[:, kc * LQ + lt * LC:kc * LQ + (lt + 1) * LC],
                                 start=(kc == 0), stop=(kc == NDF - 1))
                        T.mark()
                        t_mmk[b4] = T.tok()
                        if pend[0] is not None:
                            kq_epilogue(pend[0])
                        S.wait(P.sem["tensor"], t_mmk[b4])
                        S.wait(P.sem["vector"], t_rotv[b4])
                        S.op("copy", out=qk_b4[b4][:], in_=p_qkv4[b4][:])
                        t_qevac[b4] = S.tok()
                        pend[0] = (b4, qg[0] % 2, mch, gcs, spill)
                        qg[0] += 1
            kq_epilogue(pend[0])
            pend[0] = None
            T.wait(P.sem["vector"], t_rot1sw[0])
            T.wait(P.sem["vector"], t_rot1sw[1])
            for dvc in range(16):
                wpp = dvc % 2
                G.wait(P.sem["vector"], max(t_cast[0], t_cast[1]))
                G.dma(wtmp[:, 0:4096].rearrange("p (c k) -> p c k", k=128),
                      w_v[:, dvc * 128:(dvc + 1) * 128]
                      .rearrange("(c p) k -> p c k", p=128), wlg_s, wlg_c)
                V.wait(wlg_s, wlg_c["n"])
                V.wait(P.sem["tensor"], T.tok())
                V.op("tensor_copy", out=wslab[wpp][:], in_=wtmp[:, 0:4096])
                t_cast[wpp] = V.tok()
                for lt in range(LQ // LC):
                    gcs = lq * LQ + lt * LC
                    pp = lt % 2
                    T.wait(P.sem["vector"], t_cast[wpp])
                    T.wait(P.sem["scalar"], t_qkb_pp[pp])
                    T.wait(P.sem["scalar"], t_qevac[pp])
                    for kc in range(NDF):
                        T.mm(p_qkv[pp][:], wslab[wpp][:, kc * 128:(kc + 1) * 128],
                             xnh[:, kc * LQ + lt * LC:kc * LQ + (lt + 1) * LC],
                             start=(kc == 0), stop=(kc == NDF - 1))
                    T.mark()
                    t_mm = T.tok()
                    S.wait(P.sem["tensor"], t_mm)
                    S.wait(P.sem["tensor"], t_swt_pp[pp])
                    S.op("copy", out=qk_b2[pp][:], in_=p_qkv[pp][:])
                    t_qkb_pp[pp] = S.tok()
                    for sub in range(4):
                        tb = vg[0] % 2
                        vb = vg[0] % 4
                        T.wait(P.sem["scalar"],
                               t_qkb_pp[pp] if sub == 0 else t_vsw[tb])
                        T.raw("transpose", p_sw2[tb][:, 0:64].bitcast(BF16),
                              qk_b2[pp][:, sub * 128:(sub + 1) * 128], identb[:])
                        T.mark()
                        t_tr = T.tok()
                        if sub == 3:
                            t_swt_pp[pp] = t_tr
                        S.wait(P.sem["tensor"], t_tr)
                        S.wait(sp_s, t_ve4[vb])
                        S.op("copy", out=vout4[vb][:],
                             in_=p_sw2[tb][:, 0:64].bitcast(BF16))
                        t_vsw[tb] = S.tok()
                        Y.wait(P.sem["scalar"], t_vsw[tb])
                        t_ve4[vb] = Y.dma(
                            v_sp[gcs + sub * 128:gcs + (sub + 1) * 128,
                                 dvc * 128:(dvc + 1) * 128],
                            vout4[vb][:], sp_s, sp_c)
                        vg[0] += 1
            t_qevac[0] = t_qkb_pp[0]
            t_qevac[1] = t_qkb_pp[1]
            T.wait(P.sem["vector"], t_rot1sw[1])
            for ltt in range(LQ // 128):
                T.wait(P.sem["scalar"], S.tok())
                for kc in range(NDF):
                    T.mm(p_g8, xnh[:, kc * LQ + ltt * 128:kc * LQ + (ltt + 1) * 128],
                         gT_b[:, kc * 8:(kc + 1) * 8], start=(kc == 0),
                         stop=(kc == NDF - 1))
                T.mark()
                t_gmm = T.tok()
                S.wait(P.sem["tensor"], t_gmm)
                S.op("copy", out=gtok[:], in_=p_g8)
                t_ge = S.tok()
                V.wait(P.sem["scalar"], t_ge)
                V.op("tensor_reduce", out=g_m1[:], in_=gtok[:], axis=AX.X, op=ALU.max)
                V.op("tensor_scalar", out=g_nm1[:], in0=g_m1[:], scalar1=-1.0,
                     scalar2=0.0, op0=ALU.mult, op1=BYP)
                V.op("tensor_scalar", out=g_ind[:], in0=gtok[:], scalar1=g_m1[:],
                     scalar2=0.0, op0=ALU.is_ge, op1=BYP)
                V.op("scalar_tensor_tensor", out=g_t4[:], in0=g_ind[:], scalar=-1e30,
                     in1=gtok[:], op0=ALU.mult, op1=ALU.add)
                V.op("tensor_reduce", out=g_m2[:], in_=g_t4[:], axis=AX.X, op=ALU.max)
                V.op("tensor_scalar", out=g_ind[:], in0=gtok[:], scalar1=g_m2[:],
                     scalar2=0.0, op0=ALU.is_ge, op1=BYP)
                S.wait(P.sem["vector"], V.tok())
                S.op("activation", out=g_e[:], in_=gtok[:], func=AF.Exp, bias=g_nm1[:])
                V.wait(P.sem["scalar"], S.tok())
                V.op("tensor_tensor", out=g_raw[:], in0=g_e[:], in1=g_ind[:],
                     op=ALU.mult)
                V.op("tensor_reduce", out=g_s[:], in_=g_raw[:], axis=AX.X, op=ALU.add)
                V.op("reciprocal", out=g_sr[:], in_=g_s[:])
                V.op("tensor_scalar", out=g_raw[:], in0=g_raw[:], scalar1=g_sr[:],
                     scalar2=0.0, op0=ALU.mult, op1=BYP)
                V.op("tensor_copy", out=g_tokb[:], in_=g_raw[:])
                t_gmath = V.tok()
                T.wait(P.sem["vector"], t_gmath)
                T.wait(P.sem["scalar"], S.tok())
                T.raw("transpose", p_gt8, g_tokb[:], identb[:])
                T.mark()
                S.wait(P.sem["tensor"], T.tok())
                S.op("copy",
                     out=g_fm1[:, lq * LQ + ltt * 128:lq * LQ + (ltt + 1) * 128],
                     in_=p_gt8)
            for erc in range(2):
                wpp = erc % 2
                G.wait(P.sem["vector"], max(t_cast[0], t_cast[1]))
                G.dma(wtmp[:, 0:4096].rearrange("p (c k) -> p c k", k=128),
                      w_A1[:, erc * 128:(erc + 1) * 128]
                      .rearrange("(c p) k -> p c k", p=128), wlg_s, wlg_c)
                V.wait(wlg_s, wlg_c["n"])
                V.wait(P.sem["tensor"], T.tok())
                V.op("tensor_copy", out=wslab[wpp][:], in_=wtmp[:, 0:4096])
                t_cast[wpp] = V.tok()
                for lt in range(LQ // LC):
                    pp = lt % 2
                    gcs = lq * LQ + lt * LC
                    T.wait(P.sem["vector"], t_cast[wpp])
                    T.wait(P.sem["vector"], V.tok())
                    for kc in range(NDF):
                        T.mm(p_qkv[pp][:], wslab[wpp][:, kc * 128:(kc + 1) * 128],
                             xnh[:, kc * LQ + lt * LC:kc * LQ + (lt + 1) * LC],
                             start=(kc == 0), stop=(kc == NDF - 1))
                    T.mark()
                    t_mm = T.tok()
                    T.wait(P.sem["scalar"], S.tok())
                    T.mm(p_sw[:], sel1_s[:, erc * 128:(erc + 1) * 128],
                         g_fm1[:, gcs:gcs + LC], start=True, stop=True)
                    T.mark()
                    t_gbc = T.tok()
                    S.wait(P.sem["tensor"], t_gbc)
                    S.op("copy", out=gbc_t[:], in_=p_sw[:])
                    t_gbce = S.tok()
                    V.wait(P.sem["tensor"], t_mm)
                    V.wait(P.sem["scalar"], t_gbce)
                    V.wait(st_s, t_middma)
                    V.op("tensor_tensor", out=mid1_t[:], in0=p_qkv[pp][:],
                         in1=gbc_t[:], op=ALU.mult)
                    t_mid = V.tok()
                    Y.wait(P.sem["vector"], t_mid)
                    t_middma = Y.dma(midh[erc * 128:(erc + 1) * 128, gcs:gcs + LC],
                                     mid1_t[:], st_s, st_c)
        G.wait(st_s, st_c["n"])
        cc5 = G.cc("AllGather", BYP, midh, mid_f, cc_s, cc_c)

        p_o2 = [p_o, p_bcrF]
        p_l2 = [p_l, p_midF]
        t_oh_dma = 0
        t_ohv = [0, 0]
        t_lbe = [0, 0]
        t_mbc_prev = 0
        t_headT = [T.tok(), T.tok()]
        g3 = max(T.tok(), 1)
        S.wait(P.sem["tensor"], g3)
        V.wait(P.sem["tensor"], g3)
        ld_h2 = [0, 0]
        for h in range(16):
            hb = h % 2
            if h < 2:
                Y.wait(sp_s, sp_c["n"])
                Y.wait(P.sem["scalar"], S.tok())
                Y.wait(P.sem["vector"], V.tok())
            Y.wait(P.sem["tensor"], t_headT[hb])
            Y.dma(Kh_b[hb][:], k_sp[h * 128:(h + 1) * 128, :], ld_s, ld_c)
            Y.dma(Qh_b[hb][:], q_sp[h * 128:(h + 1) * 128, :], ld_s, ld_c)
            Y.dma(Vsb_b[hb][:].rearrange("p (c k) -> p c k", k=128),
                  v_sp[:, h * 128:(h + 1) * 128].rearrange("(c p) k -> p c k", p=128),
                  ld_s, ld_c)
            ld_h2[hb] = ld_c["n"]
            T.wait(ld_s, ld_h2[hb])
            Kh, Qh, Vsb = Kh_b[hb], Qh_b[hb], Vsb_b[hb]
            for qb in range(NLC):
                qp = qb % 2
                nkt = 4 * (qb + 1)
                t_exp = [0, 0]
                t_avm = [0, 0]
                t_msk = [0, 0]
                for kt in range(nkt + 1):
                    if kt < nkt:
                        pp = kt % 2
                        T.wait(P.sem["scalar"], t_exp[pp])
                        T.mm(p_s[:] if pp == 0 else p_s2nd[:],
                             Kh[:, kt * 128:(kt + 1) * 128],
                             Qh[:, qb * LC:(qb + 1) * LC], start=True, stop=True)
                        T.mark()
                        t_sc = T.tok()
                    if kt >= 1:
                        km = kt - 1
                        pk = km % 2
                        if km >= nkt - 4:
                            T.wait(P.sem["vector"], t_msk[pk])
                        else:
                            T.wait(P.sem["scalar"], t_exp[pk])
                        if km == 0:
                            T.wait(P.sem["vector"], t_ohv[qp])
                            T.wait(P.sem["scalar"], t_lbe[qp])
                        T.mm(p_o2[qp][:], Vsb[:, km * 128:(km + 1) * 128],
                             PT[pk][:], start=(km == 0), stop=(km == nkt - 1))
                        T.mm(p_l2[qp][0:1, :], onesc[:], PT[pk][:], start=(km == 0),
                             stop=(km == nkt - 1))
                        T.mark()
                        t_avm[pk] = T.tok()
                    if kt < nkt:
                        S.wait(P.sem["tensor"], t_sc)
                        S.wait(P.sem["tensor"], t_avm[pp])
                        if kt >= nkt - 4:
                            S.wait(P.sem["vector"], t_msk[pp])
                        S.op("activation", out=PT[pp][:],
                             in_=(p_s[:] if pp == 0 else p_s2nd[:]), func=AF.Exp,
                             scale=0.08838834764831845)
                        t_exp[pp] = S.tok()
                        if kt >= nkt - 4:
                            mj = kt - (nkt - 4)
                            V.wait(P.sem["scalar"], t_exp[pp])
                            V.op("tensor_tensor", out=PT[pp][:], in0=PT[pp][:],
                                 in1=mask_s[:, mj * 512:(mj + 1) * 512], op=ALU.mult)
                            t_msk[pp] = V.tok()
                t_av = T.tok()
                S.wait(P.sem["tensor"], t_av)
                S.wait(P.sem["vector"], t_ohv[1 - qp])
                S.op("copy", out=lrec_b[:], in_=p_l2[qp][0:1, :])
                t_lbe[qp] = S.tok()
                T.wait(P.sem["scalar"], t_lbe[qp])
                T.wait(P.sem["scalar"], t_mbc_prev)
                T.mm(p_bc[:], onesr[:], lrec_b[:], start=True, stop=True)
                T.mark()
                t_lb = T.tok()
                S.wait(P.sem["tensor"], t_lb)
                S.op("copy", out=m_bc[:], in_=p_bc[:])
                t_mbc_prev = S.tok()
                V.wait(P.sem["scalar"], t_mbc_prev)
                V.op("reciprocal", out=r_bc[:], in_=m_bc[:])
                V.wait(st_s, t_oh_dma)
                V.op("tensor_tensor", out=oh_t[:], in0=p_o2[qp][:], in1=r_bc[:],
                     op=ALU.mult)
                t_ohv[qp] = V.tok()
                Y.wait(P.sem["vector"], t_ohv[qp])
                t_oh_dma = Y.dma(ohh[h * 128:(h + 1) * 128, qb * LC:(qb + 1) * LC],
                                 oh_t[:], st_s, st_c)
            t_headT[hb] = T.tok()
        G.wait(st_s, st_c["n"])
        cc6 = G.cc("AllGather", BYP, ohh, oh_f, cc_s, cc_c)

        G.wait(P.sem["vector"], V.tok())
        G.wait(P.sem["tensor"], T.tok())
        G.wait(P.sem["scalar"], S.tok())
        T.wait(P.sem["scalar"], S.tok())
        p_y4 = [p_bcrF, p_midF, p_o, p_l]
        qk_bP = [qk_b, qk_bx3]
        x1buf4 = [xr_t[0], x1t_[0], rows_a, abs_t]
        x2t4 = [x2t[0], x2t[1], m_bc, r_bc]
        pg4 = [0]
        t_x2v = [0, 0, 0, 0]
        t_mmb = [0, 0, 0, 0]
        st_t4 = [0, 0, 0, 0]
        ld_x14 = [0, 0, 0, 0]
        t_pm = [T.tok(), T.tok()]
        for lq in range(4):
            hs = slice(lq * LQ, (lq + 1) * LQ)
            Y.wait(cc_s, cc6)
            Y.wait(st_s, st_c["n"])
            Y.wait(P.sem["tensor"], T.tok())
            Y.wait(P.sem["scalar"], S.tok())
            Y.wait(P.sem["vector"], V.tok())
            for j in range(NDF):
                Y.dma(oh_res[:, j * LQ:(j + 1) * LQ],
                      oh_f[j * 128:(j + 1) * 128, hs], ld_s, ld_c)
            for j in range(4):
                Y.dma(mid_res[:, j * LQ:(j + 1) * LQ],
                      mid_f[j * 128:(j + 1) * 128, hs], ld_s, ld_c)
            ld_all = ld_c["n"]
            T.wait(ld_s, ld_all)
            for m in range(16):
                wpp = m % 2
                G.wait(P.sem["vector"], max(t_cast[0], t_cast[1]))
                G.dma(wtmp[:, 0:4096].rearrange("p (c k) -> p c k", k=128),
                      w_o[:, m * 128:(m + 1) * 128]
                      .rearrange("(c p) k -> p c k", p=128), wlg_s, wlg_c)
                G.dma(rows_b2[:].rearrange("p (c k) -> p c k", k=128),
                      w_B1[:, m * 128:(m + 1) * 128]
                      .rearrange("(c p) k -> p c k", p=128), wlg_s, wlg_c)
                V.wait(wlg_s, wlg_c["n"])
                V.wait(P.sem["tensor"], t_pm[wpp])
                V.op("tensor_copy", out=wslab[wpp][:], in_=wtmp[:, 0:4096])
                V.op("tensor_copy", out=qk_bP[wpp][:], in_=rows_b2[:])
                t_cast[wpp] = V.tok()
                for lt in range(LQ // LC):
                    b = pg4[0] % 4
                    gcs = lq * LQ + lt * LC
                    Y.wait(P.sem["vector"], t_x2v[b])
                    Y.dma(x1buf4[b][:], x1_sc[m * 128:(m + 1) * 128, gcs:gcs + LC],
                          ld_s, ld_c)
                    ld_x14[b] = ld_c["n"]
                    T.wait(P.sem["vector"], t_cast[wpp])
                    T.wait(P.sem["vector"], t_x2v[b])
                    for kc in range(NDF):
                        T.mm(p_y4[b][:], wslab[wpp][:, kc * 128:(kc + 1) * 128],
                             oh_res[:, kc * LQ + lt * LC:kc * LQ + (lt + 1) * LC],
                             start=(kc == 0), stop=False)
                    for ec in range(4):
                        T.mm(p_y4[b][:], qk_bP[wpp][:, ec * 128:(ec + 1) * 128],
                             mid_res[:, ec * LQ + lt * LC:ec * LQ + (lt + 1) * LC],
                             start=False, stop=(ec == 3))
                    T.mark()
                    t_mmb[b] = T.tok()
                    V.wait(P.sem["tensor"], t_mmb[b])
                    V.wait(ld_s, ld_x14[b])
                    V.wait(st_s, st_t4[b])
                    V.op("tensor_tensor", out=x2t4[b][:], in0=p_y4[b][:],
                         in1=x1buf4[b][:], op=ALU.add)
                    t_x2v[b] = V.tok()
                    Y.wait(P.sem["vector"], t_x2v[b])
                    st_t4[b] = Y.dma(x2_out[m * 128:(m + 1) * 128, gcs:gcs + LC],
                                     x2t4[b][:], st_s, st_c)
                    pg4[0] += 1
                t_pm[wpp] = T.tok()
        Y.wait(st_s, st_c["n"])
    with nc.Block() as blk:
        P.finalize(blk)
    return nc


def _prep_fused_inputs(inputs):
    x = inputs["L0_attn_x"]
    xpw = inputs["L0_attn_x_proj_w"]
    dtw = inputs["L0_attn_dt_proj_w"]
    dtb = inputs["L0_attn_dt_proj_b"]
    Alog = inputs["L0_attn_A_log"]
    Dv = inputs["L0_attn_D"]
    rw = inputs["L0_ffn_router_w"]
    A0 = inputs["L0_ffn_A"]
    B0 = inputs["L0_ffn_B"]
    wq = inputs["L1_attn_wq"]
    wk = inputs["L1_attn_wk"]
    wv = inputs["L1_attn_wv"]
    wo = inputs["L1_attn_wo"]
    gw = inputs["L1_ffn_gate_w"]
    lA = inputs["L1_ffn_lora_A"]
    lB = inputs["L1_ffn_lora_B"]

    w_dtin = xpw[:128].T
    w_bcr = np.zeros((DM, 68), np.float32)
    w_bcr[:, 0:16] = xpw[128:144].T
    w_bcr[:, 32:48] = xpw[144:160].T
    w_bcr[:, 64:68] = rw.T
    w_mid = A0.reshape(64, DM).T
    sel0v = np.zeros((4, 64), np.float32)
    for e in range(4):
        sel0v[e, e * 16:(e + 1) * 16] = 1.0
    B0T = B0.transpose(0, 2, 1).reshape(64, DM)
    t = np.arange(L, dtype=np.float64)
    cosr = np.cos(t).astype(bf16).astype(np.float32)
    sinr = np.sin(t).astype(bf16).astype(np.float32)
    cosbv = np.ascontiguousarray(np.broadcast_to(cosr, (128, L))).astype(bf16)
    sinbv = np.ascontiguousarray(np.broadcast_to(sinr, (128, L))).astype(bf16)
    mk = np.zeros((128, 4, 512), np.float32)
    for mj in range(4):
        for k in range(128):
            mk[k, mj, mj * 128 + k:] = 1.0
    masksv = mk.reshape(128, 2048).astype(bf16)
    psw = np.zeros((128, 128), np.float32)
    for i in range(64):
        psw[2 * i + 1, 2 * i] = -1.0
        psw[2 * i, 2 * i + 1] = 1.0
    psw = psw.astype(bf16)
    identbv = np.eye(128, dtype=np.float32).astype(bf16)
    sel1v = np.zeros((8, 512), np.float32)
    for e in range(8):
        sel1v[e, e * 64:(e + 1) * 64] = 1.0
    sel1v = sel1v.astype(bf16)
    wA1 = lA.reshape(512, DM).T
    woT = wo.T
    wB1 = lB.transpose(0, 2, 1).reshape(512, DM)
    dtwT = dtw.T

    maps = []
    for c in range(8):
        b, hp = c // 2, c % 2
        hs = slice(hp * DH, (hp + 1) * DH)
        rs = slice(hp * 256, (hp + 1) * 256)
        m = {
            "x_fm": np.ascontiguousarray(x[b].T[hs]),
            "w_dtin": np.ascontiguousarray(w_dtin[hs]),
            "w_bcr": np.ascontiguousarray(w_bcr[hs]),
            "w_mid": np.ascontiguousarray(w_mid[hs]),
            "w_dt": np.ascontiguousarray(dtwT[:, hs]),
            "dt_bias": np.ascontiguousarray(dtb[hs].reshape(NT, 128).T),
            "A_log_t": np.ascontiguousarray(
                Alog[hs].reshape(NT, 128, 16).transpose(1, 0, 2)
                .reshape(128, NT * 16)),
            "D_t": np.ascontiguousarray(Dv[hs].reshape(NT, 128).T),
            "w_B0": np.ascontiguousarray(B0T[:, hs]),
            "sel0": sel0v,
            "w_q": np.ascontiguousarray(wq.T[:, hs]),
            "w_k": np.ascontiguousarray(wk.T[:, hs]),
            "w_v": np.ascontiguousarray(wv.T[:, hs]),
            "w_g": np.ascontiguousarray(gw.T),
            "w_A1": np.ascontiguousarray(wA1[:, rs]),
            "w_o": np.ascontiguousarray(woT[:, hs]),
            "w_B1": np.ascontiguousarray(wB1[:, hs]),
            "cosb": cosbv,
            "sinb": sinbv,
            "masks": masksv,
            "pswap": psw,
            "identbi": identbv,
            "sel1": np.ascontiguousarray(sel1v[:, rs]),
        }
        maps.append(m)
    return maps


def kernel(**inputs):
    maps = _prep_fused_inputs(inputs)
    nc = build_fused()
    res = run_bass_kernel_spmd(nc, maps, core_ids=list(range(8)))
    out = np.empty((NB, L, DM), np.float32)
    for c in range(8):
        b, hp = c // 2, c % 2
        x2 = np.asarray(res.results[c]["x2_out"])
        out[b, :, hp * DH:(hp + 1) * DH] = x2.T
    return out
